# version 13
# baseline (speedup 1.0000x reference)
import numpy as np
from contextlib import ExitStack

import concourse.bass as bass
import concourse.mybir as mybir
from concourse.bass_utils import run_bass_kernel_spmd

F32 = mybir.dt.float32
BF16 = mybir.dt.bfloat16
U32 = mybir.dt.uint32
AF = mybir.ActivationFunctionType
ALU = mybir.AluOpType
AX = mybir.AxisListType

D = 1024
S = 2048
NH = 16
DH = 64
NE = 9
NSUB = 4
TT = 16
CAPE = 1024
EPS1 = 1e-5
EPS2 = 1.1920929e-07
KB = 1024


class SC:
    def __init__(self, handle, step=1):
        self.h = handle
        self.n = 0
        self.step = step


class DeferredQ:
    def __init__(self):
        self.q = []

    def add(self, fn):
        self.q.append(fn)

    def flush(self, section_decorator):
        q = self.q
        self.q = []

        def _run(e):
            for fn in q:
                fn(e)
        section_decorator(_run)


def build_nc(n_seq=2, n_pairs=8, n_exp=NE, n_sub=NSUB):
    nc = bass.Bass("TRN2", target_bir_lowering=False)

    x_h = nc.dram_tensor("x", [n_seq, S, D], F32, kind="ExternalInput")
    wqkv_h = nc.dram_tensor("wqkv", [24, 128, 1024], F32, kind="ExternalInput")
    bqkv_h = nc.dram_tensor("bqkv", [128, 24], F32, kind="ExternalInput")
    wo_h = nc.dram_tensor("wo", [128, 8192], F32, kind="ExternalInput")
    bo_h = nc.dram_tensor("bo", [1, 1024], F32, kind="ExternalInput")
    gw_h = nc.dram_tensor("gw", [128, 64], F32, kind="ExternalInput")
    w1_h = nc.dram_tensor("w1", [NE, NSUB, 128, 8192], F32, kind="ExternalInput")
    w2_h = nc.dram_tensor("w2", [NE, NSUB, 128, 8192], F32, kind="ExternalInput")
    out_h = nc.dram_tensor("out", [n_seq, S, D], F32, kind="ExternalOutput")
    xg_d = nc.dram_tensor("xg_d", [8 * CAPE, 1024], BF16)
    eo_d = nc.dram_tensor("eo_d", [8 * CAPE, 1024], BF16)

    cur = [((nc.sbuf_base + 31) // 32) * 32]

    def place(name, shape, dt):
        t = nc.alloc_sbuf_tensor_at(name, shape, dt, offset=cur[0])
        lo, hi = t.manual_sbuf_range
        cur[0] = ((hi + 31) // 32) * 32
        return t

    def place_at(name, shape, dt, off):
        return nc.alloc_sbuf_tensor_at(name, shape, dt, offset=off)

    ident = place("ident", [128, 128], BF16)
    tri = place("tri", [128, 128], BF16)
    bqkv_t = place("bqkv_t", [128, 24], F32)
    bo_bc = place("bo_bc", [128, 1024], F32)
    bo_row = place("bo_row", [1, 1024], F32)
    gate_bf = place("gate_bf", [128, 64], BF16)
    ones_r = place("ones_r", [1, 128], F32)
    bnb = [place(f"bnb{i}", [128, 12], F32) for i in range(2)]
    bno = [place(f"bno{i}", [128, 2], F32) for i in range(2)]
    aab = [place(f"aab{i}", [128, 1], F32) for i in range(2)]
    bbb = [place(f"bbb{i}", [128, 1], F32) for i in range(2)]
    rm4 = [place(f"rm4{h}", [128, 4], F32) for h in range(2)]
    rs4 = [place(f"rs4{h}", [128, 4], F32) for h in range(2)]
    rmin = [place(f"rmin{h}", [128, 1], F32) for h in range(2)]
    rst = [place(f"rst{h}", [128, 1], F32) for h in range(2)]
    rin = [place(f"rin{h}", [128, 1], F32) for h in range(2)]
    vsc = [[place(f"vsc{h}_{i}", [128, 64], BF16) for i in range(2)] for h in range(2)]
    lgs = [place(f"lgs{i}", [128, 8], F32) for i in range(2)]
    glt = [place(f"glt{i}", [128, 8], F32) for i in range(2)]
    gt1 = [place(f"gt1{i}", [128, 8], F32) for i in range(2)]
    gm1 = [place(f"gm1{i}", [128, 1], F32) for i in range(2)]
    gm2 = [place(f"gm2{i}", [128, 1], F32) for i in range(2)]
    gsum = [place(f"gsum{i}", [128, 1], F32) for i in range(2)]
    gates = [place(f"gates{t}", [128, 8], F32) for t in range(TT)]
    eidx = place("eidx", [128, 8], F32)
    oh1 = [place(f"oh1_{i}", [128, 8], F32) for i in range(2)]
    oh2 = [place(f"oh2_{i}", [128, 8], F32) for i in range(2)]
    tf8 = [place(f"tf8_{i}", [128, 8], F32) for i in range(2)]
    e1f = [place(f"e1f{i}", [128, 1], F32) for i in range(2)]
    e2f = [place(f"e2f{i}", [128, 1], F32) for i in range(2)]
    s1f = [place(f"s1f{i}", [128, 1], F32) for i in range(2)]
    s2f = [place(f"s2f{i}", [128, 1], F32) for i in range(2)]
    slf = [place(f"slf{i}", [128, 1], F32) for i in range(2)]
    sidx1 = [place(f"sidx1_{t}", [128, 1], U32) for t in range(TT)]
    sidx2 = [place(f"sidx2_{t}", [128, 1], U32) for t in range(TT)]
    gsl1 = [place(f"gsl1_{t}", [128, 1], F32) for t in range(TT)]
    gsl2 = [place(f"gsl2_{t}", [128, 1], F32) for t in range(TT)]
    cumf = place("cumf", [8, 128], F32)
    cuml = place("cuml", [8, 1], F32)
    zero8 = place("zero8", [8, 128], F32)
    identf = place("identf", [128, 128], F32)
    cb_ln1 = place("cb_ln1", [128, 1], F32)
    cb_eps = place("cb_eps", [128, 1], F32)
    cb_att = place("cb_att", [128, 1], F32)

    NF0 = cur[0]
    nf = [place(f"nf{c}", [128, 2048], BF16) for c in range(8)]
    X0 = cur[0]
    qp = [place_at(f"qp{p}", [128, 2048], BF16, X0 + p * 4 * KB) for p in range(8)]
    vtm = [place_at(f"vtm{t}", [128, 1024], BF16, X0 + 32 * KB + t * 2 * KB)
           for t in range(TT)]
    xn = [place_at(f"xn{t}", [128, 1024], F32, X0 + t * 4 * KB) for t in range(TT)]
    Y0 = X0 + 64 * KB
    kp = [place_at(f"kp{p}", [128, 2048], BF16, Y0 + p * 4 * KB) for p in range(8)]
    zfm = [place_at(f"zfm{p}", [128, 2048], BF16, Y0 + p * 4 * KB) for p in range(8)]
    silu = [place_at(f"silu{i}", [128, 4096], BF16, Y0 + i * 8 * KB) for i in range(2)]
    Z0 = Y0 + 32 * KB
    wo_bf = place_at("wo_bf", [128, 8192], BF16, Z0)
    pt = [[place_at(f"pt{h}_{i}", [128, 2048], BF16,
                    Z0 + 16 * KB + (h * 2 + i) * 4 * KB) for i in range(2)]
          for h in range(2)]
    xs = [place_at(f"xs{i}", [128, 1024], F32, Z0 + 32 * KB + i * 4 * KB)
          for i in range(2)]
    ntb = [place_at(f"ntb{i}", [128, 1024], BF16, Z0 + 40 * KB + i * 2 * KB)
           for i in range(2)]
    wqb = [place_at(f"wqb{i}", [128, 1024], BF16, Z0 + 44 * KB + i * 2 * KB)
           for i in range(2)]
    vscr = place_at("vscr", [128, 2048], BF16, Z0 + 48 * KB)
    xr = [place_at(f"xr{i}", [128, 1024], F32, Z0 + 16 * KB + i * 4 * KB)
          for i in range(2)]
    xb = [place_at(f"xb{i}", [128, 1024], F32, Z0 + 24 * KB + i * 4 * KB)
          for i in range(2)]
    n2t = [place_at(f"n2t{i}", [128, 1024], BF16, Z0 + 52 * KB + i * 2 * KB)
           for i in range(4)]
    w1b = [place_at(f"w1b{i}", [128, 8192], BF16, Z0 + i * 16 * KB) for i in range(2)]
    w2b = [place_at(f"w2b{i}", [128, 8192], BF16, Z0 + 32 * KB + i * 16 * KB)
           for i in range(2)]
    xg_fm = [[place_at(f"xgfm{par}_{dc}", [128, 1024], BF16,
                       NF0 + par * 16 * KB + dc * 2 * KB) for dc in range(8)]
             for par in range(2)]
    geo = [[place_at(f"geo{g}_{i}", [128, 1024], BF16,
                     Y0 + 16 * KB + (g * 2 + i) * 2 * KB) for i in range(2)]
           for g in range(2)]
    eo_sb = [place_at(f"eosb{i}", [128, 1024], BF16, Y0 + 24 * KB + i * 2 * KB)
             for i in range(2)]
    assert Z0 + 64 * KB <= nc.sbuf_top, (Z0 + 64 * KB, nc.sbuf_top)

    with ExitStack() as top:
        sem_sc = top.enter_context(nc.semaphore("sc"))
        sem_gw = top.enter_context(nc.semaphore("gwl"))
        sem_x = [top.enter_context(nc.semaphore(f"x{i}")) for i in range(2)]
        sem_wq = [top.enter_context(nc.semaphore(f"wq{i}")) for i in range(2)]
        sem_wo = top.enter_context(nc.semaphore("wol"))
        sem_wf = [top.enter_context(nc.semaphore(f"wf{i}")) for i in range(2)]
        sem_st = top.enter_context(nc.semaphore("st"))
        sem_sct = [top.enter_context(nc.semaphore(f"sct{i}")) for i in range(4)]
        sem_eo2 = [top.enter_context(nc.semaphore(f"eo{i}")) for i in range(2)]
        sem_xgt = [top.enter_context(nc.semaphore(f"xgt{i}")) for i in range(2)]
        sem_geo = [top.enter_context(nc.semaphore(f"geo{i}")) for i in range(2)]
        sem_zi = top.enter_context(nc.semaphore("zi"))
        sem_pe = top.enter_context(nc.semaphore("pe"))
        sem_dv = top.enter_context(nc.semaphore("dv"))
        sem_ac = top.enter_context(nc.semaphore("ac"))
        sem_po = top.enter_context(nc.semaphore("po"))
        blk = top.enter_context(nc.Block())

        st = SC(sem_st, 16)
        c_sct = [SC(sem_sct[i], 16) for i in range(4)]
        c_eo2 = [SC(sem_eo2[0], 16), SC(sem_eo2[1], 16)]
        c_xgt = [SC(sem_xgt[0], 16), SC(sem_xgt[1], 16)]
        c_geo = [SC(sem_geo[0], 16), SC(sem_geo[1], 16)]
        c_zi = SC(sem_zi, 16)
        c_x = [SC(sem_x[0], 16), SC(sem_x[1], 16)]
        c_wq = [SC(sem_wq[0], 16), SC(sem_wq[1], 16)]
        c_wo = SC(sem_wo, 16)
        c_wf = [SC(sem_wf[0], 16), SC(sem_wf[1], 16)]
        pe, dv, ac, po = SC(sem_pe), SC(sem_dv), SC(sem_ac), SC(sem_po)

        @blk.sync
        def _(e):
            e.dma_start(out=bqkv_t[:, :], in_=bqkv_h[:, :]).then_inc(sem_sc, 16)
            e.dma_start(out=bo_row[:, :], in_=bo_h[:, :]).then_inc(sem_sc, 16)
        v_bias_ld = 32

        @blk.gpsimd
        def _(e):
            e.dma_start(out=gate_bf[:, :], in_=gw_h[:, :]).then_inc(sem_gw, 16)
            e.memset(ident[:, :], 0.0).then_inc(sem_po, 1)
            e.memset(tri[:, :], 0.0).then_inc(sem_po, 1)
            e.memset(ones_r[:, :], 1.0).then_inc(sem_po, 1)
            e.memset(cb_ln1[:, :], float(EPS2 * EPS1)).then_inc(sem_po, 1)
            e.memset(cb_eps[:, :], float(EPS1)).then_inc(sem_po, 1)
            e.memset(cb_att[:, :], -44.0).then_inc(sem_po, 1)
            e.memset(identf[:, :], 0.0).then_inc(sem_po, 1)
            e.memset(zero8[:, :], 0.0).then_inc(sem_po, 1)
            e.memset(silu[0][:, :], 0.0).then_inc(sem_po, 1)
            e.iota(eidx[:, :], pattern=[[1, 8]], base=0, channel_multiplier=0,
                   allow_small_or_imprecise_dtypes=True).then_inc(sem_po, 1)
            e.wait_ge(sem_po, 10)
            e.affine_select(out=identf[:, :], in_=identf[:, :],
                            compare_op=ALU.not_equal, fill=1.0, base=0,
                            pattern=[[-1, 128]], channel_multiplier=1
                            ).then_inc(sem_po, 1)
            e.wait_ge(sem_po, 11)
            e.affine_select(out=ident[:, :], in_=ident[:, :], compare_op=ALU.not_equal,
                            fill=1.0, base=0, pattern=[[-1, 128]],
                            channel_multiplier=1).then_inc(sem_po, 1)
            e.wait_ge(sem_po, 12)
            e.affine_select(out=tri[:, :], in_=tri[:, :], compare_op=ALU.is_gt,
                            fill=1.0, base=0, pattern=[[-1, 128]],
                            channel_multiplier=1).then_inc(sem_po, 1)
        po.n += 13
        v_gate_ld = 16
        v_consts = po.n

        with nc.psum_tensor("bc_ps", [128, 1024], F32) as bc_ps:
            @blk.tensor
            def _(e):
                e.wait_ge(sem_sc, v_bias_ld)
                e.wait_ge(sem_po, v_consts)
                e.matmul(bc_ps[:, 0:512], ones_r[:, :], bo_row[:, 0:512],
                         start=True, stop=True)
                e.matmul(bc_ps[:, 512:1024], ones_r[:, :], bo_row[:, 512:1024],
                         start=True, stop=True).then_inc(sem_pe, 1)
            pe.n += 1
            v_bc_mm = pe.n

            @blk.sync
            def _(e):
                e.wait_ge(sem_po, v_consts)
                nzc = 8 * CAPE // 512
                for c in range(nzc):
                    e.dma_start(out=xg_d[c * 512:(c + 1) * 512, :],
                                in_=silu[0][:, :]).then_inc(sem_zi, 16)
                for c in range(nzc):
                    e.dma_start(out=eo_d[c * 512:(c + 1) * 512, :],
                                in_=silu[0][:, :]).then_inc(sem_zi, 16)
            c_zi.n += 16 * 2 * (8 * CAPE // 512)
            v_zi = c_zi.n

            @blk.vector
            def _(e):
                e.wait_ge(sem_pe, v_bc_mm)
                e.tensor_copy(bo_bc[:, :], bc_ps[:, :]).then_inc(sem_dv, 1)
            dv.n += 1
        v_bc_done = dv.n

        qfbs = list(range(0, n_pairs)) + list(range(8, 8 + n_pairs)) \
            + list(range(16, 16 + n_pairs))

        actq = DeferredQ()
        for s in range(n_seq):
            if s > 0:
                prev = dict(pe=pe.n, dv=dv.n, ac=ac.n, po=po.n, st=st.n)

                def qwaits(e):
                    e.wait_ge(sem_pe, prev["pe"])
                    e.wait_ge(sem_dv, prev["dv"])
                    e.wait_ge(sem_ac, prev["ac"])
                    e.wait_ge(sem_po, prev["po"])
                    e.wait_ge(sem_st, prev["st"])

                @blk.sync
                def _(e):
                    qwaits(e)

                @blk.gpsimd
                def _(e):
                    qwaits(e)

                @blk.tensor
                def _(e):
                    qwaits(e)

                @blk.vector
                def _(e):
                    qwaits(e)

                @blk.scalar
                def _(e):
                    qwaits(e)

            with ExitStack() as p1ps:
                tp_ps = [p1ps.enter_context(
                    nc.psum_tensor(f"tp{i}_{s}", [128, 128], BF16)) for i in range(2)]
                qkv_ps = [p1ps.enter_context(
                    nc.psum_tensor(f"qk{i}_{s}", [128, 512], F32)) for i in range(4)]

                xs_free = [0, 0]
                ntb_free = [0, 0]
                tp_free = [0, 0]
                nf_ready = [0] * TT

                for t in range(TT):
                    pi = t % 2

                    @blk.sync
                    def _(e, t=t, pi=pi):
                        if xs_free[pi]:
                            e.wait_ge(sem_dv, xs_free[pi])
                        e.dma_start(out=xs[pi][:, :],
                                    in_=x_h[s, t * 128:(t + 1) * 128, :]
                                    ).then_inc(sem_x[pi], 16)
                    c_x[pi].n += 16
                    v_x = c_x[pi].n

                    @blk.vector
                    def _(e, pi=pi, v_x=v_x):
                        e.wait_ge(sem_x[pi], v_x)
                        e.bn_stats(bnb[pi][:, 0:6], xs[pi][:, 0:512])
                        e.bn_stats(bnb[pi][:, 6:12], xs[pi][:, 512:1024]
                                   ).then_inc(sem_dv, 1)
                        e.wait_ge(sem_dv, dv.n + 1)
                        e.bn_aggr(bno[pi][:, :], bnb[pi][:, :]).then_inc(sem_dv, 1)
                    dv.n += 2
                    v_aggr = dv.n

                    def _f(e, t=t, pi=pi, v_aggr=v_aggr, s=s, ac0=ac.n):
                        if t == 0 and s == 0:
                            e.wait_ge(sem_po, v_consts)
                        e.wait_ge(sem_dv, v_aggr)
                        e.activation(aab[pi][:, :], bno[pi][:, 1:2], AF.Ln,
                                     scale=float(1.0 + EPS2),
                                     bias=cb_ln1[:, :]).then_inc(sem_ac, 1)
                        e.wait_ge(sem_ac, ac0 + 1)
                        e.activation(aab[pi][:, :], aab[pi][:, :], AF.Exp,
                                     scale=-0.5).then_inc(sem_ac, 1)
                    actq.add(_f)
                    ac.n += 2
                    v_scale = ac.n

                    @blk.vector
                    def _(e, pi=pi, v_scale=v_scale):
                        e.wait_ge(sem_ac, v_scale)
                        e.tensor_scalar(bbb[pi][:, :], bno[pi][:, 0:1], aab[pi][:, :],
                                        -1.0, ALU.mult, ALU.mult).then_inc(sem_dv, 1)
                        e.wait_ge(sem_dv, dv.n + 1)
                        if ntb_free[pi]:
                            e.wait_ge(sem_pe, ntb_free[pi])
                        e.tensor_scalar(ntb[pi][:, :], xs[pi][:, :], aab[pi][:, :],
                                        bbb[pi][:, :], ALU.mult, ALU.add
                                        ).then_inc(sem_dv, 1)
                    dv.n += 2
                    v_norm = dv.n
                    xs_free[pi] = dv.n

                    base_ac = ac.n

                    @blk.tensor
                    def _(e, t=t, pi=pi, v_norm=v_norm, base_ac=base_ac):
                        e.wait_ge(sem_dv, v_norm)
                        if t == 0 and s == 0:
                            e.wait_ge(sem_po, v_consts)
                        if s > 0 and t == 0:
                            pass
                        for c in range(8):
                            w = tp_free[c % 2] if c < 2 else base_ac + c - 1
                            if w:
                                e.wait_ge(sem_ac, w)
                            e.transpose(tp_ps[c % 2][:, :],
                                        ntb[pi][:, c * 128:(c + 1) * 128],
                                        ident[:, :]).then_inc(sem_pe, 1)
                    base_pe = pe.n
                    pe.n += 8
                    ntb_free[pi] = pe.n

                    def _f(e, t=t, base_pe=base_pe):
                        for c in range(8):
                            e.wait_ge(sem_pe, base_pe + c + 1)
                            e.copy(nf[c][:, t * 128:(t + 1) * 128],
                                   tp_ps[c % 2][:, :]).then_inc(sem_ac, 1)
                    actq.add(_f)
                    ac.n += 8
                    tp_free[0] = ac.n - 1
                    tp_free[1] = ac.n
                    nf_ready[t] = ac.n

                v_nf_all = ac.n

                wqb_free = [0, 0]
                qps_free = [0, 0, 0, 0]
                vscr_free = 0

                for k, fb in enumerate(qfbs):
                    pi = k % 2

                    @blk.gpsimd
                    def _(e, fb=fb, pi=pi):
                        if wqb_free[pi]:
                            e.wait_ge(sem_pe, wqb_free[pi])
                        e.dma_start(out=wqb[pi][:, :], in_=wqkv_h[fb, :, :]
                                    ).then_inc(sem_wq[pi], 16)
                    c_wq[pi].n += 16
                    v_wq = c_wq[pi].n

                    is_v = fb >= 16
                    if fb < 8:
                        dst = qp[fb]
                    elif fb < 16:
                        dst = kp[fb - 8]
                    else:
                        dst = vscr

                    for tc in range(4):
                        pj = (k * 4 + tc) % 4

                        @blk.tensor
                        def _(e, k=k, fb=fb, pi=pi, tc=tc, pj=pj, v_wq=v_wq):
                            e.wait_ge(sem_wq[pi], v_wq)
                            if k == 0:
                                e.wait_ge(sem_ac, nf_ready[4 * tc + 3])
                            if qps_free[pj]:
                                e.wait_ge(sem_ac, qps_free[pj])
                            for dc in range(8):
                                i = e.matmul(qkv_ps[pj][:, :],
                                             wqb[pi][:, dc * 128:(dc + 1) * 128],
                                             nf[dc][:, tc * 512:(tc + 1) * 512],
                                             start=(dc == 0), stop=(dc == 7))
                            i.then_inc(sem_pe, 1)
                        pe.n += 1
                        v_mm = pe.n
                        if tc == 3:
                            wqb_free[pi] = v_mm

                        def _f(e, fb=fb, tc=tc, pj=pj, v_mm=v_mm, dst=dst,
                               is_v=is_v, vf=vscr_free, s=s):
                            e.wait_ge(sem_pe, v_mm)
                            if tc == 0:
                                e.wait_ge(sem_sc, v_bias_ld)
                                if s == 0 and fb == 8:
                                    e.wait_ge(sem_zi, v_zi)
                                if is_v and vf:
                                    e.wait_ge(sem_pe, vf)
                            e.activation(dst[:, tc * 512:(tc + 1) * 512],
                                         qkv_ps[pj][:, :], AF.Identity,
                                         bias=bqkv_t[:, fb:fb + 1]).then_inc(sem_ac, 1)
                        actq.add(_f)
                        ac.n += 1
                        qps_free[pj] = ac.n

                    if is_v:
                        v_evac = ac.n
                        col = (fb - 16) * 128
                        base_ac2 = ac.n

                        @blk.tensor
                        def _(e, v_evac=v_evac, base_ac2=base_ac2):
                            e.wait_ge(sem_ac, v_evac)
                            for t in range(TT):
                                w = tp_free[t % 2] if t < 2 else base_ac2 + t - 1
                                if w:
                                    e.wait_ge(sem_ac, w)
                                e.transpose(tp_ps[t % 2][:, :],
                                            vscr[:, t * 128:(t + 1) * 128],
                                            ident[:, :]).then_inc(sem_pe, 1)
                        base_pe2 = pe.n
                        pe.n += TT
                        vscr_free = pe.n

                        def _f(e, col=col, base_pe2=base_pe2):
                            for t in range(TT):
                                e.wait_ge(sem_pe, base_pe2 + t + 1)
                                e.copy(vtm[t][:, col:col + 128], tp_ps[t % 2][:, :]
                                       ).then_inc(sem_ac, 1)
                        actq.add(_f)
                        ac.n += TT
                        tp_free[0] = ac.n - 1
                        tp_free[1] = ac.n

                v_qkv_ac = ac.n

                @blk.gpsimd
                def _(e):
                    if wqb_free[0]:
                        e.wait_ge(sem_pe, wqb_free[0])
                    e.dma_start(out=wo_bf[:, :], in_=wo_h[:, :]).then_inc(sem_wo, 16)
                c_wo.n += 16
                v_wo = c_wo.n
                actq.flush(blk.scalar)

            with ExitStack() as p2ps:
                s_ps = [p2ps.enter_context(
                    nc.psum_tensor(f"sp{i}_{s}", [128, 512], F32)) for i in range(4)]
                z_ps = p2ps.enter_context(nc.psum_tensor(f"zp_{s}", [128, 2048], F32))

                pt_free = [[0, 0], [0, 0]]
                rs4_free = [0, 0]
                sps_exp = [0, 0, 0, 0]
                vsc_free = [[0, 0], [0, 0]]
                zwin_stop = [0] * 4
                zev_done = [0] * 4
                scnt = [0]

                def emit_S(p, kt):
                    k0 = kt * 128
                    w0 = k0 // 512
                    qa = w0 * 512
                    nch = 4 - w0
                    res = {}
                    for h in range(2):
                        rows = slice(h * 64, h * 64 + 64)
                        v_p2 = [0] * nch
                        tiles = [0] * nch
                        base_ac3 = ac.n

                        @blk.tensor
                        def _(e, p=p, h=h, k0=k0, qa=qa, nch=nch, rows=rows,
                              base_ac3=base_ac3, v_p2=v_p2, tiles=tiles):
                            if p == 0 and kt == 0 and h == 0:
                                e.wait_ge(sem_ac, v_qkv_ac)
                            for c in range(nch):
                                j = scnt[0] % 4
                                scnt[0] += 1
                                tiles[c] = j
                                if sps_exp[j]:
                                    e.wait_ge(sem_ac, sps_exp[j])
                                qc0 = qa + c * 512
                                e.matmul(s_ps[j][:, :],
                                         kp[p][rows, k0:k0 + 128],
                                         qp[p][rows, qc0:qc0 + 512],
                                         start=True, stop=True).then_inc(sem_pe, 1)
                                pe.n += 1
                                v_p2[c] = pe.n

                        ppi = kt % 2

                        def _f(e, h=h, kt=kt, k0=k0, qa=qa, nch=nch, ppi=ppi,
                               v_p2=v_p2, tiles=list(tiles),
                               ptw=pt_free[h][ppi], rs4w=rs4_free[h], ac0=ac.n):
                            for c in range(nch):
                                e.wait_ge(sem_pe, v_p2[c])
                                if c == 0:
                                    if ptw:
                                        e.wait_ge(sem_pe, ptw)
                                    off = k0 - qa
                                    e.activation(pt[h][ppi][:, k0:qa + 512],
                                                 s_ps[tiles[c]][:, off:512], AF.Exp,
                                                 bias=cb_att[:, :]).then_inc(sem_ac, 1)
                                else:
                                    if c == 1 and rs4w:
                                        e.wait_ge(sem_dv, rs4w)
                                    e.activation(
                                        pt[h][ppi][:, qa + c * 512:qa + (c + 1) * 512],
                                        s_ps[tiles[c]][:, :], AF.Exp,
                                        bias=cb_att[:, :],
                                        accum_out=rs4[h][:, c:c + 1]).then_inc(sem_ac, 1)
                        actq.add(_f)
                        for c in range(nch):
                            ac.n += 1
                            sps_exp[tiles[c]] = ac.n
                        v_exp = ac.n
                        v_exp0 = v_exp - nch + 1

                        v_zero = po.n
                        if qa < k0:
                            @blk.gpsimd
                            def _(e, h=h, qa=qa, k0=k0, ppi=ppi,
                                  ptw=pt_free[h][ppi]):
                                if ptw:
                                    e.wait_ge(sem_pe, ptw)
                                e.memset(pt[h][ppi][:, qa:k0], 0.0).then_inc(sem_po, 1)
                            po.n += 1
                            v_zero = po.n

                        @blk.vector
                        def _(e, h=h, p=p, kt=kt, k0=k0, qa=qa, nch=nch, ppi=ppi,
                              v_exp=v_exp, v_exp0=v_exp0, v_zero=v_zero):
                            e.wait_ge(sem_ac, v_exp0)
                            e.tensor_tensor(pt[h][ppi][:, k0:k0 + 128],
                                            pt[h][ppi][:, k0:k0 + 128], tri[:, :],
                                            ALU.mult).then_inc(sem_dv, 1)
                            e.wait_ge(sem_dv, dv.n + 1)
                            if qa < k0:
                                e.wait_ge(sem_po, v_zero)
                            e.tensor_reduce(rs4[h][:, 0:1],
                                            pt[h][ppi][:, qa:qa + 512], AX.X,
                                            ALU.add).then_inc(sem_dv, 1)
                            e.wait_ge(sem_ac, v_exp)
                            e.wait_ge(sem_dv, dv.n + 2)
                            e.tensor_reduce(rst[h][:, :], rs4[h][:, 0:nch], AX.X,
                                            ALU.add).then_inc(sem_dv, 1)
                            e.wait_ge(sem_dv, dv.n + 3)
                            e.reciprocal(rin[h][:, :], rst[h][:, :]).then_inc(sem_dv, 1)
                            e.wait_ge(sem_dv, dv.n + 4)
                            if vsc_free[h][kt % 2]:
                                e.wait_ge(sem_pe, vsc_free[h][kt % 2])
                            e.tensor_scalar(
                                vsc[h][kt % 2][:, :],
                                vtm[kt][:, (2 * p + h) * 64:(2 * p + h) * 64 + 64],
                                rin[h][:, :], None, ALU.mult).then_inc(sem_dv, 1)
                        dv.n += 5
                        rs4_free[h] = dv.n - 2
                        res[h] = (dv.n, v_zero, ppi)
                    return res

                def emit_Z(p, kt, res):
                    k0 = kt * 128
                    w0 = k0 // 512
                    for h in range(2):
                        v_vsc, v_zero, ppi = res[h]

                        @blk.tensor
                        def _(e, h=h, p=p, kt=kt, w0=w0, ppi=ppi,
                              v_vsc=v_vsc, v_zero=v_zero):
                            e.wait_ge(sem_dv, v_vsc)
                            if v_zero:
                                e.wait_ge(sem_po, v_zero)
                            for w in range(w0, 4):
                                if kt == 0 and p > 0 and zev_done[w]:
                                    e.wait_ge(sem_dv, zev_done[w])
                                orow = slice(h * 64, h * 64 + 64)
                                e.matmul(z_ps[orow, w * 512:(w + 1) * 512],
                                         vsc[h][kt % 2][:, :],
                                         pt[h][ppi][:, w * 512:(w + 1) * 512],
                                         start=(kt == 0), stop=(kt == 4 * w + 3),
                                         tile_position=(0, h * 64),
                                         skip_group_check=True).then_inc(sem_pe, 1)
                                pe.n += 1
                                if kt == 4 * w + 3 and h == 1:
                                    zwin_stop[w] = pe.n
                        vsc_free[h][kt % 2] = pe.n
                        pt_free[h][ppi] = pe.n

                for p in range(n_pairs):
                    prev = None
                    for kt in range(TT):
                        res = emit_S(p, kt)
                        if prev is not None:
                            emit_Z(p, kt - 1, prev)
                        prev = res
                    emit_Z(p, TT - 1, prev)

                    for w in range(4):
                        @blk.vector
                        def _(e, p=p, w=w):
                            e.wait_ge(sem_pe, zwin_stop[w])
                            e.tensor_copy(zfm[p][:, w * 512:(w + 1) * 512],
                                          z_ps[:, w * 512:(w + 1) * 512]
                                          ).then_inc(sem_dv, 1)
                        dv.n += 1
                        zev_done[w] = dv.n

                v_attn_dv = dv.n
                v_attn_pe = pe.n
                actq.flush(blk.scalar)

            with ExitStack() as p3ps:
                op_ps = [p3ps.enter_context(
                    nc.psum_tensor(f"op{i}_{s}", [128, 1024], F32)) for i in range(2)]
                tp2_ps = [p3ps.enter_context(
                    nc.psum_tensor(f"t2{i}_{s}", [128, 128], BF16)) for i in range(2)]
                lg_ps = p3ps.enter_context(
                    nc.psum_tensor(f"lg_{s}", [128, 8], F32))
                tpm_ps = p3ps.enter_context(
                    nc.psum_tensor(f"tpm_{s}", [128, 128], F32))
                ctm_ps = tpm_ps

                op_free = [0, 0]
                xr_free = [0, 0]
                xb_free = [0, 0]
                tp2_free = [0, 0]
                n2t_free = [0, 0, 0, 0]
                n2t_sct = [0, 0, 0, 0]
                lg_free = [0]
                tpm_free = [0]
                ctm_free = [0]
                gstate = {}

                def emit_gating_tail(t):
                    pi = gstate[t]["pi"]
                    v_gm = gstate[t]["v_gm"]

                    def _f(e, pi=pi, v_gm=v_gm):
                        e.wait_ge(sem_dv, v_gm)
                        e.activation(glt[pi][:, :], lgs[pi][:, :], AF.Exp,
                                     bias=gm1[pi][:, :]).then_inc(sem_ac, 1)
                    actq.add(_f)
                    ac.n += 1
                    v_ge = ac.n

                    @blk.tensor
                    def _(e, pi=pi, v_gm=v_gm, tw=tpm_free[0]):
                        e.wait_ge(sem_dv, v_gm)
                        if tw:
                            e.wait_ge(sem_dv, tw)
                        e.transpose(tpm_ps[0:8, 0:128], gt1[pi][:, :], identf[:, :]
                                    ).then_inc(sem_pe, 1)
                    pe.n += 1
                    v_mt = pe.n

                    @blk.vector
                    def _(e, t=t, pi=pi, v_ge=v_ge, v_mt=v_mt, dv0=dv.n):
                        e.wait_ge(sem_ac, v_ge)
                        e.tensor_tensor(glt[pi][:, :], glt[pi][:, :], gt1[pi][:, :],
                                        ALU.mult).then_inc(sem_dv, 1)
                        e.wait_ge(sem_dv, dv0 + 1)
                        e.tensor_reduce(gsum[pi][:, :], glt[pi][:, :], AX.X, ALU.add
                                        ).then_inc(sem_dv, 1)
                        e.wait_ge(sem_dv, dv0 + 2)
                        e.reciprocal(gm1[pi][:, :], gsum[pi][:, :]).then_inc(sem_dv, 1)
                        e.wait_ge(sem_dv, dv0 + 3)
                        e.tensor_scalar(gates[t][:, :], glt[pi][:, :], gm1[pi][:, :],
                                        None, ALU.mult).then_inc(sem_dv, 1)
                        e.wait_ge(sem_dv, dv0 + 4)
                        e.tensor_tensor(oh2[pi][:, :], gt1[pi][:, :], oh1[pi][:, :],
                                        ALU.subtract).then_inc(sem_dv, 1)
                        e.wait_ge(sem_dv, dv0 + 5)
                        e.scalar_tensor_tensor(tf8[pi][:, :], eidx[:, :], 1.0,
                                               oh1[pi][:, :], ALU.mult, ALU.mult,
                                               accum_out=e1f[pi][:, :]).then_inc(sem_dv, 1)
                        e.wait_ge(sem_dv, dv0 + 6)
                        e.scalar_tensor_tensor(tf8[pi][:, :], eidx[:, :], 1.0,
                                               oh2[pi][:, :], ALU.mult, ALU.mult,
                                               accum_out=e2f[pi][:, :]).then_inc(sem_dv, 1)
                        e.wait_ge(sem_dv, dv0 + 7)
                        e.scalar_tensor_tensor(tf8[pi][:, :], gates[t][:, :], 1.0,
                                               oh1[pi][:, :], ALU.mult, ALU.mult,
                                               accum_out=gsl1[t][:, :]).then_inc(sem_dv, 1)
                        e.wait_ge(sem_dv, dv0 + 8)
                        e.scalar_tensor_tensor(tf8[pi][:, :], gates[t][:, :], 1.0,
                                               oh2[pi][:, :], ALU.mult, ALU.mult,
                                               accum_out=gsl2[t][:, :]).then_inc(sem_dv, 1)
                        e.wait_ge(sem_pe, v_mt)
                        e.tensor_tensor_scan(cumf[:, :], tpm_ps[0:8, 0:128],
                                             zero8[:, :],
                                             (0.0 if t == 0 else cuml[:, :]),
                                             ALU.add, ALU.add).then_inc(sem_dv, 1)
                        e.wait_ge(sem_dv, dv0 + 10)
                        e.tensor_copy(cuml[:, :], cumf[:, 127:128]).then_inc(sem_dv, 1)
                    dv.n += 11
                    tpm_free[0] = dv.n
                    v_scan = dv.n

                    @blk.tensor
                    def _(e, v_scan=v_scan, cw=ctm_free[0]):
                        e.wait_ge(sem_dv, v_scan)
                        if cw:
                            e.wait_ge(sem_dv, cw)
                        e.transpose(ctm_ps[0:128, 0:8], cumf[:, :], identf[0:8, 0:8]
                                    ).then_inc(sem_pe, 1)
                    pe.n += 1
                    v_ct = pe.n

                    @blk.vector
                    def _(e, t=t, pi=pi, v_ct=v_ct, dv0=dv.n):
                        e.wait_ge(sem_pe, v_ct)
                        e.scalar_tensor_tensor(tf8[pi][:, :], ctm_ps[0:128, 0:8], 1.0,
                                               oh1[pi][:, :], ALU.mult, ALU.mult,
                                               accum_out=s1f[pi][:, :]).then_inc(sem_dv, 1)
                        e.wait_ge(sem_dv, dv0 + 1)
                        e.scalar_tensor_tensor(tf8[pi][:, :], ctm_ps[0:128, 0:8], 1.0,
                                               oh2[pi][:, :], ALU.mult, ALU.mult,
                                               accum_out=s2f[pi][:, :]).then_inc(sem_dv, 1)
                        e.wait_ge(sem_dv, dv0 + 2)
                        e.tensor_scalar(slf[pi][:, :], s1f[pi][:, :], -1.0,
                                        float(CAPE - 1), ALU.add, ALU.min
                                        ).then_inc(sem_dv, 1)
                        e.wait_ge(sem_dv, dv0 + 3)
                        e.scalar_tensor_tensor(s1f[pi][:, :], e1f[pi][:, :],
                                               float(CAPE), slf[pi][:, :],
                                               ALU.mult, ALU.add).then_inc(sem_dv, 1)
                        e.wait_ge(sem_dv, dv0 + 4)
                        e.tensor_copy(sidx1[t][:, :], s1f[pi][:, :]).then_inc(sem_dv, 1)
                        e.tensor_scalar(slf[pi][:, :], s2f[pi][:, :], -1.0,
                                        float(CAPE - 1), ALU.add, ALU.min
                                        ).then_inc(sem_dv, 1)
                        e.wait_ge(sem_dv, dv0 + 6)
                        e.scalar_tensor_tensor(s2f[pi][:, :], e2f[pi][:, :],
                                               float(CAPE), slf[pi][:, :],
                                               ALU.mult, ALU.add).then_inc(sem_dv, 1)
                        e.wait_ge(sem_dv, dv0 + 7)
                        e.tensor_copy(sidx2[t][:, :], s2f[pi][:, :]).then_inc(sem_dv, 1)
                    dv.n += 8
                    ctm_free[0] = dv.n
                    tpm_free[0] = dv.n
                    v_sidx = dv.n

                    @blk.gpsimd
                    def _(e, t=t, pi=pi, v_sidx=v_sidx):
                        e.wait_ge(sem_dv, v_sidx)
                        e.indirect_dma_start(
                            out=xg_d[:, :],
                            out_offset=bass.IndirectOffsetOnAxis(
                                ap=sidx1[t][:, :], axis=0),
                            in_=n2t[t % 4][:, :], in_offset=None
                            ).then_inc(sem_sct[t % 4], 16)
                        e.indirect_dma_start(
                            out=xg_d[:, :],
                            out_offset=bass.IndirectOffsetOnAxis(
                                ap=sidx2[t][:, :], axis=0),
                            in_=n2t[t % 4][:, :], in_offset=None
                            ).then_inc(sem_sct[t % 4], 16)
                    c_sct[t % 4].n += 32
                    n2t_sct[t % 4] = c_sct[t % 4].n

                for t in range(TT):
                    pi = t % 2

                    @blk.sync
                    def _(e, t=t, pi=pi):
                        if t < 2:
                            e.wait_ge(sem_pe, v_attn_pe)
                        if xr_free[pi]:
                            e.wait_ge(sem_po, xr_free[pi])
                        e.dma_start(out=xr[pi][:, :],
                                    in_=x_h[s, t * 128:(t + 1) * 128, :]
                                    ).then_inc(sem_x[pi], 16)
                    c_x[pi].n += 16
                    v_xr = c_x[pi].n

                    @blk.gpsimd
                    def _(e, pi=pi, v_xr=v_xr):
                        e.wait_ge(sem_x[pi], v_xr)
                        if s == 0 and pi == 0:
                            e.wait_ge(sem_dv, v_bc_done)
                        if xb_free[pi]:
                            e.wait_ge(sem_dv, xb_free[pi])
                        e.tensor_tensor(xb[pi][:, :], xr[pi][:, :], bo_bc[:, :],
                                        ALU.add).then_inc(sem_po, 1)
                    po.n += 1
                    v_xb = po.n
                    xr_free[pi] = po.n

                    @blk.tensor
                    def _(e, t=t, pi=pi):
                        if t == 0:
                            e.wait_ge(sem_dv, v_attn_dv)
                            e.wait_ge(sem_wo, v_wo)
                        if op_free[pi]:
                            e.wait_ge(sem_dv, op_free[pi])
                        for dc in range(n_pairs):
                            for j in range(2):
                                i = e.matmul(op_ps[pi][:, j * 512:(j + 1) * 512],
                                             zfm[dc][:, t * 128:(t + 1) * 128],
                                             wo_bf[:, dc * 1024 + j * 512:
                                                   dc * 1024 + (j + 1) * 512],
                                             start=(dc == 0), stop=(dc == n_pairs - 1))
                        i.then_inc(sem_pe, 1)
                    pe.n += 1
                    v_op = pe.n

                    @blk.vector
                    def _(e, t=t, pi=pi, v_op=v_op, v_xb=v_xb):
                        e.wait_ge(sem_pe, v_op)
                        e.wait_ge(sem_po, v_xb)
                        e.tensor_tensor(xn[t][:, :], op_ps[pi][:, :], xb[pi][:, :],
                                        ALU.add).then_inc(sem_dv, 1)
                        e.wait_ge(sem_dv, dv.n + 1)
                        e.bn_stats(bnb[pi][:, 0:6], xn[t][:, 0:512])
                        e.bn_stats(bnb[pi][:, 6:12], xn[t][:, 512:1024]
                                   ).then_inc(sem_dv, 1)
                        e.wait_ge(sem_dv, dv.n + 2)
                        e.bn_aggr(bno[pi][:, :], bnb[pi][:, :]).then_inc(sem_dv, 1)
                    dv.n += 3
                    v_ag2 = dv.n
                    op_free[pi] = dv.n
                    xb_free[pi] = dv.n

                    def _f(e, pi=pi, v_ag2=v_ag2, ac0=ac.n):
                        e.wait_ge(sem_dv, v_ag2)
                        e.activation(aab[pi][:, :], bno[pi][:, 1:2], AF.Ln,
                                     bias=cb_eps[:, :]).then_inc(sem_ac, 1)
                        e.wait_ge(sem_ac, ac0 + 1)
                        e.activation(aab[pi][:, :], aab[pi][:, :], AF.Exp,
                                     scale=-0.5).then_inc(sem_ac, 1)
                    actq.add(_f)
                    ac.n += 2
                    v_sc2 = ac.n

                    @blk.vector
                    def _(e, t=t, pi=pi, v_sc2=v_sc2):
                        e.wait_ge(sem_ac, v_sc2)
                        e.tensor_scalar(bbb[pi][:, :], bno[pi][:, 0:1], aab[pi][:, :],
                                        -1.0, ALU.mult, ALU.mult).then_inc(sem_dv, 1)
                        e.wait_ge(sem_dv, dv.n + 1)
                        if n2t_free[t % 4]:
                            e.wait_ge(sem_pe, n2t_free[t % 4])
                        if n2t_sct[t % 4]:
                            e.wait_ge(sem_sct[t % 4], n2t_sct[t % 4])
                        e.tensor_scalar(n2t[t % 4][:, :], xn[t][:, :], aab[pi][:, :],
                                        bbb[pi][:, :], ALU.mult, ALU.add
                                        ).then_inc(sem_dv, 1)
                    dv.n += 2
                    v_n2 = dv.n

                    base_ac4 = ac.n

                    @blk.tensor
                    def _(e, pi=pi, v_n2=v_n2, base_ac4=base_ac4):
                        e.wait_ge(sem_dv, v_n2)
                        for c in range(8):
                            w = tp2_free[c % 2] if c < 2 else base_ac4 + c - 1
                            if w:
                                e.wait_ge(sem_ac, w)
                            e.transpose(tp2_ps[c % 2][:, :],
                                        n2t[t % 4][:, c * 128:(c + 1) * 128],
                                        ident[:, :]).then_inc(sem_pe, 1)
                    base_pe4 = pe.n
                    pe.n += 8
                    n2t_free[t % 4] = pe.n

                    def _f(e, t=t, base_pe4=base_pe4):
                        for c in range(8):
                            e.wait_ge(sem_pe, base_pe4 + c + 1)
                            e.copy(nf[c][:, t * 128:(t + 1) * 128],
                                   tp2_ps[c % 2][:, :]).then_inc(sem_ac, 1)
                    actq.add(_f)
                    ac.n += 8
                    tp2_free[0] = ac.n - 1
                    tp2_free[1] = ac.n
                    v_n2fm = ac.n

                    @blk.tensor
                    def _(e, t=t, pi=pi, v_n2fm=v_n2fm):
                        e.wait_ge(sem_ac, v_n2fm)
                        if t == 0 and s == 0:
                            e.wait_ge(sem_gw, v_gate_ld)
                        if lg_free[0]:
                            e.wait_ge(sem_dv, lg_free[0])
                        for dc in range(8):
                            i = e.matmul(lg_ps[:, :],
                                         nf[dc][:, t * 128:(t + 1) * 128],
                                         gate_bf[:, dc * 8:(dc + 1) * 8],
                                         start=(dc == 0), stop=(dc == 7))
                        i.then_inc(sem_pe, 1)
                    pe.n += 1
                    v_lg = pe.n

                    @blk.vector
                    def _(e, t=t, pi=pi, v_lg=v_lg, dv0=dv.n):
                        e.wait_ge(sem_pe, v_lg)
                        e.tensor_copy(lgs[pi][:, :], lg_ps[:, :]).then_inc(sem_dv, 1)
                        e.wait_ge(sem_dv, dv0 + 1)
                        e.tensor_reduce(gm1[pi][:, :], lgs[pi][:, :], AX.X, ALU.max,
                                        negate=True).then_inc(sem_dv, 1)
                        e.wait_ge(sem_dv, dv0 + 2)
                        e.tensor_scalar(oh1[pi][:, :], lgs[pi][:, :], gm1[pi][:, :],
                                        0.0, ALU.add, ALU.is_ge).then_inc(sem_dv, 1)
                        e.tensor_scalar(glt[pi][:, :], lgs[pi][:, :], gm1[pi][:, :],
                                        0.0, ALU.add, ALU.is_lt).then_inc(sem_dv, 1)
                        e.wait_ge(sem_dv, dv0 + 4)
                        e.tensor_scalar(gt1[pi][:, :], glt[pi][:, :], 1e30, -1e30,
                                        ALU.mult, ALU.add).then_inc(sem_dv, 1)
                        e.wait_ge(sem_dv, dv0 + 5)
                        e.tensor_tensor(gt1[pi][:, :], lgs[pi][:, :], gt1[pi][:, :],
                                        ALU.add).then_inc(sem_dv, 1)
                        e.wait_ge(sem_dv, dv0 + 6)
                        e.tensor_reduce(gm2[pi][:, :], gt1[pi][:, :], AX.X, ALU.max
                                        ).then_inc(sem_dv, 1)
                        e.wait_ge(sem_dv, dv0 + 7)
                        e.tensor_scalar(gt1[pi][:, :], lgs[pi][:, :], gm2[pi][:, :],
                                        None, ALU.is_ge).then_inc(sem_dv, 1)
                    lg_free[0] = dv.n + 1
                    dv.n += 8
                    gstate[t] = dict(pi=pi, v_gm=dv.n, v_n2=v_n2)

                    if t > 0:
                        emit_gating_tail(t - 1)

                for t in (TT - 1,):
                    emit_gating_tail(t)

                v_p4_dv = dv.n
                v_p4_dv = dv.n
                v_p4_ac = ac.n
                v_p4_pe = pe.n
                v_p4_po = po.n
                actq.flush(blk.scalar)

            with ExitStack() as p5ps:
                ha_ps = [p5ps.enter_context(
                    nc.psum_tensor(f"ha{i}_{s}", [128, 512], F32)) for i in range(2)]
                fb_ps = [p5ps.enter_context(
                    nc.psum_tensor(f"fp{i}_{s}", [128, 1024], F32)) for i in range(2)]

                w_free = [0, 0]
                ha_free = [0, 0]
                silu_free = [0, 0]
                fbps_free = [0, 0]
                eo_sb_free = [0, 0]
                phaseA_last = {}
                xgt_val = {}
                xgf_free = [0, 0]

                def emit_xgT(e_x, wait_pe):
                    par = (e_x - 1) % 2

                    @blk.sync
                    def _(e, e_x=e_x, par=par, wait_pe=wait_pe,
                          vs=[c_sct[q].n for q in range(4)]):
                        e.wait_ge(sem_pe, wait_pe)
                        for q in range(4):
                            e.wait_ge(sem_sct[q], vs[q])
                        for dc in range(8):
                            e.dma_start(
                                out=xg_fm[par][dc][:, 0:CAPE],
                                in_=xg_d[(e_x - 1) * CAPE:e_x * CAPE,
                                         dc * 128:(dc + 1) * 128],
                                transpose=True).then_inc(sem_xgt[par], 16)
                    c_xgt[par].n += 16 * 8
                    xgt_val[e_x] = (par, c_xgt[par].n)

                n_es = n_exp * n_sub
                v_w_arr = [0] * n_es

                def emit_wdma(es2):
                    e2_i, sub2 = divmod(es2, n_sub)
                    wpi2 = es2 % 2

                    @blk.gpsimd
                    def _(e, e2_i=e2_i, sub2=sub2, wpi2=wpi2, es2=es2):
                        if es2 == 0:
                            e.wait_ge(sem_dv, v_p4_dv)
                            e.wait_ge(sem_ac, v_p4_ac)
                            e.wait_ge(sem_pe, v_p4_pe)
                            e.wait_ge(sem_po, v_p4_po)
                            for q in range(4):
                                e.wait_ge(sem_sct[q], c_sct[q].n)
                        if w_free[wpi2]:
                            e.wait_ge(sem_pe, w_free[wpi2])
                        e.dma_start(out=w1b[wpi2][:, :], in_=w1_h[e2_i, sub2, :, :],
                                    max_dma_last_dim=8192).then_inc(sem_wf[wpi2], 16)
                        e.dma_start(out=w2b[wpi2][:, :], in_=w2_h[e2_i, sub2, :, :],
                                    max_dma_last_dim=8192).then_inc(sem_wf[wpi2], 16)
                    c_wf[wpi2].n += 32
                    v_w_arr[es2] = c_wf[wpi2].n

                emit_wdma(0)
                for es in range(n_es):
                    e_i, sub = divmod(es, n_sub)
                    wpi = es % 2
                    if e_i == 0:
                        blocks = [(tb * 512, 512) for tb in range(4)]
                    else:
                        blocks = [(0, 512), (512, CAPE - 512)]
                    ntbk = len(blocks)
                    if es + 1 < n_es:
                        emit_wdma(es + 1)
                    v_w = v_w_arr[es]

                    xpar = (e_i - 1) % 2
                    for tb, (c0, cw) in enumerate(blocks):
                        spi = tb % 2
                        for hc in range(8):
                            hpi = hc % 2

                            @blk.tensor
                            def _(e, e_i=e_i, sub=sub, hc=hc, hpi=hpi, tb=tb,
                                  spi=spi, wpi=wpi, v_w=v_w, xpar=xpar,
                                  c0=c0, cw=cw):
                                if tb == 0 and hc == 0:
                                    e.wait_ge(sem_wf[wpi], v_w)
                                    if e_i >= 1 and sub == 0:
                                        xp, xv = xgt_val[e_i]
                                        e.wait_ge(sem_xgt[xp], xv)
                                if ha_free[hpi]:
                                    e.wait_ge(sem_ac, ha_free[hpi])
                                if hc < 2 and silu_free[spi]:
                                    e.wait_ge(sem_pe, silu_free[spi])
                                for dc in range(8):
                                    rhs = (nf[dc][:, c0:c0 + cw]
                                           if e_i == 0 else
                                           xg_fm[xpar][dc][:, c0:c0 + cw])
                                    i = e.matmul(
                                        ha_ps[hpi][:, 0:cw],
                                        w1b[wpi][:, dc * 1024 + hc * 128:
                                                 dc * 1024 + (hc + 1) * 128],
                                        rhs, start=(dc == 0), stop=(dc == 7))
                                i.then_inc(sem_pe, 1)
                            pe.n += 1
                            v_h = pe.n

                            def _f(e, hc=hc, hpi=hpi, spi=spi, v_h=v_h, cw=cw):
                                e.wait_ge(sem_pe, v_h)
                                e.activation(silu[spi][:, hc * 512:hc * 512 + cw],
                                             ha_ps[hpi][:, 0:cw], AF.Silu
                                             ).then_inc(sem_ac, 1)
                            actq.add(_f)
                            ac.n += 1
                            ha_free[hpi] = ac.n
                        v_silu = ac.n
                        phaseA_last[e_i] = pe.n

                        for t2 in range(cw // 128):
                            t = tb * 4 + t2
                            fpi = t2 % 2

                            @blk.tensor
                            def _(e, t2=t2, fpi=fpi, spi=spi, wpi=wpi, v_silu=v_silu):
                                e.wait_ge(sem_ac, v_silu)
                                if fbps_free[fpi]:
                                    e.wait_ge(sem_dv, fbps_free[fpi])
                                for hc in range(8):
                                    for j in range(2):
                                        i = e.matmul(
                                            fb_ps[fpi][:, j * 512:(j + 1) * 512],
                                            silu[spi][:, hc * 512 + t2 * 128:
                                                      hc * 512 + (t2 + 1) * 128],
                                            w2b[wpi][:, hc * 1024 + j * 512:
                                                     hc * 1024 + (j + 1) * 512],
                                            start=(hc == 0), stop=(hc == 7))
                                i.then_inc(sem_pe, 1)
                            pe.n += 1
                            v_fb = pe.n
                            if t2 == cw // 128 - 1:
                                silu_free[spi] = pe.n
                                if tb == ntbk - 1:
                                    w_free[wpi] = pe.n

                            if e_i == 0:
                                @blk.vector
                                def _(e, t=t, fpi=fpi, v_fb=v_fb):
                                    e.wait_ge(sem_pe, v_fb)
                                    e.tensor_tensor(xn[t][:, :], fb_ps[fpi][:, :],
                                                    xn[t][:, :], ALU.add
                                                    ).then_inc(sem_dv, 1)
                                dv.n += 1
                                fbps_free[fpi] = dv.n
                            else:
                                @blk.vector
                                def _(e, fpi=fpi, v_fb=v_fb, ew=eo_sb_free[fpi]):
                                    e.wait_ge(sem_pe, v_fb)
                                    if ew:
                                        e.wait_ge(sem_eo2[fpi], ew)
                                    e.tensor_copy(eo_sb[fpi][:, :], fb_ps[fpi][:, :]
                                                  ).then_inc(sem_dv, 1)
                                dv.n += 1
                                fbps_free[fpi] = dv.n
                                v_ev = dv.n
                                row0 = (e_i - 1) * CAPE + c0 + t2 * 128

                                @blk.gpsimd
                                def _(e, fpi=fpi, sub=sub, row0=row0, v_ev=v_ev):
                                    e.wait_ge(sem_dv, v_ev)
                                    if sub == 0:
                                        e.dma_start(out=eo_d[row0:row0 + 128, :],
                                                    in_=eo_sb[fpi][:, :]
                                                    ).then_inc(sem_eo2[fpi], 16)
                                    else:
                                        e.dma_start(out=eo_d[row0:row0 + 128, :],
                                                    in_=eo_sb[fpi][:, :],
                                                    accum_op=ALU.add
                                                    ).then_inc(sem_eo2[fpi], 16)
                                c_eo2[fpi].n += 16
                                eo_sb_free[fpi] = c_eo2[fpi].n

                    if e_i == 0 and sub == n_sub - 1 and n_exp > 1:
                        emit_xgT(1, phaseA_last[0])
                    elif e_i >= 1 and sub == 0 and e_i + 1 <= n_exp - 1:
                        emit_xgT(e_i + 1, phaseA_last[e_i - 1])

                v_ffn_dv = dv.n
                v_eo0 = c_eo2[0].n
                v_eo1 = c_eo2[1].n

                geo_free = [0, 0]
                for t in range(TT):
                    pi = t % 2

                    @blk.gpsimd
                    def _(e, t=t, pi=pi):
                        if t == 0:
                            e.wait_ge(sem_eo2[0], v_eo0)
                            e.wait_ge(sem_eo2[1], v_eo1)
                            e.wait_ge(sem_dv, v_ffn_dv)
                        if geo_free[pi]:
                            e.wait_ge(sem_dv, geo_free[pi])
                        e.indirect_dma_start(
                            out=geo[0][pi][:, :], out_offset=None,
                            in_=eo_d[:, :],
                            in_offset=bass.IndirectOffsetOnAxis(
                                ap=sidx1[t][:, :], axis=0)).then_inc(sem_geo[pi], 16)
                        e.indirect_dma_start(
                            out=geo[1][pi][:, :], out_offset=None,
                            in_=eo_d[:, :],
                            in_offset=bass.IndirectOffsetOnAxis(
                                ap=sidx2[t][:, :], axis=0)).then_inc(sem_geo[pi], 16)
                    c_geo[pi].n += 32
                    v_g = c_geo[pi].n

                    @blk.vector
                    def _(e, t=t, pi=pi, v_g=v_g, dv0=dv.n):
                        e.wait_ge(sem_geo[pi], v_g)
                        e.scalar_tensor_tensor(xn[t][:, :], geo[0][pi][:, :],
                                               gsl1[t][:, :], xn[t][:, :],
                                               ALU.mult, ALU.add).then_inc(sem_dv, 1)
                        e.wait_ge(sem_dv, dv0 + 1)
                        e.scalar_tensor_tensor(xn[t][:, :], geo[1][pi][:, :],
                                               gsl2[t][:, :], xn[t][:, :],
                                               ALU.mult, ALU.add).then_inc(sem_dv, 1)
                    dv.n += 2
                    geo_free[pi] = dv.n
                    v_fin = dv.n

                    @blk.sync
                    def _(e, t=t, v_fin=v_fin):
                        e.wait_ge(sem_dv, v_fin)
                        e.dma_start(out=out_h[s, t * 128:(t + 1) * 128, :],
                                    in_=xn[t][:, :]).then_inc(sem_st, 16)
                    st.n += 16

                actq.flush(blk.scalar)

        @blk.sync
        def _(e):
            e.wait_ge(sem_st, st.n)

    return nc



def _prep_weights(W_qkv, b_qkv, W_o, b_o, sw1, sw2, ew1, ew2, gw):
    w4 = np.asarray(W_qkv, np.float32).reshape(NH, DH, 3, D)
    wq = w4[:, :, 0, :].reshape(D, D)
    wk = w4[:, :, 1, :].reshape(D, D)
    wv = w4[:, :, 2, :].reshape(D, D)
    wqkvT = np.concatenate([wq.T, wk.T, wv.T], axis=1)
    wqkv_pre = np.empty((24, 128, 1024), np.float32)
    for fbk in range(24):
        sl = wqkvT[:, fbk * 128:(fbk + 1) * 128]
        wqkv_pre[fbk] = sl.reshape(8, 128, 128).transpose(1, 0, 2).reshape(128, 1024)

    b4 = np.asarray(b_qkv, np.float32).reshape(NH, DH, 3)
    b_r = np.concatenate([b4[:, :, 0].reshape(-1), b4[:, :, 1].reshape(-1),
                          b4[:, :, 2].reshape(-1)])
    bqkv_pre = np.ascontiguousarray(b_r.reshape(24, 128).T)

    woT = np.asarray(W_o, np.float32).T
    wo_pre = woT.reshape(8, 128, 1024).transpose(1, 0, 2).reshape(128, 8192)

    gwT = np.asarray(gw, np.float32).T
    gw_pre = np.ascontiguousarray(gwT.reshape(8, 128, 8).transpose(1, 0, 2)
                                  .reshape(128, 64))

    w1T = np.concatenate([np.asarray(sw1, np.float32).T[None],
                          np.asarray(ew1, np.float32).transpose(0, 2, 1)], axis=0)
    w2T = np.concatenate([np.asarray(sw2, np.float32).T[None],
                          np.asarray(ew2, np.float32).transpose(0, 2, 1)], axis=0)
    w1_pre = np.empty((NE, NSUB, 128, 8192), np.float32)
    w2_pre = np.empty((NE, NSUB, 128, 8192), np.float32)
    for e in range(NE):
        for sub in range(NSUB):
            sl = w1T[e][:, sub * 1024:(sub + 1) * 1024]
            w1_pre[e, sub] = sl.reshape(8, 128, 1024).transpose(1, 0, 2).reshape(128, 8192)
            sl2 = w2T[e][sub * 1024:(sub + 1) * 1024, :]
            w2_pre[e, sub] = sl2.reshape(8, 128, 1024).transpose(1, 0, 2).reshape(128, 8192)

    return {
        "wqkv": np.ascontiguousarray(wqkv_pre),
        "bqkv": bqkv_pre,
        "wo": np.ascontiguousarray(wo_pre),
        "bo": np.ascontiguousarray(np.asarray(b_o, np.float32).reshape(1, 1024)),
        "gw": gw_pre,
        "w1": np.ascontiguousarray(w1_pre),
        "w2": np.ascontiguousarray(w2_pre),
    }


def kernel(L0_attn_x, L0_attn_W_qkv, L0_attn_b_qkv, L0_attn_W_o, L0_attn_b_o,
           L0_ffn_shared_w1, L0_ffn_shared_w2, L0_ffn_experts_w1,
           L0_ffn_experts_w2, L0_ffn_gate_w):
    x = np.ascontiguousarray(np.asarray(L0_attn_x, dtype=np.float32))
    wmap = _prep_weights(L0_attn_W_qkv, L0_attn_b_qkv, L0_attn_W_o, L0_attn_b_o,
                         L0_ffn_shared_w1, L0_ffn_shared_w2, L0_ffn_experts_w1,
                         L0_ffn_experts_w2, L0_ffn_gate_w)
    nc = build_nc()
    in_maps = []
    for c in range(8):
        m = dict(wmap)
        m["x"] = np.ascontiguousarray(x[2 * c:2 * c + 2])
        in_maps.append(m)
    res = run_bass_kernel_spmd(nc, in_maps, core_ids=list(range(8)))
    out = np.concatenate([res.results[c]["out"] for c in range(8)], axis=0)
    return out.astype(np.float32)


# version 14
# speedup vs baseline: 1.0578x; 1.0578x over previous
import numpy as np
from contextlib import ExitStack

import concourse.bass as bass
import concourse.mybir as mybir
from concourse.bass_utils import run_bass_kernel_spmd

F32 = mybir.dt.float32
BF16 = mybir.dt.bfloat16
U32 = mybir.dt.uint32
AF = mybir.ActivationFunctionType
ALU = mybir.AluOpType
AX = mybir.AxisListType

D = 1024
S = 2048
NH = 16
DH = 64
NE = 9
NSUB = 4
TT = 16
CAPE = 1024
EPS1 = 1e-5
EPS2 = 1.1920929e-07
KB = 1024


class SC:
    def __init__(self, handle, step=1):
        self.h = handle
        self.n = 0
        self.step = step


class DeferredQ:
    def __init__(self):
        self.q = []

    def add(self, fn):
        self.q.append(fn)

    def flush(self, section_decorator):
        q = self.q
        self.q = []

        def _run(e):
            for fn in q:
                fn(e)
        section_decorator(_run)


def build_nc(n_seq=2, n_pairs=8, n_exp=NE, n_sub=NSUB):
    nc = bass.Bass("TRN2", target_bir_lowering=False)

    x_h = nc.dram_tensor("x", [n_seq, S, D], F32, kind="ExternalInput")
    wqkv_h = nc.dram_tensor("wqkv", [24, 128, 1024], F32, kind="ExternalInput")
    bqkv_h = nc.dram_tensor("bqkv", [128, 24], F32, kind="ExternalInput")
    wo_h = nc.dram_tensor("wo", [128, 8192], F32, kind="ExternalInput")
    bo_h = nc.dram_tensor("bo", [1, 1024], F32, kind="ExternalInput")
    gw_h = nc.dram_tensor("gw", [128, 64], F32, kind="ExternalInput")
    w1_h = nc.dram_tensor("w1", [NE, NSUB, 128, 8192], F32, kind="ExternalInput")
    w2_h = nc.dram_tensor("w2", [NE, NSUB, 128, 8192], F32, kind="ExternalInput")
    out_h = nc.dram_tensor("out", [n_seq, S, D], F32, kind="ExternalOutput")
    xg_d = nc.dram_tensor("xg_d", [8 * CAPE, 1024], BF16)
    eo_d = nc.dram_tensor("eo_d", [8 * CAPE, 1024], BF16)

    cur = [((nc.sbuf_base + 31) // 32) * 32]

    def place(name, shape, dt):
        t = nc.alloc_sbuf_tensor_at(name, shape, dt, offset=cur[0])
        lo, hi = t.manual_sbuf_range
        cur[0] = ((hi + 31) // 32) * 32
        return t

    def place_at(name, shape, dt, off):
        return nc.alloc_sbuf_tensor_at(name, shape, dt, offset=off)

    ident = place("ident", [128, 128], BF16)
    tri = place("tri", [128, 128], BF16)
    bqkv_t = place("bqkv_t", [128, 24], F32)
    bo_bc = place("bo_bc", [128, 1024], F32)
    bo_row = place("bo_row", [1, 1024], F32)
    gate_bf = place("gate_bf", [128, 64], BF16)
    ones_r = place("ones_r", [1, 128], F32)
    bnb = [place(f"bnb{i}", [128, 12], F32) for i in range(2)]
    bno = [place(f"bno{i}", [128, 2], F32) for i in range(2)]
    aab = [place(f"aab{i}", [128, 1], F32) for i in range(2)]
    bbb = [place(f"bbb{i}", [128, 1], F32) for i in range(2)]
    rm4 = [place(f"rm4{h}", [128, 4], F32) for h in range(2)]
    rs4 = [place(f"rs4{h}", [128, 4], F32) for h in range(2)]
    rmin = [place(f"rmin{h}", [128, 1], F32) for h in range(2)]
    rst = [place(f"rst{h}", [128, 1], F32) for h in range(2)]
    rin = [place(f"rin{h}", [128, 1], F32) for h in range(2)]
    vsc = [[place(f"vsc{h}_{i}", [128, 64], BF16) for i in range(2)] for h in range(2)]
    lgs = [place(f"lgs{i}", [128, 8], F32) for i in range(2)]
    glt = [place(f"glt{i}", [128, 8], F32) for i in range(2)]
    gt1 = [place(f"gt1{i}", [128, 8], F32) for i in range(2)]
    gm1 = [place(f"gm1{i}", [128, 1], F32) for i in range(2)]
    gm2 = [place(f"gm2{i}", [128, 1], F32) for i in range(2)]
    gsum = [place(f"gsum{i}", [128, 1], F32) for i in range(2)]
    gates = [place(f"gates{t}", [128, 8], F32) for t in range(TT)]
    eidx = place("eidx", [128, 8], F32)
    oh1 = [place(f"oh1_{i}", [128, 8], F32) for i in range(2)]
    oh2 = [place(f"oh2_{i}", [128, 8], F32) for i in range(2)]
    tf8 = [place(f"tf8_{i}", [128, 8], F32) for i in range(2)]
    e1f = [place(f"e1f{i}", [128, 1], F32) for i in range(2)]
    e2f = [place(f"e2f{i}", [128, 1], F32) for i in range(2)]
    s1f = [place(f"s1f{i}", [128, 1], F32) for i in range(2)]
    s2f = [place(f"s2f{i}", [128, 1], F32) for i in range(2)]
    slf = [place(f"slf{i}", [128, 1], F32) for i in range(2)]
    sidx1 = [place(f"sidx1_{t}", [128, 1], U32) for t in range(TT)]
    sidx2 = [place(f"sidx2_{t}", [128, 1], U32) for t in range(TT)]
    gsl1 = [place(f"gsl1_{t}", [128, 1], F32) for t in range(TT)]
    gsl2 = [place(f"gsl2_{t}", [128, 1], F32) for t in range(TT)]
    cumf = place("cumf", [8, 128], F32)
    cuml = place("cuml", [8, 1], F32)
    zero8 = place("zero8", [8, 128], F32)
    identf = place("identf", [128, 128], F32)
    cb_ln1 = place("cb_ln1", [128, 1], F32)
    cb_eps = place("cb_eps", [128, 1], F32)
    cb_att = place("cb_att", [128, 1], F32)

    NF0 = cur[0]
    nf = [place(f"nf{c}", [128, 2048], BF16) for c in range(8)]
    X0 = cur[0]
    qp = [place_at(f"qp{p}", [128, 2048], BF16, X0 + p * 4 * KB) for p in range(8)]
    vtm = [place_at(f"vtm{t}", [128, 1024], BF16, X0 + 32 * KB + t * 2 * KB)
           for t in range(TT)]
    xn = [place_at(f"xn{t}", [128, 1024], F32, X0 + t * 4 * KB) for t in range(TT)]
    Y0 = X0 + 64 * KB
    kp = [place_at(f"kp{p}", [128, 2048], BF16, Y0 + p * 4 * KB) for p in range(8)]
    zfm = [place_at(f"zfm{p}", [128, 2048], BF16, Y0 + p * 4 * KB) for p in range(8)]
    silu = [place_at(f"silu{i}", [128, 4096], BF16, Y0 + i * 8 * KB) for i in range(2)]
    Z0 = Y0 + 32 * KB
    wo_bf = place_at("wo_bf", [128, 8192], BF16, Z0)
    pt = [[place_at(f"pt{h}_{i}", [128, 2048], BF16,
                    Z0 + 16 * KB + (h * 2 + i) * 4 * KB) for i in range(2)]
          for h in range(2)]
    xs = [place_at(f"xs{i}", [128, 1024], F32, Z0 + 32 * KB + i * 4 * KB)
          for i in range(2)]
    ntb = [place_at(f"ntb{i}", [128, 1024], BF16, Z0 + 40 * KB + i * 2 * KB)
           for i in range(2)]
    wqb = [place_at(f"wqb{i}", [128, 1024], BF16, Z0 + 44 * KB + i * 2 * KB)
           for i in range(2)]
    vscr = place_at("vscr", [128, 2048], BF16, Z0 + 48 * KB)
    xr = [place_at(f"xr{i}", [128, 1024], F32, Z0 + 16 * KB + i * 4 * KB)
          for i in range(2)]
    xb = [place_at(f"xb{i}", [128, 1024], F32, Z0 + 24 * KB + i * 4 * KB)
          for i in range(2)]
    n2t = [place_at(f"n2t{i}", [128, 1024], BF16, Z0 + 52 * KB + i * 2 * KB)
           for i in range(4)]
    w1b = [place_at(f"w1b{i}", [128, 8192], BF16, Z0 + i * 16 * KB) for i in range(2)]
    w2b = [place_at(f"w2b{i}", [128, 8192], BF16, Z0 + 32 * KB + i * 16 * KB)
           for i in range(2)]
    xg_fm = [[place_at(f"xgfm{par}_{dc}", [128, 1024], BF16,
                       NF0 + par * 16 * KB + dc * 2 * KB) for dc in range(8)]
             for par in range(2)]
    geo = [[place_at(f"geo{g}_{i}", [128, 1024], BF16,
                     Y0 + 16 * KB + (g * 2 + i) * 2 * KB) for i in range(2)]
           for g in range(2)]
    eo_sb = [place_at(f"eosb{i}", [128, 1024], BF16, Y0 + 24 * KB + i * 2 * KB)
             for i in range(2)]
    assert Z0 + 64 * KB <= nc.sbuf_top, (Z0 + 64 * KB, nc.sbuf_top)

    with ExitStack() as top:
        sem_sc = top.enter_context(nc.semaphore("sc"))
        sem_gw = top.enter_context(nc.semaphore("gwl"))
        sem_x = [top.enter_context(nc.semaphore(f"x{i}")) for i in range(2)]
        sem_wq = [top.enter_context(nc.semaphore(f"wq{i}")) for i in range(2)]
        sem_wo = top.enter_context(nc.semaphore("wol"))
        sem_wf = [top.enter_context(nc.semaphore(f"wf{i}")) for i in range(2)]
        sem_st = top.enter_context(nc.semaphore("st"))
        sem_sct = [top.enter_context(nc.semaphore(f"sct{i}")) for i in range(4)]
        sem_eo2 = [top.enter_context(nc.semaphore(f"eo{i}")) for i in range(2)]
        sem_xgt = [top.enter_context(nc.semaphore(f"xgt{i}")) for i in range(2)]
        sem_geo = [top.enter_context(nc.semaphore(f"geo{i}")) for i in range(2)]
        sem_zi = top.enter_context(nc.semaphore("zi"))
        sem_pe = top.enter_context(nc.semaphore("pe"))
        sem_dv = top.enter_context(nc.semaphore("dv"))
        sem_ac = top.enter_context(nc.semaphore("ac"))
        sem_po = top.enter_context(nc.semaphore("po"))
        blk = top.enter_context(nc.Block())

        st = SC(sem_st, 16)
        c_sct = [SC(sem_sct[i], 16) for i in range(4)]
        c_eo2 = [SC(sem_eo2[0], 16), SC(sem_eo2[1], 16)]
        c_xgt = [SC(sem_xgt[0], 16), SC(sem_xgt[1], 16)]
        c_geo = [SC(sem_geo[0], 16), SC(sem_geo[1], 16)]
        c_zi = SC(sem_zi, 16)
        c_x = [SC(sem_x[0], 16), SC(sem_x[1], 16)]
        c_wq = [SC(sem_wq[0], 16), SC(sem_wq[1], 16)]
        c_wo = SC(sem_wo, 16)
        c_wf = [SC(sem_wf[0], 16), SC(sem_wf[1], 16)]
        pe, dv, ac, po = SC(sem_pe), SC(sem_dv), SC(sem_ac), SC(sem_po)

        @blk.sync
        def _(e):
            e.dma_start(out=bqkv_t[:, :], in_=bqkv_h[:, :]).then_inc(sem_sc, 16)
            e.dma_start(out=bo_row[:, :], in_=bo_h[:, :]).then_inc(sem_sc, 16)
        v_bias_ld = 32

        @blk.gpsimd
        def _(e):
            e.dma_start(out=gate_bf[:, :], in_=gw_h[:, :]).then_inc(sem_gw, 16)
            e.memset(ident[:, :], 0.0).then_inc(sem_po, 1)
            e.memset(tri[:, :], 0.0).then_inc(sem_po, 1)
            e.memset(ones_r[:, :], 1.0).then_inc(sem_po, 1)
            e.memset(cb_ln1[:, :], float(EPS2 * EPS1)).then_inc(sem_po, 1)
            e.memset(cb_eps[:, :], float(EPS1)).then_inc(sem_po, 1)
            e.memset(cb_att[:, :], -44.0).then_inc(sem_po, 1)
            e.memset(identf[:, :], 0.0).then_inc(sem_po, 1)
            e.memset(zero8[:, :], 0.0).then_inc(sem_po, 1)
            e.memset(silu[0][:, :], 0.0).then_inc(sem_po, 1)
            e.iota(eidx[:, :], pattern=[[1, 8]], base=0, channel_multiplier=0,
                   allow_small_or_imprecise_dtypes=True).then_inc(sem_po, 1)
            e.wait_ge(sem_po, 10)
            e.affine_select(out=identf[:, :], in_=identf[:, :],
                            compare_op=ALU.not_equal, fill=1.0, base=0,
                            pattern=[[-1, 128]], channel_multiplier=1
                            ).then_inc(sem_po, 1)
            e.wait_ge(sem_po, 11)
            e.affine_select(out=ident[:, :], in_=ident[:, :], compare_op=ALU.not_equal,
                            fill=1.0, base=0, pattern=[[-1, 128]],
                            channel_multiplier=1).then_inc(sem_po, 1)
            e.wait_ge(sem_po, 12)
            e.affine_select(out=tri[:, :], in_=tri[:, :], compare_op=ALU.is_gt,
                            fill=1.0, base=0, pattern=[[-1, 128]],
                            channel_multiplier=1).then_inc(sem_po, 1)
        po.n += 13
        v_gate_ld = 16
        v_consts = po.n

        with nc.psum_tensor("bc_ps", [128, 1024], F32) as bc_ps:
            @blk.tensor
            def _(e):
                e.wait_ge(sem_sc, v_bias_ld)
                e.wait_ge(sem_po, v_consts)
                e.matmul(bc_ps[:, 0:512], ones_r[:, :], bo_row[:, 0:512],
                         start=True, stop=True)
                e.matmul(bc_ps[:, 512:1024], ones_r[:, :], bo_row[:, 512:1024],
                         start=True, stop=True).then_inc(sem_pe, 1)
            pe.n += 1
            v_bc_mm = pe.n

            @blk.sync
            def _(e):
                e.wait_ge(sem_po, v_consts)
                nzc = 8 * CAPE // 512
                for c in range(nzc):
                    e.dma_start(out=xg_d[c * 512:(c + 1) * 512, :],
                                in_=silu[0][:, :]).then_inc(sem_zi, 16)
                for c in range(nzc):
                    e.dma_start(out=eo_d[c * 512:(c + 1) * 512, :],
                                in_=silu[0][:, :]).then_inc(sem_zi, 16)
            c_zi.n += 16 * 2 * (8 * CAPE // 512)
            v_zi = c_zi.n

            @blk.vector
            def _(e):
                e.wait_ge(sem_pe, v_bc_mm)
                e.tensor_copy(bo_bc[:, :], bc_ps[:, :]).then_inc(sem_dv, 1)
            dv.n += 1
        v_bc_done = dv.n

        qfbs = list(range(0, n_pairs)) + list(range(8, 8 + n_pairs)) \
            + list(range(16, 16 + n_pairs))

        actq = DeferredQ()
        for s in range(n_seq):
            if s > 0:
                prev = dict(pe=pe.n, dv=dv.n, ac=ac.n, po=po.n, st=st.n)

                def qwaits(e):
                    e.wait_ge(sem_pe, prev["pe"])
                    e.wait_ge(sem_dv, prev["dv"])
                    e.wait_ge(sem_ac, prev["ac"])
                    e.wait_ge(sem_po, prev["po"])
                    e.wait_ge(sem_st, prev["st"])

                @blk.sync
                def _(e):
                    qwaits(e)

                @blk.gpsimd
                def _(e):
                    qwaits(e)

                @blk.tensor
                def _(e):
                    qwaits(e)

                @blk.vector
                def _(e):
                    qwaits(e)

                @blk.scalar
                def _(e):
                    qwaits(e)

            with ExitStack() as p1ps:
                tp_ps = [p1ps.enter_context(
                    nc.psum_tensor(f"tp{i}_{s}", [128, 128], BF16)) for i in range(2)]
                qkv_ps = [p1ps.enter_context(
                    nc.psum_tensor(f"qk{i}_{s}", [128, 512], F32)) for i in range(4)]

                xs_free = [0, 0]
                ntb_free = [0, 0]
                tp_free = [0, 0]
                nf_ready = [0] * TT

                for t in range(TT):
                    pi = t % 2

                    @blk.sync
                    def _(e, t=t, pi=pi):
                        if xs_free[pi]:
                            e.wait_ge(sem_dv, xs_free[pi])
                        e.dma_start(out=xs[pi][:, :],
                                    in_=x_h[s, t * 128:(t + 1) * 128, :]
                                    ).then_inc(sem_x[pi], 16)
                    c_x[pi].n += 16
                    v_x = c_x[pi].n

                    @blk.vector
                    def _(e, pi=pi, v_x=v_x):
                        e.wait_ge(sem_x[pi], v_x)
                        e.bn_stats(bnb[pi][:, 0:6], xs[pi][:, 0:512])
                        e.bn_stats(bnb[pi][:, 6:12], xs[pi][:, 512:1024]
                                   ).then_inc(sem_dv, 1)
                        e.wait_ge(sem_dv, dv.n + 1)
                        e.bn_aggr(bno[pi][:, :], bnb[pi][:, :]).then_inc(sem_dv, 1)
                    dv.n += 2
                    v_aggr = dv.n

                    def _f(e, t=t, pi=pi, v_aggr=v_aggr, s=s, ac0=ac.n):
                        if t == 0 and s == 0:
                            e.wait_ge(sem_po, v_consts)
                        e.wait_ge(sem_dv, v_aggr)
                        e.activation(aab[pi][:, :], bno[pi][:, 1:2], AF.Ln,
                                     scale=float(1.0 + EPS2),
                                     bias=cb_ln1[:, :]).then_inc(sem_ac, 1)
                        e.wait_ge(sem_ac, ac0 + 1)
                        e.activation(aab[pi][:, :], aab[pi][:, :], AF.Exp,
                                     scale=-0.5).then_inc(sem_ac, 1)
                    actq.add(_f)
                    ac.n += 2
                    v_scale = ac.n

                    @blk.vector
                    def _(e, pi=pi, v_scale=v_scale):
                        e.wait_ge(sem_ac, v_scale)
                        e.tensor_scalar(bbb[pi][:, :], bno[pi][:, 0:1], aab[pi][:, :],
                                        -1.0, ALU.mult, ALU.mult).then_inc(sem_dv, 1)
                        e.wait_ge(sem_dv, dv.n + 1)
                        if ntb_free[pi]:
                            e.wait_ge(sem_pe, ntb_free[pi])
                        e.tensor_scalar(ntb[pi][:, :], xs[pi][:, :], aab[pi][:, :],
                                        bbb[pi][:, :], ALU.mult, ALU.add
                                        ).then_inc(sem_dv, 1)
                    dv.n += 2
                    v_norm = dv.n
                    xs_free[pi] = dv.n

                    base_ac = ac.n

                    @blk.tensor
                    def _(e, t=t, pi=pi, v_norm=v_norm, base_ac=base_ac):
                        e.wait_ge(sem_dv, v_norm)
                        if t == 0 and s == 0:
                            e.wait_ge(sem_po, v_consts)
                        if s > 0 and t == 0:
                            pass
                        for c in range(8):
                            w = tp_free[c % 2] if c < 2 else base_ac + c - 1
                            if w:
                                e.wait_ge(sem_ac, w)
                            e.transpose(tp_ps[c % 2][:, :],
                                        ntb[pi][:, c * 128:(c + 1) * 128],
                                        ident[:, :]).then_inc(sem_pe, 1)
                    base_pe = pe.n
                    pe.n += 8
                    ntb_free[pi] = pe.n

                    def _f(e, t=t, base_pe=base_pe):
                        for c in range(8):
                            e.wait_ge(sem_pe, base_pe + c + 1)
                            e.copy(nf[c][:, t * 128:(t + 1) * 128],
                                   tp_ps[c % 2][:, :]).then_inc(sem_ac, 1)
                    actq.add(_f)
                    ac.n += 8
                    tp_free[0] = ac.n - 1
                    tp_free[1] = ac.n
                    nf_ready[t] = ac.n

                v_nf_all = ac.n

                wqb_free = [0, 0]
                qps_free = [0, 0, 0, 0]
                vscr_free = 0

                for k, fb in enumerate(qfbs):
                    pi = k % 2

                    @blk.gpsimd
                    def _(e, fb=fb, pi=pi):
                        if wqb_free[pi]:
                            e.wait_ge(sem_pe, wqb_free[pi])
                        e.dma_start(out=wqb[pi][:, :], in_=wqkv_h[fb, :, :]
                                    ).then_inc(sem_wq[pi], 16)
                    c_wq[pi].n += 16
                    v_wq = c_wq[pi].n

                    is_v = fb >= 16
                    if fb < 8:
                        dst = qp[fb]
                    elif fb < 16:
                        dst = kp[fb - 8]
                    else:
                        dst = vscr

                    for tc in range(4):
                        pj = (k * 4 + tc) % 4

                        @blk.tensor
                        def _(e, k=k, fb=fb, pi=pi, tc=tc, pj=pj, v_wq=v_wq):
                            e.wait_ge(sem_wq[pi], v_wq)
                            if k == 0:
                                e.wait_ge(sem_ac, nf_ready[4 * tc + 3])
                            if qps_free[pj]:
                                e.wait_ge(sem_ac, qps_free[pj])
                            for dc in range(8):
                                i = e.matmul(qkv_ps[pj][:, :],
                                             wqb[pi][:, dc * 128:(dc + 1) * 128],
                                             nf[dc][:, tc * 512:(tc + 1) * 512],
                                             start=(dc == 0), stop=(dc == 7))
                            i.then_inc(sem_pe, 1)
                        pe.n += 1
                        v_mm = pe.n
                        if tc == 3:
                            wqb_free[pi] = v_mm

                        def _f(e, fb=fb, tc=tc, pj=pj, v_mm=v_mm, dst=dst,
                               is_v=is_v, vf=vscr_free, s=s):
                            e.wait_ge(sem_pe, v_mm)
                            if tc == 0:
                                e.wait_ge(sem_sc, v_bias_ld)
                                if s == 0 and fb == 8:
                                    e.wait_ge(sem_zi, v_zi)
                                if is_v and vf:
                                    e.wait_ge(sem_pe, vf)
                            e.activation(dst[:, tc * 512:(tc + 1) * 512],
                                         qkv_ps[pj][:, :], AF.Identity,
                                         bias=bqkv_t[:, fb:fb + 1]).then_inc(sem_ac, 1)
                        actq.add(_f)
                        ac.n += 1
                        qps_free[pj] = ac.n

                    if is_v:
                        v_evac = ac.n
                        col = (fb - 16) * 128
                        base_ac2 = ac.n

                        @blk.tensor
                        def _(e, v_evac=v_evac, base_ac2=base_ac2):
                            e.wait_ge(sem_ac, v_evac)
                            for t in range(TT):
                                w = tp_free[t % 2] if t < 2 else base_ac2 + t - 1
                                if w:
                                    e.wait_ge(sem_ac, w)
                                e.transpose(tp_ps[t % 2][:, :],
                                            vscr[:, t * 128:(t + 1) * 128],
                                            ident[:, :]).then_inc(sem_pe, 1)
                        base_pe2 = pe.n
                        pe.n += TT
                        vscr_free = pe.n

                        def _f(e, col=col, base_pe2=base_pe2):
                            for t in range(TT):
                                e.wait_ge(sem_pe, base_pe2 + t + 1)
                                e.copy(vtm[t][:, col:col + 128], tp_ps[t % 2][:, :]
                                       ).then_inc(sem_ac, 1)
                        actq.add(_f)
                        ac.n += TT
                        tp_free[0] = ac.n - 1
                        tp_free[1] = ac.n

                v_qkv_ac = ac.n

                @blk.gpsimd
                def _(e):
                    if wqb_free[0]:
                        e.wait_ge(sem_pe, wqb_free[0])
                    e.dma_start(out=wo_bf[:, :], in_=wo_h[:, :]).then_inc(sem_wo, 16)
                c_wo.n += 16
                v_wo = c_wo.n
                actq.flush(blk.scalar)

            with ExitStack() as p2ps:
                s_ps = [p2ps.enter_context(
                    nc.psum_tensor(f"sp{i}_{s}", [128, 512], F32)) for i in range(4)]
                z_ps = p2ps.enter_context(nc.psum_tensor(f"zp_{s}", [128, 2048], F32))

                pt_free = [[0, 0], [0, 0]]
                rs4_free = [0, 0]
                sps_exp = [0, 0, 0, 0]
                vsc_free = [[0, 0], [0, 0]]
                zwin_stop = [0] * 4
                zev_done = [0] * 4
                scnt = [0]

                def emit_S(p, kt):
                    k0 = kt * 128
                    w0 = k0 // 512
                    qa = w0 * 512
                    nch = 4 - w0
                    res = {}
                    ac_base = ac.n
                    v_p2 = {}
                    tiles = {}

                    @blk.tensor
                    def _(e, p=p, kt=kt, k0=k0, qa=qa, nch=nch, ac_base=ac_base):
                        local_exp = {}
                        if p == 0 and kt == 0:
                            e.wait_ge(sem_ac, v_qkv_ac)
                        for c in range(nch):
                            for h in range(2):
                                rows = slice(h * 64, h * 64 + 64)
                                j = scnt[0] % 4
                                scnt[0] += 1
                                tiles[(h, c)] = j
                                w = local_exp.get(j, sps_exp[j])
                                if w:
                                    e.wait_ge(sem_ac, w)
                                local_exp[j] = ac_base + 2 * c + h + 1
                                qc0 = qa + c * 512
                                e.matmul(s_ps[j][:, :],
                                         kp[p][rows, k0:k0 + 128],
                                         qp[p][rows, qc0:qc0 + 512],
                                         start=True, stop=True).then_inc(sem_pe, 1)
                                pe.n += 1
                                v_p2[(h, c)] = pe.n

                    ppi = kt % 2
                    ptw = {h: pt_free[h][ppi] for h in range(2)}
                    rs4w = {h: rs4_free[h] for h in range(2)}

                    def _f(e, kt=kt, k0=k0, qa=qa, nch=nch, ppi=ppi,
                           v_p2=dict(v_p2), tiles=dict(tiles), ptw=ptw, rs4w=rs4w):
                        for c in range(nch):
                            for h in range(2):
                                e.wait_ge(sem_pe, v_p2[(h, c)])
                                if c == 0:
                                    if ptw[h]:
                                        e.wait_ge(sem_pe, ptw[h])
                                    if rs4w[h]:
                                        e.wait_ge(sem_dv, rs4w[h])
                                    off = k0 - qa
                                    e.activation(pt[h][ppi][:, k0:qa + 512],
                                                 s_ps[tiles[(h, c)]][:, off:512],
                                                 AF.Exp,
                                                 bias=cb_att[:, :]).then_inc(sem_ac, 1)
                                else:
                                    e.activation(
                                        pt[h][ppi][:, qa + c * 512:qa + (c + 1) * 512],
                                        s_ps[tiles[(h, c)]][:, :], AF.Exp,
                                        bias=cb_att[:, :],
                                        accum_out=rs4[h][:, c:c + 1]).then_inc(sem_ac, 1)
                    actq.add(_f)
                    for (h, c), j in tiles.items():
                        sps_exp[j] = ac_base + 2 * c + h + 1
                    ac.n += 2 * nch

                    for h in range(2):
                        v_exp = ac_base + 2 * (nch - 1) + h + 1
                        v_exp0 = ac_base + h + 1

                        v_zero = po.n
                        if qa < k0:
                            @blk.gpsimd
                            def _(e, h=h, qa=qa, k0=k0, ppi=ppi, v_exp0=v_exp0):
                                e.wait_ge(sem_ac, v_exp0)
                                e.memset(pt[h][ppi][:, qa:k0], 0.0).then_inc(sem_po, 1)
                            po.n += 1
                            v_zero = po.n

                        @blk.vector
                        def _(e, h=h, p=p, kt=kt, k0=k0, qa=qa, nch=nch, ppi=ppi,
                              v_exp=v_exp, v_exp0=v_exp0, v_zero=v_zero, dv0=dv.n):
                            e.wait_ge(sem_ac, v_exp0)
                            e.tensor_tensor(pt[h][ppi][:, k0:k0 + 128],
                                            pt[h][ppi][:, k0:k0 + 128], tri[:, :],
                                            ALU.mult).then_inc(sem_dv, 1)
                            e.wait_ge(sem_dv, dv0 + 1)
                            if qa < k0:
                                e.wait_ge(sem_po, v_zero)
                            e.tensor_reduce(rs4[h][:, 0:1],
                                            pt[h][ppi][:, qa:qa + 512], AX.X,
                                            ALU.add).then_inc(sem_dv, 1)
                            e.wait_ge(sem_ac, v_exp)
                            e.wait_ge(sem_dv, dv0 + 2)
                            e.tensor_reduce(rst[h][:, :], rs4[h][:, 0:nch], AX.X,
                                            ALU.add).then_inc(sem_dv, 1)
                            e.wait_ge(sem_dv, dv0 + 3)
                            e.reciprocal(rin[h][:, :], rst[h][:, :]
                                         ).then_inc(sem_dv, 1)
                            e.wait_ge(sem_dv, dv0 + 4)
                            if vsc_free[h][kt % 2]:
                                e.wait_ge(sem_pe, vsc_free[h][kt % 2])
                            e.tensor_scalar(
                                vsc[h][kt % 2][:, :],
                                vtm[kt][:, (2 * p + h) * 64:(2 * p + h) * 64 + 64],
                                rin[h][:, :], None, ALU.mult).then_inc(sem_dv, 1)
                        dv.n += 5
                        rs4_free[h] = dv.n - 2
                        res[h] = (dv.n, v_zero, ppi)
                    return res

                def emit_Z(p, kt, res):
                    k0 = kt * 128
                    w0 = k0 // 512

                    @blk.tensor
                    def _(e, p=p, kt=kt, w0=w0, res=dict(res)):
                        for h in range(2):
                            v_vsc, v_zero, ppi = res[h]
                            e.wait_ge(sem_dv, v_vsc)
                            if v_zero:
                                e.wait_ge(sem_po, v_zero)
                        for w in range(w0, 4):
                            for h in range(2):
                                v_vsc, v_zero, ppi = res[h]
                                if kt == 0 and p > 0 and zev_done[w]:
                                    e.wait_ge(sem_dv, zev_done[w])
                                orow = slice(h * 64, h * 64 + 64)
                                e.matmul(z_ps[orow, w * 512:(w + 1) * 512],
                                         vsc[h][kt % 2][:, :],
                                         pt[h][ppi][:, w * 512:(w + 1) * 512],
                                         start=(kt == 0), stop=(kt == 4 * w + 3),
                                         tile_position=(0, h * 64),
                                         skip_group_check=True).then_inc(sem_pe, 1)
                                pe.n += 1
                                if kt == 4 * w + 3 and h == 1:
                                    zwin_stop[w] = pe.n
                    for h in range(2):
                        _, _, ppi = res[h]
                        vsc_free[h][kt % 2] = pe.n
                        pt_free[h][ppi] = pe.n

                for p in range(n_pairs):
                    prev = None
                    for kt in range(TT):
                        res = emit_S(p, kt)
                        if prev is not None:
                            emit_Z(p, kt - 1, prev)
                        prev = res
                    emit_Z(p, TT - 1, prev)

                    for w in range(4):
                        @blk.vector
                        def _(e, p=p, w=w):
                            e.wait_ge(sem_pe, zwin_stop[w])
                            e.tensor_copy(zfm[p][:, w * 512:(w + 1) * 512],
                                          z_ps[:, w * 512:(w + 1) * 512]
                                          ).then_inc(sem_dv, 1)
                        dv.n += 1
                        zev_done[w] = dv.n

                v_attn_dv = dv.n
                v_attn_pe = pe.n
                actq.flush(blk.scalar)

            with ExitStack() as p3ps:
                op_ps = [p3ps.enter_context(
                    nc.psum_tensor(f"op{i}_{s}", [128, 1024], F32)) for i in range(2)]
                tp2_ps = [p3ps.enter_context(
                    nc.psum_tensor(f"t2{i}_{s}", [128, 128], BF16)) for i in range(2)]
                lg_ps = p3ps.enter_context(
                    nc.psum_tensor(f"lg_{s}", [128, 8], F32))
                tpm_ps = p3ps.enter_context(
                    nc.psum_tensor(f"tpm_{s}", [128, 128], F32))
                ctm_ps = tpm_ps

                op_free = [0, 0]
                xr_free = [0, 0]
                xb_free = [0, 0]
                tp2_free = [0, 0]
                n2t_free = [0, 0, 0, 0]
                n2t_sct = [0, 0, 0, 0]
                lg_free = [0]
                tpm_free = [0]
                ctm_free = [0]
                gstate = {}

                def emit_gating_tail(t):
                    pi = gstate[t]["pi"]
                    v_gm = gstate[t]["v_gm"]

                    def _f(e, pi=pi, v_gm=v_gm):
                        e.wait_ge(sem_dv, v_gm)
                        e.activation(glt[pi][:, :], lgs[pi][:, :], AF.Exp,
                                     bias=gm1[pi][:, :]).then_inc(sem_ac, 1)
                    actq.add(_f)
                    ac.n += 1
                    v_ge = ac.n

                    @blk.tensor
                    def _(e, pi=pi, v_gm=v_gm, tw=tpm_free[0]):
                        e.wait_ge(sem_dv, v_gm)
                        if tw:
                            e.wait_ge(sem_dv, tw)
                        e.transpose(tpm_ps[0:8, 0:128], gt1[pi][:, :], identf[:, :]
                                    ).then_inc(sem_pe, 1)
                    pe.n += 1
                    v_mt = pe.n

                    @blk.vector
                    def _(e, t=t, pi=pi, v_ge=v_ge, v_mt=v_mt, dv0=dv.n):
                        e.wait_ge(sem_ac, v_ge)
                        e.tensor_tensor(glt[pi][:, :], glt[pi][:, :], gt1[pi][:, :],
                                        ALU.mult).then_inc(sem_dv, 1)
                        e.wait_ge(sem_dv, dv0 + 1)
                        e.tensor_reduce(gsum[pi][:, :], glt[pi][:, :], AX.X, ALU.add
                                        ).then_inc(sem_dv, 1)
                        e.wait_ge(sem_dv, dv0 + 2)
                        e.reciprocal(gm1[pi][:, :], gsum[pi][:, :]).then_inc(sem_dv, 1)
                        e.wait_ge(sem_dv, dv0 + 3)
                        e.tensor_scalar(gates[t][:, :], glt[pi][:, :], gm1[pi][:, :],
                                        None, ALU.mult).then_inc(sem_dv, 1)
                        e.wait_ge(sem_dv, dv0 + 4)
                        e.tensor_tensor(oh2[pi][:, :], gt1[pi][:, :], oh1[pi][:, :],
                                        ALU.subtract).then_inc(sem_dv, 1)
                        e.wait_ge(sem_dv, dv0 + 5)
                        e.scalar_tensor_tensor(tf8[pi][:, :], eidx[:, :], 1.0,
                                               oh1[pi][:, :], ALU.mult, ALU.mult,
                                               accum_out=e1f[pi][:, :]).then_inc(sem_dv, 1)
                        e.wait_ge(sem_dv, dv0 + 6)
                        e.scalar_tensor_tensor(tf8[pi][:, :], eidx[:, :], 1.0,
                                               oh2[pi][:, :], ALU.mult, ALU.mult,
                                               accum_out=e2f[pi][:, :]).then_inc(sem_dv, 1)
                        e.wait_ge(sem_dv, dv0 + 7)
                        e.scalar_tensor_tensor(tf8[pi][:, :], gates[t][:, :], 1.0,
                                               oh1[pi][:, :], ALU.mult, ALU.mult,
                                               accum_out=gsl1[t][:, :]).then_inc(sem_dv, 1)
                        e.wait_ge(sem_dv, dv0 + 8)
                        e.scalar_tensor_tensor(tf8[pi][:, :], gates[t][:, :], 1.0,
                                               oh2[pi][:, :], ALU.mult, ALU.mult,
                                               accum_out=gsl2[t][:, :]).then_inc(sem_dv, 1)
                        e.wait_ge(sem_pe, v_mt)
                        e.tensor_tensor_scan(cumf[:, :], tpm_ps[0:8, 0:128],
                                             zero8[:, :],
                                             (0.0 if t == 0 else cuml[:, :]),
                                             ALU.add, ALU.add).then_inc(sem_dv, 1)
                        e.wait_ge(sem_dv, dv0 + 10)
                        e.tensor_copy(cuml[:, :], cumf[:, 127:128]).then_inc(sem_dv, 1)
                    dv.n += 11
                    tpm_free[0] = dv.n
                    v_scan = dv.n

                    @blk.tensor
                    def _(e, v_scan=v_scan, cw=ctm_free[0]):
                        e.wait_ge(sem_dv, v_scan)
                        if cw:
                            e.wait_ge(sem_dv, cw)
                        e.transpose(ctm_ps[0:128, 0:8], cumf[:, :], identf[0:8, 0:8]
                                    ).then_inc(sem_pe, 1)
                    pe.n += 1
                    v_ct = pe.n

                    @blk.vector
                    def _(e, t=t, pi=pi, v_ct=v_ct, dv0=dv.n):
                        e.wait_ge(sem_pe, v_ct)
                        e.scalar_tensor_tensor(tf8[pi][:, :], ctm_ps[0:128, 0:8], 1.0,
                                               oh1[pi][:, :], ALU.mult, ALU.mult,
                                               accum_out=s1f[pi][:, :]).then_inc(sem_dv, 1)
                        e.wait_ge(sem_dv, dv0 + 1)
                        e.scalar_tensor_tensor(tf8[pi][:, :], ctm_ps[0:128, 0:8], 1.0,
                                               oh2[pi][:, :], ALU.mult, ALU.mult,
                                               accum_out=s2f[pi][:, :]).then_inc(sem_dv, 1)
                        e.wait_ge(sem_dv, dv0 + 2)
                        e.tensor_scalar(slf[pi][:, :], s1f[pi][:, :], -1.0,
                                        float(CAPE - 1), ALU.add, ALU.min
                                        ).then_inc(sem_dv, 1)
                        e.wait_ge(sem_dv, dv0 + 3)
                        e.scalar_tensor_tensor(s1f[pi][:, :], e1f[pi][:, :],
                                               float(CAPE), slf[pi][:, :],
                                               ALU.mult, ALU.add).then_inc(sem_dv, 1)
                        e.wait_ge(sem_dv, dv0 + 4)
                        e.tensor_copy(sidx1[t][:, :], s1f[pi][:, :]).then_inc(sem_dv, 1)
                        e.tensor_scalar(slf[pi][:, :], s2f[pi][:, :], -1.0,
                                        float(CAPE - 1), ALU.add, ALU.min
                                        ).then_inc(sem_dv, 1)
                        e.wait_ge(sem_dv, dv0 + 6)
                        e.scalar_tensor_tensor(s2f[pi][:, :], e2f[pi][:, :],
                                               float(CAPE), slf[pi][:, :],
                                               ALU.mult, ALU.add).then_inc(sem_dv, 1)
                        e.wait_ge(sem_dv, dv0 + 7)
                        e.tensor_copy(sidx2[t][:, :], s2f[pi][:, :]).then_inc(sem_dv, 1)
                    dv.n += 8
                    ctm_free[0] = dv.n
                    tpm_free[0] = dv.n
                    v_sidx = dv.n

                    @blk.gpsimd
                    def _(e, t=t, pi=pi, v_sidx=v_sidx):
                        e.wait_ge(sem_dv, v_sidx)
                        e.indirect_dma_start(
                            out=xg_d[:, :],
                            out_offset=bass.IndirectOffsetOnAxis(
                                ap=sidx1[t][:, :], axis=0),
                            in_=n2t[t % 4][:, :], in_offset=None
                            ).then_inc(sem_sct[t % 4], 16)
                        e.indirect_dma_start(
                            out=xg_d[:, :],
                            out_offset=bass.IndirectOffsetOnAxis(
                                ap=sidx2[t][:, :], axis=0),
                            in_=n2t[t % 4][:, :], in_offset=None
                            ).then_inc(sem_sct[t % 4], 16)
                    c_sct[t % 4].n += 32
                    n2t_sct[t % 4] = c_sct[t % 4].n

                for t in range(TT):
                    pi = t % 2

                    @blk.sync
                    def _(e, t=t, pi=pi):
                        if t < 2:
                            e.wait_ge(sem_pe, v_attn_pe)
                        if xr_free[pi]:
                            e.wait_ge(sem_po, xr_free[pi])
                        e.dma_start(out=xr[pi][:, :],
                                    in_=x_h[s, t * 128:(t + 1) * 128, :]
                                    ).then_inc(sem_x[pi], 16)
                    c_x[pi].n += 16
                    v_xr = c_x[pi].n

                    @blk.gpsimd
                    def _(e, pi=pi, v_xr=v_xr):
                        e.wait_ge(sem_x[pi], v_xr)
                        if s == 0 and pi == 0:
                            e.wait_ge(sem_dv, v_bc_done)
                        if xb_free[pi]:
                            e.wait_ge(sem_dv, xb_free[pi])
                        e.tensor_tensor(xb[pi][:, :], xr[pi][:, :], bo_bc[:, :],
                                        ALU.add).then_inc(sem_po, 1)
                    po.n += 1
                    v_xb = po.n
                    xr_free[pi] = po.n

                    @blk.tensor
                    def _(e, t=t, pi=pi):
                        if t == 0:
                            e.wait_ge(sem_dv, v_attn_dv)
                            e.wait_ge(sem_wo, v_wo)
                        if op_free[pi]:
                            e.wait_ge(sem_dv, op_free[pi])
                        for dc in range(n_pairs):
                            for j in range(2):
                                i = e.matmul(op_ps[pi][:, j * 512:(j + 1) * 512],
                                             zfm[dc][:, t * 128:(t + 1) * 128],
                                             wo_bf[:, dc * 1024 + j * 512:
                                                   dc * 1024 + (j + 1) * 512],
                                             start=(dc == 0), stop=(dc == n_pairs - 1))
                        i.then_inc(sem_pe, 1)
                    pe.n += 1
                    v_op = pe.n

                    @blk.vector
                    def _(e, t=t, pi=pi, v_op=v_op, v_xb=v_xb):
                        e.wait_ge(sem_pe, v_op)
                        e.wait_ge(sem_po, v_xb)
                        e.tensor_tensor(xn[t][:, :], op_ps[pi][:, :], xb[pi][:, :],
                                        ALU.add).then_inc(sem_dv, 1)
                        e.wait_ge(sem_dv, dv.n + 1)
                        e.bn_stats(bnb[pi][:, 0:6], xn[t][:, 0:512])
                        e.bn_stats(bnb[pi][:, 6:12], xn[t][:, 512:1024]
                                   ).then_inc(sem_dv, 1)
                        e.wait_ge(sem_dv, dv.n + 2)
                        e.bn_aggr(bno[pi][:, :], bnb[pi][:, :]).then_inc(sem_dv, 1)
                    dv.n += 3
                    v_ag2 = dv.n
                    op_free[pi] = dv.n
                    xb_free[pi] = dv.n

                    def _f(e, pi=pi, v_ag2=v_ag2, ac0=ac.n):
                        e.wait_ge(sem_dv, v_ag2)
                        e.activation(aab[pi][:, :], bno[pi][:, 1:2], AF.Ln,
                                     bias=cb_eps[:, :]).then_inc(sem_ac, 1)
                        e.wait_ge(sem_ac, ac0 + 1)
                        e.activation(aab[pi][:, :], aab[pi][:, :], AF.Exp,
                                     scale=-0.5).then_inc(sem_ac, 1)
                    actq.add(_f)
                    ac.n += 2
                    v_sc2 = ac.n

                    @blk.vector
                    def _(e, t=t, pi=pi, v_sc2=v_sc2):
                        e.wait_ge(sem_ac, v_sc2)
                        e.tensor_scalar(bbb[pi][:, :], bno[pi][:, 0:1], aab[pi][:, :],
                                        -1.0, ALU.mult, ALU.mult).then_inc(sem_dv, 1)
                        e.wait_ge(sem_dv, dv.n + 1)
                        if n2t_free[t % 4]:
                            e.wait_ge(sem_pe, n2t_free[t % 4])
                        if n2t_sct[t % 4]:
                            e.wait_ge(sem_sct[t % 4], n2t_sct[t % 4])
                        e.tensor_scalar(n2t[t % 4][:, :], xn[t][:, :], aab[pi][:, :],
                                        bbb[pi][:, :], ALU.mult, ALU.add
                                        ).then_inc(sem_dv, 1)
                    dv.n += 2
                    v_n2 = dv.n

                    base_ac4 = ac.n

                    @blk.tensor
                    def _(e, pi=pi, v_n2=v_n2, base_ac4=base_ac4):
                        e.wait_ge(sem_dv, v_n2)
                        for c in range(8):
                            w = tp2_free[c % 2] if c < 2 else base_ac4 + c - 1
                            if w:
                                e.wait_ge(sem_ac, w)
                            e.transpose(tp2_ps[c % 2][:, :],
                                        n2t[t % 4][:, c * 128:(c + 1) * 128],
                                        ident[:, :]).then_inc(sem_pe, 1)
                    base_pe4 = pe.n
                    pe.n += 8
                    n2t_free[t % 4] = pe.n

                    def _f(e, t=t, base_pe4=base_pe4):
                        for c in range(8):
                            e.wait_ge(sem_pe, base_pe4 + c + 1)
                            e.copy(nf[c][:, t * 128:(t + 1) * 128],
                                   tp2_ps[c % 2][:, :]).then_inc(sem_ac, 1)
                    actq.add(_f)
                    ac.n += 8
                    tp2_free[0] = ac.n - 1
                    tp2_free[1] = ac.n
                    v_n2fm = ac.n

                    @blk.tensor
                    def _(e, t=t, pi=pi, v_n2fm=v_n2fm):
                        e.wait_ge(sem_ac, v_n2fm)
                        if t == 0 and s == 0:
                            e.wait_ge(sem_gw, v_gate_ld)
                        if lg_free[0]:
                            e.wait_ge(sem_dv, lg_free[0])
                        for dc in range(8):
                            i = e.matmul(lg_ps[:, :],
                                         nf[dc][:, t * 128:(t + 1) * 128],
                                         gate_bf[:, dc * 8:(dc + 1) * 8],
                                         start=(dc == 0), stop=(dc == 7))
                        i.then_inc(sem_pe, 1)
                    pe.n += 1
                    v_lg = pe.n

                    @blk.vector
                    def _(e, t=t, pi=pi, v_lg=v_lg, dv0=dv.n):
                        e.wait_ge(sem_pe, v_lg)
                        e.tensor_copy(lgs[pi][:, :], lg_ps[:, :]).then_inc(sem_dv, 1)
                        e.wait_ge(sem_dv, dv0 + 1)
                        e.tensor_reduce(gm1[pi][:, :], lgs[pi][:, :], AX.X, ALU.max,
                                        negate=True).then_inc(sem_dv, 1)
                        e.wait_ge(sem_dv, dv0 + 2)
                        e.tensor_scalar(oh1[pi][:, :], lgs[pi][:, :], gm1[pi][:, :],
                                        0.0, ALU.add, ALU.is_ge).then_inc(sem_dv, 1)
                        e.tensor_scalar(glt[pi][:, :], lgs[pi][:, :], gm1[pi][:, :],
                                        0.0, ALU.add, ALU.is_lt).then_inc(sem_dv, 1)
                        e.wait_ge(sem_dv, dv0 + 4)
                        e.tensor_scalar(gt1[pi][:, :], glt[pi][:, :], 1e30, -1e30,
                                        ALU.mult, ALU.add).then_inc(sem_dv, 1)
                        e.wait_ge(sem_dv, dv0 + 5)
                        e.tensor_tensor(gt1[pi][:, :], lgs[pi][:, :], gt1[pi][:, :],
                                        ALU.add).then_inc(sem_dv, 1)
                        e.wait_ge(sem_dv, dv0 + 6)
                        e.tensor_reduce(gm2[pi][:, :], gt1[pi][:, :], AX.X, ALU.max
                                        ).then_inc(sem_dv, 1)
                        e.wait_ge(sem_dv, dv0 + 7)
                        e.tensor_scalar(gt1[pi][:, :], lgs[pi][:, :], gm2[pi][:, :],
                                        None, ALU.is_ge).then_inc(sem_dv, 1)
                    lg_free[0] = dv.n + 1
                    dv.n += 8
                    gstate[t] = dict(pi=pi, v_gm=dv.n, v_n2=v_n2)

                    if t > 0:
                        emit_gating_tail(t - 1)

                for t in (TT - 1,):
                    emit_gating_tail(t)

                v_p4_dv = dv.n
                v_p4_dv = dv.n
                v_p4_ac = ac.n
                v_p4_pe = pe.n
                v_p4_po = po.n
                actq.flush(blk.scalar)

            with ExitStack() as p5ps:
                ha_ps = [p5ps.enter_context(
                    nc.psum_tensor(f"ha{i}_{s}", [128, 512], F32)) for i in range(2)]
                fb_ps = [p5ps.enter_context(
                    nc.psum_tensor(f"fp{i}_{s}", [128, 1024], F32)) for i in range(2)]

                w_free = [0, 0]
                ha_free = [0, 0]
                silu_free = [0, 0]
                fbps_free = [0, 0]
                eo_sb_free = [0, 0]
                phaseA_last = {}
                xgt_val = {}
                xgf_free = [0, 0]

                def emit_xgT(e_x, wait_pe):
                    par = (e_x - 1) % 2

                    @blk.sync
                    def _(e, e_x=e_x, par=par, wait_pe=wait_pe,
                          vs=[c_sct[q].n for q in range(4)]):
                        e.wait_ge(sem_pe, wait_pe)
                        for q in range(4):
                            e.wait_ge(sem_sct[q], vs[q])
                        for dc in range(8):
                            e.dma_start(
                                out=xg_fm[par][dc][:, 0:CAPE],
                                in_=xg_d[(e_x - 1) * CAPE:e_x * CAPE,
                                         dc * 128:(dc + 1) * 128],
                                transpose=True).then_inc(sem_xgt[par], 16)
                    c_xgt[par].n += 16 * 8
                    xgt_val[e_x] = (par, c_xgt[par].n)

                n_es = n_exp * n_sub
                v_w_arr = [0] * n_es

                def emit_wdma(es2):
                    e2_i, sub2 = divmod(es2, n_sub)
                    wpi2 = es2 % 2

                    @blk.gpsimd
                    def _(e, e2_i=e2_i, sub2=sub2, wpi2=wpi2, es2=es2):
                        if es2 == 0:
                            e.wait_ge(sem_dv, v_p4_dv)
                            e.wait_ge(sem_ac, v_p4_ac)
                            e.wait_ge(sem_pe, v_p4_pe)
                            e.wait_ge(sem_po, v_p4_po)
                            for q in range(4):
                                e.wait_ge(sem_sct[q], c_sct[q].n)
                        if w_free[wpi2]:
                            e.wait_ge(sem_pe, w_free[wpi2])
                        e.dma_start(out=w1b[wpi2][:, :], in_=w1_h[e2_i, sub2, :, :],
                                    max_dma_last_dim=8192).then_inc(sem_wf[wpi2], 16)
                        e.dma_start(out=w2b[wpi2][:, :], in_=w2_h[e2_i, sub2, :, :],
                                    max_dma_last_dim=8192).then_inc(sem_wf[wpi2], 16)
                    c_wf[wpi2].n += 32
                    v_w_arr[es2] = c_wf[wpi2].n

                emit_wdma(0)
                for es in range(n_es):
                    e_i, sub = divmod(es, n_sub)
                    wpi = es % 2
                    if e_i == 0:
                        blocks = [(tb * 512, 512) for tb in range(4)]
                    else:
                        blocks = [(0, 512), (512, CAPE - 512)]
                    ntbk = len(blocks)
                    if es + 1 < n_es:
                        emit_wdma(es + 1)
                    v_w = v_w_arr[es]

                    xpar = (e_i - 1) % 2
                    for tb, (c0, cw) in enumerate(blocks):
                        spi = tb % 2
                        for hc in range(8):
                            hpi = hc % 2

                            @blk.tensor
                            def _(e, e_i=e_i, sub=sub, hc=hc, hpi=hpi, tb=tb,
                                  spi=spi, wpi=wpi, v_w=v_w, xpar=xpar,
                                  c0=c0, cw=cw):
                                if tb == 0 and hc == 0:
                                    e.wait_ge(sem_wf[wpi], v_w)
                                    if e_i >= 1 and sub == 0:
                                        xp, xv = xgt_val[e_i]
                                        e.wait_ge(sem_xgt[xp], xv)
                                if ha_free[hpi]:
                                    e.wait_ge(sem_ac, ha_free[hpi])
                                if hc < 2 and silu_free[spi]:
                                    e.wait_ge(sem_pe, silu_free[spi])
                                for dc in range(8):
                                    rhs = (nf[dc][:, c0:c0 + cw]
                                           if e_i == 0 else
                                           xg_fm[xpar][dc][:, c0:c0 + cw])
                                    i = e.matmul(
                                        ha_ps[hpi][:, 0:cw],
                                        w1b[wpi][:, dc * 1024 + hc * 128:
                                                 dc * 1024 + (hc + 1) * 128],
                                        rhs, start=(dc == 0), stop=(dc == 7))
                                i.then_inc(sem_pe, 1)
                            pe.n += 1
                            v_h = pe.n

                            def _f(e, hc=hc, hpi=hpi, spi=spi, v_h=v_h, cw=cw):
                                e.wait_ge(sem_pe, v_h)
                                e.activation(silu[spi][:, hc * 512:hc * 512 + cw],
                                             ha_ps[hpi][:, 0:cw], AF.Silu
                                             ).then_inc(sem_ac, 1)
                            actq.add(_f)
                            ac.n += 1
                            ha_free[hpi] = ac.n
                        v_silu = ac.n
                        phaseA_last[e_i] = pe.n

                        for t2 in range(cw // 128):
                            t = tb * 4 + t2
                            fpi = t2 % 2

                            @blk.tensor
                            def _(e, t2=t2, fpi=fpi, spi=spi, wpi=wpi, v_silu=v_silu):
                                e.wait_ge(sem_ac, v_silu)
                                if fbps_free[fpi]:
                                    e.wait_ge(sem_dv, fbps_free[fpi])
                                for hc in range(8):
                                    for j in range(2):
                                        i = e.matmul(
                                            fb_ps[fpi][:, j * 512:(j + 1) * 512],
                                            silu[spi][:, hc * 512 + t2 * 128:
                                                      hc * 512 + (t2 + 1) * 128],
                                            w2b[wpi][:, hc * 1024 + j * 512:
                                                     hc * 1024 + (j + 1) * 512],
                                            start=(hc == 0), stop=(hc == 7))
                                i.then_inc(sem_pe, 1)
                            pe.n += 1
                            v_fb = pe.n
                            if t2 == cw // 128 - 1:
                                silu_free[spi] = pe.n
                                if tb == ntbk - 1:
                                    w_free[wpi] = pe.n

                            if e_i == 0:
                                @blk.vector
                                def _(e, t=t, fpi=fpi, v_fb=v_fb):
                                    e.wait_ge(sem_pe, v_fb)
                                    e.tensor_tensor(xn[t][:, :], fb_ps[fpi][:, :],
                                                    xn[t][:, :], ALU.add
                                                    ).then_inc(sem_dv, 1)
                                dv.n += 1
                                fbps_free[fpi] = dv.n
                            else:
                                @blk.vector
                                def _(e, fpi=fpi, v_fb=v_fb, ew=eo_sb_free[fpi]):
                                    e.wait_ge(sem_pe, v_fb)
                                    if ew:
                                        e.wait_ge(sem_eo2[fpi], ew)
                                    e.tensor_copy(eo_sb[fpi][:, :], fb_ps[fpi][:, :]
                                                  ).then_inc(sem_dv, 1)
                                dv.n += 1
                                fbps_free[fpi] = dv.n
                                v_ev = dv.n
                                row0 = (e_i - 1) * CAPE + c0 + t2 * 128

                                @blk.gpsimd
                                def _(e, fpi=fpi, sub=sub, row0=row0, v_ev=v_ev):
                                    e.wait_ge(sem_dv, v_ev)
                                    if sub == 0:
                                        e.dma_start(out=eo_d[row0:row0 + 128, :],
                                                    in_=eo_sb[fpi][:, :]
                                                    ).then_inc(sem_eo2[fpi], 16)
                                    else:
                                        e.dma_start(out=eo_d[row0:row0 + 128, :],
                                                    in_=eo_sb[fpi][:, :],
                                                    accum_op=ALU.add
                                                    ).then_inc(sem_eo2[fpi], 16)
                                c_eo2[fpi].n += 16
                                eo_sb_free[fpi] = c_eo2[fpi].n

                    if e_i == 0 and sub == n_sub - 1 and n_exp > 1:
                        emit_xgT(1, phaseA_last[0])
                    elif e_i >= 1 and sub == 0 and e_i + 1 <= n_exp - 1:
                        emit_xgT(e_i + 1, phaseA_last[e_i - 1])

                v_ffn_dv = dv.n
                v_eo0 = c_eo2[0].n
                v_eo1 = c_eo2[1].n

                geo_free = [0, 0]
                for t in range(TT):
                    pi = t % 2

                    @blk.gpsimd
                    def _(e, t=t, pi=pi):
                        if t == 0:
                            e.wait_ge(sem_eo2[0], v_eo0)
                            e.wait_ge(sem_eo2[1], v_eo1)
                            e.wait_ge(sem_dv, v_ffn_dv)
                        if geo_free[pi]:
                            e.wait_ge(sem_dv, geo_free[pi])
                        e.indirect_dma_start(
                            out=geo[0][pi][:, :], out_offset=None,
                            in_=eo_d[:, :],
                            in_offset=bass.IndirectOffsetOnAxis(
                                ap=sidx1[t][:, :], axis=0)).then_inc(sem_geo[pi], 16)
                        e.indirect_dma_start(
                            out=geo[1][pi][:, :], out_offset=None,
                            in_=eo_d[:, :],
                            in_offset=bass.IndirectOffsetOnAxis(
                                ap=sidx2[t][:, :], axis=0)).then_inc(sem_geo[pi], 16)
                    c_geo[pi].n += 32
                    v_g = c_geo[pi].n

                    @blk.vector
                    def _(e, t=t, pi=pi, v_g=v_g, dv0=dv.n):
                        e.wait_ge(sem_geo[pi], v_g)
                        e.scalar_tensor_tensor(xn[t][:, :], geo[0][pi][:, :],
                                               gsl1[t][:, :], xn[t][:, :],
                                               ALU.mult, ALU.add).then_inc(sem_dv, 1)
                        e.wait_ge(sem_dv, dv0 + 1)
                        e.scalar_tensor_tensor(xn[t][:, :], geo[1][pi][:, :],
                                               gsl2[t][:, :], xn[t][:, :],
                                               ALU.mult, ALU.add).then_inc(sem_dv, 1)
                    dv.n += 2
                    geo_free[pi] = dv.n
                    v_fin = dv.n

                    @blk.sync
                    def _(e, t=t, v_fin=v_fin):
                        e.wait_ge(sem_dv, v_fin)
                        e.dma_start(out=out_h[s, t * 128:(t + 1) * 128, :],
                                    in_=xn[t][:, :]).then_inc(sem_st, 16)
                    st.n += 16

                actq.flush(blk.scalar)

        @blk.sync
        def _(e):
            e.wait_ge(sem_st, st.n)

    return nc



def _prep_weights(W_qkv, b_qkv, W_o, b_o, sw1, sw2, ew1, ew2, gw):
    w4 = np.asarray(W_qkv, np.float32).reshape(NH, DH, 3, D)
    wq = w4[:, :, 0, :].reshape(D, D)
    wk = w4[:, :, 1, :].reshape(D, D)
    wv = w4[:, :, 2, :].reshape(D, D)
    wqkvT = np.concatenate([wq.T, wk.T, wv.T], axis=1)
    wqkv_pre = np.empty((24, 128, 1024), np.float32)
    for fbk in range(24):
        sl = wqkvT[:, fbk * 128:(fbk + 1) * 128]
        wqkv_pre[fbk] = sl.reshape(8, 128, 128).transpose(1, 0, 2).reshape(128, 1024)

    b4 = np.asarray(b_qkv, np.float32).reshape(NH, DH, 3)
    b_r = np.concatenate([b4[:, :, 0].reshape(-1), b4[:, :, 1].reshape(-1),
                          b4[:, :, 2].reshape(-1)])
    bqkv_pre = np.ascontiguousarray(b_r.reshape(24, 128).T)

    woT = np.asarray(W_o, np.float32).T
    wo_pre = woT.reshape(8, 128, 1024).transpose(1, 0, 2).reshape(128, 8192)

    gwT = np.asarray(gw, np.float32).T
    gw_pre = np.ascontiguousarray(gwT.reshape(8, 128, 8).transpose(1, 0, 2)
                                  .reshape(128, 64))

    w1T = np.concatenate([np.asarray(sw1, np.float32).T[None],
                          np.asarray(ew1, np.float32).transpose(0, 2, 1)], axis=0)
    w2T = np.concatenate([np.asarray(sw2, np.float32).T[None],
                          np.asarray(ew2, np.float32).transpose(0, 2, 1)], axis=0)
    w1_pre = np.empty((NE, NSUB, 128, 8192), np.float32)
    w2_pre = np.empty((NE, NSUB, 128, 8192), np.float32)
    for e in range(NE):
        for sub in range(NSUB):
            sl = w1T[e][:, sub * 1024:(sub + 1) * 1024]
            w1_pre[e, sub] = sl.reshape(8, 128, 1024).transpose(1, 0, 2).reshape(128, 8192)
            sl2 = w2T[e][sub * 1024:(sub + 1) * 1024, :]
            w2_pre[e, sub] = sl2.reshape(8, 128, 1024).transpose(1, 0, 2).reshape(128, 8192)

    return {
        "wqkv": np.ascontiguousarray(wqkv_pre),
        "bqkv": bqkv_pre,
        "wo": np.ascontiguousarray(wo_pre),
        "bo": np.ascontiguousarray(np.asarray(b_o, np.float32).reshape(1, 1024)),
        "gw": gw_pre,
        "w1": np.ascontiguousarray(w1_pre),
        "w2": np.ascontiguousarray(w2_pre),
    }


def kernel(L0_attn_x, L0_attn_W_qkv, L0_attn_b_qkv, L0_attn_W_o, L0_attn_b_o,
           L0_ffn_shared_w1, L0_ffn_shared_w2, L0_ffn_experts_w1,
           L0_ffn_experts_w2, L0_ffn_gate_w):
    x = np.ascontiguousarray(np.asarray(L0_attn_x, dtype=np.float32))
    wmap = _prep_weights(L0_attn_W_qkv, L0_attn_b_qkv, L0_attn_W_o, L0_attn_b_o,
                         L0_ffn_shared_w1, L0_ffn_shared_w2, L0_ffn_experts_w1,
                         L0_ffn_experts_w2, L0_ffn_gate_w)
    nc = build_nc()
    in_maps = []
    for c in range(8):
        m = dict(wmap)
        m["x"] = np.ascontiguousarray(x[2 * c:2 * c + 2])
        in_maps.append(m)
    res = run_bass_kernel_spmd(nc, in_maps, core_ids=list(range(8)))
    out = np.concatenate([res.results[c]["out"] for c in range(8)], axis=0)
    return out.astype(np.float32)


# version 16
# speedup vs baseline: 1.0595x; 1.0016x over previous
import numpy as np
from contextlib import ExitStack

import concourse.bass as bass
import concourse.mybir as mybir
from concourse.bass_utils import run_bass_kernel_spmd

F32 = mybir.dt.float32
BF16 = mybir.dt.bfloat16
U32 = mybir.dt.uint32
AF = mybir.ActivationFunctionType
ALU = mybir.AluOpType
AX = mybir.AxisListType

D = 1024
S = 2048
NH = 16
DH = 64
NE = 9
NSUB = 4
TT = 16
CAPE = 1024
EPS1 = 1e-5
EPS2 = 1.1920929e-07
KB = 1024


class SC:
    def __init__(self, handle, step=1):
        self.h = handle
        self.n = 0
        self.step = step


class DeferredQ:
    def __init__(self):
        self.q = []

    def add(self, fn):
        self.q.append(fn)

    def flush(self, section_decorator):
        q = self.q
        self.q = []

        def _run(e):
            for fn in q:
                fn(e)
        section_decorator(_run)


def build_nc(n_seq=2, n_pairs=8, n_exp=NE, n_sub=NSUB):
    nc = bass.Bass("TRN2", target_bir_lowering=False)

    x_h = nc.dram_tensor("x", [n_seq, S, D], F32, kind="ExternalInput")
    wqkv_h = nc.dram_tensor("wqkv", [24, 128, 1024], F32, kind="ExternalInput")
    bqkv_h = nc.dram_tensor("bqkv", [128, 24], F32, kind="ExternalInput")
    wo_h = nc.dram_tensor("wo", [128, 8192], F32, kind="ExternalInput")
    bo_h = nc.dram_tensor("bo", [1, 1024], F32, kind="ExternalInput")
    gw_h = nc.dram_tensor("gw", [128, 64], F32, kind="ExternalInput")
    w1_h = nc.dram_tensor("w1", [NE, NSUB, 128, 8192], F32, kind="ExternalInput")
    w2_h = nc.dram_tensor("w2", [NE, NSUB, 128, 8192], F32, kind="ExternalInput")
    out_h = nc.dram_tensor("out", [n_seq, S, D], F32, kind="ExternalOutput")
    xg_d = nc.dram_tensor("xg_d", [8 * CAPE, 1024], BF16)
    eo_d = nc.dram_tensor("eo_d", [8 * CAPE, 1024], BF16)

    cur = [((nc.sbuf_base + 31) // 32) * 32]

    def place(name, shape, dt):
        t = nc.alloc_sbuf_tensor_at(name, shape, dt, offset=cur[0])
        lo, hi = t.manual_sbuf_range
        cur[0] = ((hi + 31) // 32) * 32
        return t

    def place_at(name, shape, dt, off):
        return nc.alloc_sbuf_tensor_at(name, shape, dt, offset=off)

    ident = place("ident", [128, 128], BF16)
    tri = place("tri", [128, 128], BF16)
    bqkv_t = place("bqkv_t", [128, 24], F32)
    bo_bc = place("bo_bc", [128, 1024], F32)
    bo_row = place("bo_row", [1, 1024], F32)
    gate_bf = place("gate_bf", [128, 64], BF16)
    ones_r = place("ones_r", [1, 128], F32)
    bnb = [place(f"bnb{i}", [128, 12], F32) for i in range(2)]
    bno = [place(f"bno{i}", [128, 2], F32) for i in range(2)]
    aab = [place(f"aab{i}", [128, 1], F32) for i in range(2)]
    bbb = [place(f"bbb{i}", [128, 1], F32) for i in range(2)]
    rm4 = [place(f"rm4{h}", [128, 4], F32) for h in range(2)]
    rs4 = [place(f"rs4{h}", [128, 8], F32) for h in range(2)]
    rmin = [place(f"rmin{h}", [128, 1], F32) for h in range(2)]
    rst = [place(f"rst{h}", [128, 1], F32) for h in range(2)]
    rin = [place(f"rin{h}", [128, 1], F32) for h in range(2)]
    vsc = [[place(f"vsc{h}_{i}", [128, 64], BF16) for i in range(2)] for h in range(2)]
    lgs = [place(f"lgs{i}", [128, 8], F32) for i in range(2)]
    glt = [place(f"glt{i}", [128, 8], F32) for i in range(2)]
    gt1 = [place(f"gt1{i}", [128, 8], F32) for i in range(2)]
    gm1 = [place(f"gm1{i}", [128, 1], F32) for i in range(2)]
    gm2 = [place(f"gm2{i}", [128, 1], F32) for i in range(2)]
    gsum = [place(f"gsum{i}", [128, 1], F32) for i in range(2)]
    gates = [place(f"gates{t}", [128, 8], F32) for t in range(TT)]
    eidx = place("eidx", [128, 8], F32)
    oh1 = [place(f"oh1_{i}", [128, 8], F32) for i in range(2)]
    oh2 = [place(f"oh2_{i}", [128, 8], F32) for i in range(2)]
    tf8 = [place(f"tf8_{i}", [128, 8], F32) for i in range(2)]
    e1f = [place(f"e1f{i}", [128, 1], F32) for i in range(2)]
    e2f = [place(f"e2f{i}", [128, 1], F32) for i in range(2)]
    s1f = [place(f"s1f{i}", [128, 1], F32) for i in range(2)]
    s2f = [place(f"s2f{i}", [128, 1], F32) for i in range(2)]
    slf = [place(f"slf{i}", [128, 1], F32) for i in range(2)]
    sidx1 = [place(f"sidx1_{t}", [128, 1], U32) for t in range(TT)]
    sidx2 = [place(f"sidx2_{t}", [128, 1], U32) for t in range(TT)]
    gsl1 = [place(f"gsl1_{t}", [128, 1], F32) for t in range(TT)]
    gsl2 = [place(f"gsl2_{t}", [128, 1], F32) for t in range(TT)]
    cumf = place("cumf", [8, 128], F32)
    cuml = place("cuml", [8, 1], F32)
    zero8 = place("zero8", [8, 128], F32)
    identf = place("identf", [128, 128], F32)
    cb_ln1 = place("cb_ln1", [128, 1], F32)
    cb_eps = place("cb_eps", [128, 1], F32)
    cb_att = place("cb_att", [128, 1], F32)

    NF0 = cur[0]
    nf = [place(f"nf{c}", [128, 2048], BF16) for c in range(8)]
    X0 = cur[0]
    qp = [place_at(f"qp{p}", [128, 2048], BF16, X0 + p * 4 * KB) for p in range(8)]
    vtm = [place_at(f"vtm{t}", [128, 1024], BF16, X0 + 32 * KB + t * 2 * KB)
           for t in range(TT)]
    xn = [place_at(f"xn{t}", [128, 1024], F32, X0 + t * 4 * KB) for t in range(TT)]
    Y0 = X0 + 64 * KB
    kp = [place_at(f"kp{p}", [128, 2048], BF16, Y0 + p * 4 * KB) for p in range(8)]
    zfm = [place_at(f"zfm{p}", [128, 2048], BF16, Y0 + p * 4 * KB) for p in range(8)]
    silu = [place_at(f"silu{i}", [128, 4096], BF16, Y0 + i * 8 * KB) for i in range(2)]
    Z0 = Y0 + 32 * KB
    wo_bf = place_at("wo_bf", [128, 8192], BF16, Z0)
    pt = [[place_at(f"pt{h}_{i}", [128, 2048], BF16,
                    Z0 + 16 * KB + (h * 2 + i) * 4 * KB) for i in range(2)]
          for h in range(2)]
    xs = [place_at(f"xs{i}", [128, 1024], F32, Z0 + 32 * KB + i * 4 * KB)
          for i in range(2)]
    ntb = [place_at(f"ntb{i}", [128, 1024], BF16, Z0 + 40 * KB + i * 2 * KB)
           for i in range(2)]
    wqb = [place_at(f"wqb{i}", [128, 1024], BF16, Z0 + 44 * KB + i * 2 * KB)
           for i in range(2)]
    vscr = place_at("vscr", [128, 2048], BF16, Z0 + 48 * KB)
    xr = [place_at(f"xr{i}", [128, 1024], F32, Z0 + 16 * KB + i * 4 * KB)
          for i in range(2)]
    xb = [place_at(f"xb{i}", [128, 1024], F32, Z0 + 24 * KB + i * 4 * KB)
          for i in range(2)]
    n2t = [place_at(f"n2t{i}", [128, 1024], BF16, Z0 + 52 * KB + i * 2 * KB)
           for i in range(4)]
    w1b = [place_at(f"w1b{i}", [128, 8192], BF16, Z0 + i * 16 * KB) for i in range(2)]
    w2b = [place_at(f"w2b{i}", [128, 8192], BF16, Z0 + 32 * KB + i * 16 * KB)
           for i in range(2)]
    xg_fm = [[place_at(f"xgfm{par}_{dc}", [128, 1024], BF16,
                       NF0 + par * 16 * KB + dc * 2 * KB) for dc in range(8)]
             for par in range(2)]
    geo = [[place_at(f"geo{g}_{i}", [128, 1024], BF16,
                     Y0 + 16 * KB + (g * 2 + i) * 2 * KB) for i in range(2)]
           for g in range(2)]
    eo_sb = [place_at(f"eosb{i}", [128, 1024], BF16, Y0 + 24 * KB + i * 2 * KB)
             for i in range(2)]
    assert Z0 + 64 * KB <= nc.sbuf_top, (Z0 + 64 * KB, nc.sbuf_top)

    with ExitStack() as top:
        sem_sc = top.enter_context(nc.semaphore("sc"))
        sem_gw = top.enter_context(nc.semaphore("gwl"))
        sem_x = [top.enter_context(nc.semaphore(f"x{i}")) for i in range(2)]
        sem_wq = [top.enter_context(nc.semaphore(f"wq{i}")) for i in range(2)]
        sem_wo = top.enter_context(nc.semaphore("wol"))
        sem_wf = [top.enter_context(nc.semaphore(f"wf{i}")) for i in range(2)]
        sem_st = top.enter_context(nc.semaphore("st"))
        sem_st2 = top.enter_context(nc.semaphore("st2"))
        sem_sct = [top.enter_context(nc.semaphore(f"sct{i}")) for i in range(4)]
        sem_eo2 = [top.enter_context(nc.semaphore(f"eo{i}")) for i in range(2)]
        sem_xgt = [top.enter_context(nc.semaphore(f"xgt{i}")) for i in range(2)]
        sem_geo = [top.enter_context(nc.semaphore(f"geo{i}")) for i in range(2)]
        sem_zi = top.enter_context(nc.semaphore("zi"))
        sem_pe = top.enter_context(nc.semaphore("pe"))
        sem_dv = top.enter_context(nc.semaphore("dv"))
        sem_ac = top.enter_context(nc.semaphore("ac"))
        sem_po = top.enter_context(nc.semaphore("po"))
        blk = top.enter_context(nc.Block())

        st = SC(sem_st, 16)
        st2 = SC(sem_st2, 16)
        c_sct = [SC(sem_sct[i], 16) for i in range(4)]
        c_eo2 = [SC(sem_eo2[0], 16), SC(sem_eo2[1], 16)]
        c_xgt = [SC(sem_xgt[0], 16), SC(sem_xgt[1], 16)]
        c_geo = [SC(sem_geo[0], 16), SC(sem_geo[1], 16)]
        c_zi = SC(sem_zi, 16)
        c_x = [SC(sem_x[0], 16), SC(sem_x[1], 16)]
        c_wq = [SC(sem_wq[0], 16), SC(sem_wq[1], 16)]
        c_wo = SC(sem_wo, 16)
        c_wf = [SC(sem_wf[0], 16), SC(sem_wf[1], 16)]
        pe, dv, ac, po = SC(sem_pe), SC(sem_dv), SC(sem_ac), SC(sem_po)

        @blk.sync
        def _(e):
            e.dma_start(out=bqkv_t[:, :], in_=bqkv_h[:, :]).then_inc(sem_sc, 16)
            e.dma_start(out=bo_row[:, :], in_=bo_h[:, :]).then_inc(sem_sc, 16)
        v_bias_ld = 32

        @blk.gpsimd
        def _(e):
            e.dma_start(out=gate_bf[:, :], in_=gw_h[:, :]).then_inc(sem_gw, 16)
            e.memset(ident[:, :], 0.0).then_inc(sem_po, 1)
            e.memset(tri[:, :], 0.0).then_inc(sem_po, 1)
            e.memset(ones_r[:, :], 1.0).then_inc(sem_po, 1)
            e.memset(cb_ln1[:, :], float(EPS2 * EPS1)).then_inc(sem_po, 1)
            e.memset(cb_eps[:, :], float(EPS1)).then_inc(sem_po, 1)
            e.memset(cb_att[:, :], -44.0).then_inc(sem_po, 1)
            e.memset(identf[:, :], 0.0).then_inc(sem_po, 1)
            e.memset(zero8[:, :], 0.0).then_inc(sem_po, 1)
            e.memset(silu[0][:, :], 0.0).then_inc(sem_po, 1)
            e.iota(eidx[:, :], pattern=[[1, 8]], base=0, channel_multiplier=0,
                   allow_small_or_imprecise_dtypes=True).then_inc(sem_po, 1)
            e.wait_ge(sem_po, 10)
            e.affine_select(out=identf[:, :], in_=identf[:, :],
                            compare_op=ALU.not_equal, fill=1.0, base=0,
                            pattern=[[-1, 128]], channel_multiplier=1
                            ).then_inc(sem_po, 1)
            e.wait_ge(sem_po, 11)
            e.affine_select(out=ident[:, :], in_=ident[:, :], compare_op=ALU.not_equal,
                            fill=1.0, base=0, pattern=[[-1, 128]],
                            channel_multiplier=1).then_inc(sem_po, 1)
            e.wait_ge(sem_po, 12)
            e.affine_select(out=tri[:, :], in_=tri[:, :], compare_op=ALU.is_gt,
                            fill=1.0, base=0, pattern=[[-1, 128]],
                            channel_multiplier=1).then_inc(sem_po, 1)
        po.n += 13
        v_gate_ld = 16
        v_consts = po.n

        with nc.psum_tensor("bc_ps", [128, 1024], F32) as bc_ps:
            @blk.tensor
            def _(e):
                e.wait_ge(sem_sc, v_bias_ld)
                e.wait_ge(sem_po, v_consts)
                e.matmul(bc_ps[:, 0:512], ones_r[:, :], bo_row[:, 0:512],
                         start=True, stop=True)
                e.matmul(bc_ps[:, 512:1024], ones_r[:, :], bo_row[:, 512:1024],
                         start=True, stop=True).then_inc(sem_pe, 1)
            pe.n += 1
            v_bc_mm = pe.n

            @blk.sync
            def _(e):
                e.wait_ge(sem_po, v_consts)
                nzc = 8 * CAPE // 512
                for c in range(nzc):
                    e.dma_start(out=xg_d[c * 512:(c + 1) * 512, :],
                                in_=silu[0][:, :]).then_inc(sem_zi, 16)
                for c in range(nzc):
                    e.dma_start(out=eo_d[c * 512:(c + 1) * 512, :],
                                in_=silu[0][:, :]).then_inc(sem_zi, 16)
            c_zi.n += 16 * 2 * (8 * CAPE // 512)
            v_zi = c_zi.n

            @blk.vector
            def _(e):
                e.wait_ge(sem_pe, v_bc_mm)
                e.tensor_copy(bo_bc[:, :], bc_ps[:, :]).then_inc(sem_dv, 1)
            dv.n += 1
        v_bc_done = dv.n

        qfbs = list(range(0, n_pairs)) + list(range(8, 8 + n_pairs)) \
            + list(range(16, 16 + n_pairs))

        actq = DeferredQ()
        for s in range(n_seq):
            if s > 0:
                bprev = dict(pe=pe.n, dv=dv.n, ac=ac.n, po=po.n,
                             st=st.n, st2=st2.n)

                @blk.sync
                def _(e, bprev=bprev):
                    e.wait_ge(sem_pe, bprev["pe"])

                @blk.gpsimd
                def _(e, bprev=bprev):
                    e.wait_ge(sem_pe, bprev["pe"])

                @blk.tensor
                def _(e, bprev=bprev):
                    e.wait_ge(sem_ac, bprev["ac"])

            with ExitStack() as p1ps:
                tp_ps = [p1ps.enter_context(
                    nc.psum_tensor(f"tp{i}_{s}", [128, 128], BF16)) for i in range(2)]
                qkv_ps = [p1ps.enter_context(
                    nc.psum_tensor(f"qk{i}_{s}", [128, 512], F32)) for i in range(4)]

                xs_free = [0, 0]
                ntb_free = [0, 0]
                tp_free = [0, 0]
                nf_ready = [0] * TT
                xv = {}

                def add_xs_load(tt, on_sync, after=None):
                    wfree = xs_free[tt % 2]
                    if on_sync:
                        @blk.sync
                        def _(e, tt=tt, wfree=wfree, after=after):
                            if after:
                                e.wait_ge(sem_x[after[0]], after[1])
                            if wfree:
                                e.wait_ge(sem_dv, wfree)
                            e.dma_start(out=xs[tt % 2][:, :],
                                        in_=x_h[s, tt * 128:(tt + 1) * 128, :]
                                        ).then_inc(sem_x[0], 16)
                        c_x[0].n += 16
                        xv[tt] = (0, c_x[0].n)
                    else:
                        def _f(e, tt=tt, wfree=wfree):
                            if wfree:
                                e.wait_ge(sem_dv, wfree)
                            e.dma_start(out=xs[tt % 2][:, :],
                                        in_=x_h[s, tt * 128:(tt + 1) * 128, :]
                                        ).then_inc(sem_x[1], 16)
                        actq.add(_f)
                        c_x[1].n += 16
                        xv[tt] = (1, c_x[1].n)

                add_xs_load(0, True)
                add_xs_load(1, False)

                for t in range(TT):
                    pi = t % 2
                    si, v_x = xv[t]

                    @blk.vector
                    def _(e, pi=pi, si=si, v_x=v_x):
                        e.wait_ge(sem_x[si], v_x)
                        e.bn_stats(bnb[pi][:, 0:6], xs[pi][:, 0:512])
                        e.bn_stats(bnb[pi][:, 6:12], xs[pi][:, 512:1024]
                                   ).then_inc(sem_dv, 1)
                        e.wait_ge(sem_dv, dv.n + 1)
                        e.bn_aggr(bno[pi][:, :], bnb[pi][:, :]).then_inc(sem_dv, 1)
                    dv.n += 2
                    v_aggr = dv.n

                    def _f(e, t=t, pi=pi, v_aggr=v_aggr, s=s, ac0=ac.n):
                        if t == 0 and s == 0:
                            e.wait_ge(sem_po, v_consts)
                        e.wait_ge(sem_dv, v_aggr)
                        e.activation(aab[pi][:, :], bno[pi][:, 1:2], AF.Ln,
                                     scale=float(1.0 + EPS2),
                                     bias=cb_ln1[:, :]).then_inc(sem_ac, 1)
                        e.wait_ge(sem_ac, ac0 + 1)
                        e.activation(aab[pi][:, :], aab[pi][:, :], AF.Exp,
                                     scale=-0.5).then_inc(sem_ac, 1)
                    actq.add(_f)
                    ac.n += 2
                    v_scale = ac.n

                    @blk.vector
                    def _(e, pi=pi, v_scale=v_scale):
                        e.wait_ge(sem_ac, v_scale)
                        e.tensor_scalar(bbb[pi][:, :], bno[pi][:, 0:1], aab[pi][:, :],
                                        -1.0, ALU.mult, ALU.mult).then_inc(sem_dv, 1)
                        e.wait_ge(sem_dv, dv.n + 1)
                        if ntb_free[pi]:
                            e.wait_ge(sem_pe, ntb_free[pi])
                        e.tensor_scalar(ntb[pi][:, :], xs[pi][:, :], aab[pi][:, :],
                                        bbb[pi][:, :], ALU.mult, ALU.add
                                        ).then_inc(sem_dv, 1)
                    dv.n += 2
                    v_norm = dv.n
                    xs_free[pi] = dv.n

                    base_ac = ac.n

                    @blk.tensor
                    def _(e, t=t, pi=pi, v_norm=v_norm, base_ac=base_ac):
                        e.wait_ge(sem_dv, v_norm)
                        if t == 0 and s == 0:
                            e.wait_ge(sem_po, v_consts)
                        if s > 0 and t == 0:
                            pass
                        for c in range(8):
                            w = tp_free[c % 2] if c < 2 else base_ac + c - 1
                            if w:
                                e.wait_ge(sem_ac, w)
                            e.transpose(tp_ps[c % 2][:, :],
                                        ntb[pi][:, c * 128:(c + 1) * 128],
                                        ident[:, :]).then_inc(sem_pe, 1)
                    base_pe = pe.n
                    pe.n += 8
                    ntb_free[pi] = pe.n

                    def _f(e, t=t, base_pe=base_pe):
                        for c in range(8):
                            e.wait_ge(sem_pe, base_pe + c + 1)
                            e.copy(nf[c][:, t * 128:(t + 1) * 128],
                                   tp_ps[c % 2][:, :]).then_inc(sem_ac, 1)
                    actq.add(_f)
                    ac.n += 8
                    tp_free[0] = ac.n - 1
                    tp_free[1] = ac.n
                    nf_ready[t] = ac.n
                    if t + 2 < TT:
                        add_xs_load(t + 2, t % 2 == 0)

                v_nf_all = ac.n

                wqb_free = [0, 0]
                qps_free = [0, 0, 0, 0]
                vscr_free = 0

                for k, fb in enumerate(qfbs):
                    pi = k % 2

                    @blk.gpsimd
                    def _(e, fb=fb, pi=pi):
                        if wqb_free[pi]:
                            e.wait_ge(sem_pe, wqb_free[pi])
                        e.dma_start(out=wqb[pi][:, :], in_=wqkv_h[fb, :, :]
                                    ).then_inc(sem_wq[pi], 16)
                    c_wq[pi].n += 16
                    v_wq = c_wq[pi].n

                    is_v = fb >= 16
                    if fb < 8:
                        dst = qp[fb]
                    elif fb < 16:
                        dst = kp[fb - 8]
                    else:
                        dst = vscr

                    for tc in range(4):
                        pj = (k * 4 + tc) % 4

                        @blk.tensor
                        def _(e, k=k, fb=fb, pi=pi, tc=tc, pj=pj, v_wq=v_wq):
                            e.wait_ge(sem_wq[pi], v_wq)
                            if k == 0:
                                e.wait_ge(sem_ac, nf_ready[4 * tc + 3])
                            if qps_free[pj]:
                                e.wait_ge(sem_ac, qps_free[pj])
                            for dc in range(8):
                                i = e.matmul(qkv_ps[pj][:, :],
                                             wqb[pi][:, dc * 128:(dc + 1) * 128],
                                             nf[dc][:, tc * 512:(tc + 1) * 512],
                                             start=(dc == 0), stop=(dc == 7))
                            i.then_inc(sem_pe, 1)
                        pe.n += 1
                        v_mm = pe.n
                        if tc == 3:
                            wqb_free[pi] = v_mm

                        def _f(e, k=k, fb=fb, tc=tc, pj=pj, v_mm=v_mm, dst=dst,
                               is_v=is_v, vf=vscr_free, s=s):
                            e.wait_ge(sem_pe, v_mm)
                            if tc == 0:
                                if s > 0 and k == 0:
                                    e.wait_ge(sem_st, bprev["st"])
                                    e.wait_ge(sem_st2, bprev["st2"])
                                e.wait_ge(sem_sc, v_bias_ld)
                                if s == 0 and fb == 8:
                                    e.wait_ge(sem_zi, v_zi)
                                if is_v and vf:
                                    e.wait_ge(sem_pe, vf)
                            e.activation(dst[:, tc * 512:(tc + 1) * 512],
                                         qkv_ps[pj][:, :], AF.Identity,
                                         bias=bqkv_t[:, fb:fb + 1]).then_inc(sem_ac, 1)
                        actq.add(_f)
                        ac.n += 1
                        qps_free[pj] = ac.n

                    if is_v:
                        v_evac = ac.n
                        col = (fb - 16) * 128
                        base_ac2 = ac.n

                        @blk.tensor
                        def _(e, v_evac=v_evac, base_ac2=base_ac2):
                            e.wait_ge(sem_ac, v_evac)
                            for t in range(TT):
                                w = tp_free[t % 2] if t < 2 else base_ac2 + t - 1
                                if w:
                                    e.wait_ge(sem_ac, w)
                                e.transpose(tp_ps[t % 2][:, :],
                                            vscr[:, t * 128:(t + 1) * 128],
                                            ident[:, :]).then_inc(sem_pe, 1)
                        base_pe2 = pe.n
                        pe.n += TT
                        vscr_free = pe.n

                        def _f(e, col=col, base_pe2=base_pe2):
                            for t in range(TT):
                                e.wait_ge(sem_pe, base_pe2 + t + 1)
                                e.copy(vtm[t][:, col:col + 128], tp_ps[t % 2][:, :]
                                       ).then_inc(sem_ac, 1)
                        actq.add(_f)
                        ac.n += TT
                        tp_free[0] = ac.n - 1
                        tp_free[1] = ac.n

                v_qkv_ac = ac.n

                @blk.gpsimd
                def _(e):
                    if wqb_free[0]:
                        e.wait_ge(sem_pe, wqb_free[0])
                    e.dma_start(out=wo_bf[:, :], in_=wo_h[:, :]).then_inc(sem_wo, 16)
                c_wo.n += 16
                v_wo = c_wo.n
                actq.flush(blk.scalar)

            with ExitStack() as p2ps:
                s_ps = [p2ps.enter_context(
                    nc.psum_tensor(f"sp{i}_{s}", [128, 512], F32)) for i in range(4)]
                z_ps = p2ps.enter_context(nc.psum_tensor(f"zp_{s}", [128, 2048], F32))

                pt_free = [[0, 0], [0, 0]]
                rs4_free = [[0, 0], [0, 0]]
                sps_exp = [0, 0, 0, 0]
                vsc_free = [[0, 0], [0, 0]]
                zwin_stop = [0] * 4
                zev_done = [0] * 4
                scnt = [0]

                def emit_S(p, kt):
                    k0 = kt * 128
                    w0 = k0 // 512
                    qa = w0 * 512
                    nch = 4 - w0
                    res = {}
                    ac_base = ac.n
                    v_p2 = {}
                    tiles = {}

                    @blk.tensor
                    def _(e, p=p, kt=kt, k0=k0, qa=qa, nch=nch, ac_base=ac_base):
                        local_exp = {}
                        if p == 0 and kt == 0:
                            e.wait_ge(sem_ac, v_qkv_ac)
                        for c in range(nch):
                            for h in range(2):
                                rows = slice(h * 64, h * 64 + 64)
                                j = scnt[0] % 4
                                scnt[0] += 1
                                tiles[(h, c)] = j
                                w = local_exp.get(j, sps_exp[j])
                                if w:
                                    e.wait_ge(sem_ac, w)
                                local_exp[j] = ac_base + 2 * c + h + 1
                                qc0 = qa + c * 512
                                e.matmul(s_ps[j][:, :],
                                         kp[p][rows, k0:k0 + 128],
                                         qp[p][rows, qc0:qc0 + 512],
                                         start=True, stop=True).then_inc(sem_pe, 1)
                                pe.n += 1
                                v_p2[(h, c)] = pe.n

                    ppi = kt % 2
                    ptw = {h: pt_free[h][ppi] for h in range(2)}
                    rs4w = {h: rs4_free[h][ppi] for h in range(2)}

                    def _f(e, kt=kt, k0=k0, qa=qa, nch=nch, ppi=ppi,
                           v_p2=dict(v_p2), tiles=dict(tiles), ptw=ptw, rs4w=rs4w):
                        for c in range(nch):
                            for h in range(2):
                                e.wait_ge(sem_pe, v_p2[(h, c)])
                                if c == 0:
                                    if ptw[h]:
                                        e.wait_ge(sem_pe, ptw[h])
                                    if rs4w[h]:
                                        e.wait_ge(sem_dv, rs4w[h])
                                    off = k0 - qa
                                    e.activation(pt[h][ppi][:, k0:qa + 512],
                                                 s_ps[tiles[(h, c)]][:, off:512],
                                                 AF.Exp,
                                                 bias=cb_att[:, :]).then_inc(sem_ac, 1)
                                else:
                                    e.activation(
                                        pt[h][ppi][:, qa + c * 512:qa + (c + 1) * 512],
                                        s_ps[tiles[(h, c)]][:, :], AF.Exp,
                                        bias=cb_att[:, :],
                                        accum_out=rs4[h][:, 4 * ppi + c:4 * ppi + c + 1]
                                        ).then_inc(sem_ac, 1)
                    actq.add(_f)
                    for (h, c), j in tiles.items():
                        sps_exp[j] = ac_base + 2 * c + h + 1
                    ac.n += 2 * nch

                    for h in range(2):
                        v_exp = ac_base + 2 * (nch - 1) + h + 1
                        v_exp0 = ac_base + h + 1

                        v_zero = po.n
                        if qa < k0:
                            @blk.gpsimd
                            def _(e, h=h, qa=qa, k0=k0, ppi=ppi, v_exp0=v_exp0):
                                e.wait_ge(sem_ac, v_exp0)
                                e.memset(pt[h][ppi][:, qa:k0], 0.0).then_inc(sem_po, 1)
                            po.n += 1
                            v_zero = po.n

                        @blk.vector
                        def _(e, h=h, p=p, kt=kt, k0=k0, qa=qa, nch=nch, ppi=ppi,
                              v_exp=v_exp, v_exp0=v_exp0, v_zero=v_zero, dv0=dv.n):
                            e.wait_ge(sem_ac, v_exp0)
                            e.tensor_tensor(pt[h][ppi][:, k0:k0 + 128],
                                            pt[h][ppi][:, k0:k0 + 128], tri[:, :],
                                            ALU.mult).then_inc(sem_dv, 1)
                            e.wait_ge(sem_dv, dv0 + 1)
                            if qa < k0:
                                e.wait_ge(sem_po, v_zero)
                            e.tensor_reduce(rs4[h][:, 4 * ppi:4 * ppi + 1],
                                            pt[h][ppi][:, qa:qa + 512], AX.X,
                                            ALU.add).then_inc(sem_dv, 1)
                            e.wait_ge(sem_ac, v_exp)
                            e.wait_ge(sem_dv, dv0 + 2)
                            e.tensor_reduce(rst[h][:, :],
                                            rs4[h][:, 4 * ppi:4 * ppi + nch], AX.X,
                                            ALU.add).then_inc(sem_dv, 1)
                            e.wait_ge(sem_dv, dv0 + 3)
                            e.reciprocal(rin[h][:, :], rst[h][:, :]
                                         ).then_inc(sem_dv, 1)
                            e.wait_ge(sem_dv, dv0 + 4)
                            if vsc_free[h][kt % 2]:
                                e.wait_ge(sem_pe, vsc_free[h][kt % 2])
                            e.tensor_scalar(
                                vsc[h][kt % 2][:, :],
                                vtm[kt][:, (2 * p + h) * 64:(2 * p + h) * 64 + 64],
                                rin[h][:, :], None, ALU.mult).then_inc(sem_dv, 1)
                        dv.n += 5
                        rs4_free[h][ppi] = dv.n - 2
                        res[h] = (dv.n, v_zero, ppi)
                    return res

                def emit_Z(p, kt, res):
                    k0 = kt * 128
                    w0 = k0 // 512

                    @blk.tensor
                    def _(e, p=p, kt=kt, w0=w0, res=dict(res)):
                        for h in range(2):
                            v_vsc, v_zero, ppi = res[h]
                            e.wait_ge(sem_dv, v_vsc)
                            if v_zero:
                                e.wait_ge(sem_po, v_zero)
                        for w in range(w0, 4):
                            for h in range(2):
                                v_vsc, v_zero, ppi = res[h]
                                if kt == 0 and p > 0 and zev_done[w]:
                                    e.wait_ge(sem_dv, zev_done[w])
                                orow = slice(h * 64, h * 64 + 64)
                                e.matmul(z_ps[orow, w * 512:(w + 1) * 512],
                                         vsc[h][kt % 2][:, :],
                                         pt[h][ppi][:, w * 512:(w + 1) * 512],
                                         start=(kt == 0), stop=(kt == 4 * w + 3),
                                         tile_position=(0, h * 64),
                                         skip_group_check=True).then_inc(sem_pe, 1)
                                pe.n += 1
                                if kt == 4 * w + 3 and h == 1:
                                    zwin_stop[w] = pe.n
                    for h in range(2):
                        _, _, ppi = res[h]
                        vsc_free[h][kt % 2] = pe.n
                        pt_free[h][ppi] = pe.n

                for p in range(n_pairs):
                    prev = None
                    for kt in range(TT):
                        res = emit_S(p, kt)
                        if prev is not None:
                            emit_Z(p, kt - 1, prev)
                        prev = res
                    emit_Z(p, TT - 1, prev)

                    for w in range(4):
                        @blk.vector
                        def _(e, p=p, w=w):
                            e.wait_ge(sem_pe, zwin_stop[w])
                            e.tensor_copy(zfm[p][:, w * 512:(w + 1) * 512],
                                          z_ps[:, w * 512:(w + 1) * 512]
                                          ).then_inc(sem_dv, 1)
                        dv.n += 1
                        zev_done[w] = dv.n

                v_attn_dv = dv.n
                v_attn_pe = pe.n
                actq.flush(blk.scalar)

            with ExitStack() as p3ps:
                op_ps = [p3ps.enter_context(
                    nc.psum_tensor(f"op{i}_{s}", [128, 1024], F32)) for i in range(2)]
                tp2_ps = [p3ps.enter_context(
                    nc.psum_tensor(f"t2{i}_{s}", [128, 128], BF16)) for i in range(2)]
                lg_ps = p3ps.enter_context(
                    nc.psum_tensor(f"lg_{s}", [128, 8], F32))
                tpm_ps = p3ps.enter_context(
                    nc.psum_tensor(f"tpm_{s}", [128, 128], F32))
                ctm_ps = tpm_ps

                op_free = [0, 0]
                xr_free = [0, 0]
                xb_free = [0, 0]
                tp2_free = [0, 0]
                n2t_free = [0, 0, 0, 0]
                n2t_sct = [0, 0, 0, 0]
                lg_free = [0]
                tpm_free = [0]
                ctm_free = [0]
                gstate = {}

                def emit_gating_tail(t):
                    pi = gstate[t]["pi"]
                    v_gm = gstate[t]["v_gm"]

                    def _f(e, pi=pi, v_gm=v_gm):
                        e.wait_ge(sem_dv, v_gm)
                        e.activation(glt[pi][:, :], lgs[pi][:, :], AF.Exp,
                                     bias=gm1[pi][:, :]).then_inc(sem_ac, 1)
                    actq.add(_f)
                    ac.n += 1
                    v_ge = ac.n

                    @blk.tensor
                    def _(e, pi=pi, v_gm=v_gm, tw=tpm_free[0]):
                        e.wait_ge(sem_dv, v_gm)
                        if tw:
                            e.wait_ge(sem_dv, tw)
                        e.transpose(tpm_ps[0:8, 0:128], gt1[pi][:, :], identf[:, :]
                                    ).then_inc(sem_pe, 1)
                    pe.n += 1
                    v_mt = pe.n

                    @blk.vector
                    def _(e, t=t, pi=pi, v_ge=v_ge, v_mt=v_mt, dv0=dv.n):
                        e.wait_ge(sem_ac, v_ge)
                        e.tensor_tensor(glt[pi][:, :], glt[pi][:, :], gt1[pi][:, :],
                                        ALU.mult).then_inc(sem_dv, 1)
                        e.wait_ge(sem_dv, dv0 + 1)
                        e.tensor_reduce(gsum[pi][:, :], glt[pi][:, :], AX.X, ALU.add
                                        ).then_inc(sem_dv, 1)
                        e.wait_ge(sem_dv, dv0 + 2)
                        e.reciprocal(gm1[pi][:, :], gsum[pi][:, :]).then_inc(sem_dv, 1)
                        e.wait_ge(sem_dv, dv0 + 3)
                        e.tensor_scalar(gates[t][:, :], glt[pi][:, :], gm1[pi][:, :],
                                        None, ALU.mult).then_inc(sem_dv, 1)
                        e.wait_ge(sem_dv, dv0 + 4)
                        e.tensor_tensor(oh2[pi][:, :], gt1[pi][:, :], oh1[pi][:, :],
                                        ALU.subtract).then_inc(sem_dv, 1)
                        e.wait_ge(sem_dv, dv0 + 5)
                        e.scalar_tensor_tensor(tf8[pi][:, :], eidx[:, :], 1.0,
                                               oh1[pi][:, :], ALU.mult, ALU.mult,
                                               accum_out=e1f[pi][:, :]).then_inc(sem_dv, 1)
                        e.wait_ge(sem_dv, dv0 + 6)
                        e.scalar_tensor_tensor(tf8[pi][:, :], eidx[:, :], 1.0,
                                               oh2[pi][:, :], ALU.mult, ALU.mult,
                                               accum_out=e2f[pi][:, :]).then_inc(sem_dv, 1)
                        e.wait_ge(sem_dv, dv0 + 7)
                        e.scalar_tensor_tensor(tf8[pi][:, :], gates[t][:, :], 1.0,
                                               oh1[pi][:, :], ALU.mult, ALU.mult,
                                               accum_out=gsl1[t][:, :]).then_inc(sem_dv, 1)
                        e.wait_ge(sem_dv, dv0 + 8)
                        e.scalar_tensor_tensor(tf8[pi][:, :], gates[t][:, :], 1.0,
                                               oh2[pi][:, :], ALU.mult, ALU.mult,
                                               accum_out=gsl2[t][:, :]).then_inc(sem_dv, 1)
                        e.wait_ge(sem_pe, v_mt)
                        e.tensor_tensor_scan(cumf[:, :], tpm_ps[0:8, 0:128],
                                             zero8[:, :],
                                             (0.0 if t == 0 else cuml[:, :]),
                                             ALU.add, ALU.add).then_inc(sem_dv, 1)
                        e.wait_ge(sem_dv, dv0 + 10)
                        e.tensor_copy(cuml[:, :], cumf[:, 127:128]).then_inc(sem_dv, 1)
                    dv.n += 11
                    tpm_free[0] = dv.n
                    v_scan = dv.n

                    @blk.tensor
                    def _(e, v_scan=v_scan, cw=ctm_free[0]):
                        e.wait_ge(sem_dv, v_scan)
                        if cw:
                            e.wait_ge(sem_dv, cw)
                        e.transpose(ctm_ps[0:128, 0:8], cumf[:, :], identf[0:8, 0:8]
                                    ).then_inc(sem_pe, 1)
                    pe.n += 1
                    v_ct = pe.n

                    @blk.vector
                    def _(e, t=t, pi=pi, v_ct=v_ct, dv0=dv.n):
                        e.wait_ge(sem_pe, v_ct)
                        e.scalar_tensor_tensor(tf8[pi][:, :], ctm_ps[0:128, 0:8], 1.0,
                                               oh1[pi][:, :], ALU.mult, ALU.mult,
                                               accum_out=s1f[pi][:, :]).then_inc(sem_dv, 1)
                        e.wait_ge(sem_dv, dv0 + 1)
                        e.scalar_tensor_tensor(tf8[pi][:, :], ctm_ps[0:128, 0:8], 1.0,
                                               oh2[pi][:, :], ALU.mult, ALU.mult,
                                               accum_out=s2f[pi][:, :]).then_inc(sem_dv, 1)
                        e.wait_ge(sem_dv, dv0 + 2)
                        e.tensor_scalar(slf[pi][:, :], s1f[pi][:, :], -1.0,
                                        float(CAPE - 1), ALU.add, ALU.min
                                        ).then_inc(sem_dv, 1)
                        e.wait_ge(sem_dv, dv0 + 3)
                        e.scalar_tensor_tensor(s1f[pi][:, :], e1f[pi][:, :],
                                               float(CAPE), slf[pi][:, :],
                                               ALU.mult, ALU.add).then_inc(sem_dv, 1)
                        e.wait_ge(sem_dv, dv0 + 4)
                        e.tensor_copy(sidx1[t][:, :], s1f[pi][:, :]).then_inc(sem_dv, 1)
                        e.tensor_scalar(slf[pi][:, :], s2f[pi][:, :], -1.0,
                                        float(CAPE - 1), ALU.add, ALU.min
                                        ).then_inc(sem_dv, 1)
                        e.wait_ge(sem_dv, dv0 + 6)
                        e.scalar_tensor_tensor(s2f[pi][:, :], e2f[pi][:, :],
                                               float(CAPE), slf[pi][:, :],
                                               ALU.mult, ALU.add).then_inc(sem_dv, 1)
                        e.wait_ge(sem_dv, dv0 + 7)
                        e.tensor_copy(sidx2[t][:, :], s2f[pi][:, :]).then_inc(sem_dv, 1)
                    dv.n += 8
                    ctm_free[0] = dv.n
                    tpm_free[0] = dv.n
                    v_sidx = dv.n

                    @blk.gpsimd
                    def _(e, t=t, pi=pi, v_sidx=v_sidx):
                        e.wait_ge(sem_dv, v_sidx)
                        e.indirect_dma_start(
                            out=xg_d[:, :],
                            out_offset=bass.IndirectOffsetOnAxis(
                                ap=sidx1[t][:, :], axis=0),
                            in_=n2t[t % 4][:, :], in_offset=None
                            ).then_inc(sem_sct[t % 4], 16)
                        e.indirect_dma_start(
                            out=xg_d[:, :],
                            out_offset=bass.IndirectOffsetOnAxis(
                                ap=sidx2[t][:, :], axis=0),
                            in_=n2t[t % 4][:, :], in_offset=None
                            ).then_inc(sem_sct[t % 4], 16)
                    c_sct[t % 4].n += 32
                    n2t_sct[t % 4] = c_sct[t % 4].n

                xrv = {}

                def add_xr_load(tt, on_sync):
                    wfree = xr_free[tt % 2]
                    if on_sync:
                        @blk.sync
                        def _(e, tt=tt, wfree=wfree):
                            if tt < 2:
                                e.wait_ge(sem_pe, v_attn_pe)
                            if wfree:
                                e.wait_ge(sem_po, wfree)
                            e.dma_start(out=xr[tt % 2][:, :],
                                        in_=x_h[s, tt * 128:(tt + 1) * 128, :]
                                        ).then_inc(sem_x[0], 16)
                        c_x[0].n += 16
                        xrv[tt] = (0, c_x[0].n)
                    else:
                        def _f(e, tt=tt, wfree=wfree):
                            if tt < 2:
                                e.wait_ge(sem_pe, v_attn_pe)
                            if wfree:
                                e.wait_ge(sem_po, wfree)
                            e.dma_start(out=xr[tt % 2][:, :],
                                        in_=x_h[s, tt * 128:(tt + 1) * 128, :]
                                        ).then_inc(sem_x[1], 16)
                        actq.add(_f)
                        c_x[1].n += 16
                        xrv[tt] = (1, c_x[1].n)

                add_xr_load(0, True)
                add_xr_load(1, False)

                for t in range(TT):
                    pi = t % 2
                    si, v_xr = xrv[t]

                    @blk.gpsimd
                    def _(e, pi=pi, si=si, v_xr=v_xr):
                        e.wait_ge(sem_x[si], v_xr)
                        if s == 0 and pi == 0:
                            e.wait_ge(sem_dv, v_bc_done)
                        if xb_free[pi]:
                            e.wait_ge(sem_dv, xb_free[pi])
                        e.tensor_tensor(xb[pi][:, :], xr[pi][:, :], bo_bc[:, :],
                                        ALU.add).then_inc(sem_po, 1)
                    po.n += 1
                    v_xb = po.n
                    xr_free[pi] = po.n

                    @blk.tensor
                    def _(e, t=t, pi=pi):
                        if t == 0:
                            e.wait_ge(sem_dv, v_attn_dv)
                            e.wait_ge(sem_wo, v_wo)
                        if op_free[pi]:
                            e.wait_ge(sem_dv, op_free[pi])
                        for dc in range(n_pairs):
                            for j in range(2):
                                i = e.matmul(op_ps[pi][:, j * 512:(j + 1) * 512],
                                             zfm[dc][:, t * 128:(t + 1) * 128],
                                             wo_bf[:, dc * 1024 + j * 512:
                                                   dc * 1024 + (j + 1) * 512],
                                             start=(dc == 0), stop=(dc == n_pairs - 1))
                        i.then_inc(sem_pe, 1)
                    pe.n += 1
                    v_op = pe.n

                    @blk.vector
                    def _(e, t=t, pi=pi, v_op=v_op, v_xb=v_xb, s=s):
                        if s > 0 and t == 0:
                            e.wait_ge(sem_st, bprev["st"])
                            e.wait_ge(sem_st2, bprev["st2"])
                        e.wait_ge(sem_pe, v_op)
                        e.wait_ge(sem_po, v_xb)
                        e.tensor_tensor(xn[t][:, :], op_ps[pi][:, :], xb[pi][:, :],
                                        ALU.add).then_inc(sem_dv, 1)
                        e.wait_ge(sem_dv, dv.n + 1)
                        e.bn_stats(bnb[pi][:, 0:6], xn[t][:, 0:512])
                        e.bn_stats(bnb[pi][:, 6:12], xn[t][:, 512:1024]
                                   ).then_inc(sem_dv, 1)
                        e.wait_ge(sem_dv, dv.n + 2)
                        e.bn_aggr(bno[pi][:, :], bnb[pi][:, :]).then_inc(sem_dv, 1)
                    dv.n += 3
                    v_ag2 = dv.n
                    op_free[pi] = dv.n
                    xb_free[pi] = dv.n

                    def _f(e, pi=pi, v_ag2=v_ag2, ac0=ac.n):
                        e.wait_ge(sem_dv, v_ag2)
                        e.activation(aab[pi][:, :], bno[pi][:, 1:2], AF.Ln,
                                     bias=cb_eps[:, :]).then_inc(sem_ac, 1)
                        e.wait_ge(sem_ac, ac0 + 1)
                        e.activation(aab[pi][:, :], aab[pi][:, :], AF.Exp,
                                     scale=-0.5).then_inc(sem_ac, 1)
                    actq.add(_f)
                    ac.n += 2
                    v_sc2 = ac.n

                    @blk.vector
                    def _(e, t=t, pi=pi, v_sc2=v_sc2):
                        e.wait_ge(sem_ac, v_sc2)
                        e.tensor_scalar(bbb[pi][:, :], bno[pi][:, 0:1], aab[pi][:, :],
                                        -1.0, ALU.mult, ALU.mult).then_inc(sem_dv, 1)
                        e.wait_ge(sem_dv, dv.n + 1)
                        if n2t_free[t % 4]:
                            e.wait_ge(sem_pe, n2t_free[t % 4])
                        if n2t_sct[t % 4]:
                            e.wait_ge(sem_sct[t % 4], n2t_sct[t % 4])
                        e.tensor_scalar(n2t[t % 4][:, :], xn[t][:, :], aab[pi][:, :],
                                        bbb[pi][:, :], ALU.mult, ALU.add
                                        ).then_inc(sem_dv, 1)
                    dv.n += 2
                    v_n2 = dv.n

                    base_ac4 = ac.n

                    @blk.tensor
                    def _(e, pi=pi, v_n2=v_n2, base_ac4=base_ac4):
                        e.wait_ge(sem_dv, v_n2)
                        for c in range(8):
                            w = tp2_free[c % 2] if c < 2 else base_ac4 + c - 1
                            if w:
                                e.wait_ge(sem_ac, w)
                            e.transpose(tp2_ps[c % 2][:, :],
                                        n2t[t % 4][:, c * 128:(c + 1) * 128],
                                        ident[:, :]).then_inc(sem_pe, 1)
                    base_pe4 = pe.n
                    pe.n += 8
                    n2t_free[t % 4] = pe.n

                    def _f(e, t=t, base_pe4=base_pe4):
                        for c in range(8):
                            e.wait_ge(sem_pe, base_pe4 + c + 1)
                            e.copy(nf[c][:, t * 128:(t + 1) * 128],
                                   tp2_ps[c % 2][:, :]).then_inc(sem_ac, 1)
                    actq.add(_f)
                    ac.n += 8
                    tp2_free[0] = ac.n - 1
                    tp2_free[1] = ac.n
                    v_n2fm = ac.n

                    @blk.tensor
                    def _(e, t=t, pi=pi, v_n2fm=v_n2fm):
                        e.wait_ge(sem_ac, v_n2fm)
                        if t == 0 and s == 0:
                            e.wait_ge(sem_gw, v_gate_ld)
                        if lg_free[0]:
                            e.wait_ge(sem_dv, lg_free[0])
                        for dc in range(8):
                            i = e.matmul(lg_ps[:, :],
                                         nf[dc][:, t * 128:(t + 1) * 128],
                                         gate_bf[:, dc * 8:(dc + 1) * 8],
                                         start=(dc == 0), stop=(dc == 7))
                        i.then_inc(sem_pe, 1)
                    pe.n += 1
                    v_lg = pe.n

                    @blk.vector
                    def _(e, t=t, pi=pi, v_lg=v_lg, dv0=dv.n):
                        e.wait_ge(sem_pe, v_lg)
                        e.tensor_copy(lgs[pi][:, :], lg_ps[:, :]).then_inc(sem_dv, 1)
                        e.wait_ge(sem_dv, dv0 + 1)
                        e.tensor_reduce(gm1[pi][:, :], lgs[pi][:, :], AX.X, ALU.max,
                                        negate=True).then_inc(sem_dv, 1)
                        e.wait_ge(sem_dv, dv0 + 2)
                        e.tensor_scalar(oh1[pi][:, :], lgs[pi][:, :], gm1[pi][:, :],
                                        0.0, ALU.add, ALU.is_ge).then_inc(sem_dv, 1)
                        e.tensor_scalar(glt[pi][:, :], lgs[pi][:, :], gm1[pi][:, :],
                                        0.0, ALU.add, ALU.is_lt).then_inc(sem_dv, 1)
                        e.wait_ge(sem_dv, dv0 + 4)
                        e.tensor_scalar(gt1[pi][:, :], glt[pi][:, :], 1e30, -1e30,
                                        ALU.mult, ALU.add).then_inc(sem_dv, 1)
                        e.wait_ge(sem_dv, dv0 + 5)
                        e.tensor_tensor(gt1[pi][:, :], lgs[pi][:, :], gt1[pi][:, :],
                                        ALU.add).then_inc(sem_dv, 1)
                        e.wait_ge(sem_dv, dv0 + 6)
                        e.tensor_reduce(gm2[pi][:, :], gt1[pi][:, :], AX.X, ALU.max
                                        ).then_inc(sem_dv, 1)
                        e.wait_ge(sem_dv, dv0 + 7)
                        e.tensor_scalar(gt1[pi][:, :], lgs[pi][:, :], gm2[pi][:, :],
                                        None, ALU.is_ge).then_inc(sem_dv, 1)
                    lg_free[0] = dv.n + 1
                    dv.n += 8
                    gstate[t] = dict(pi=pi, v_gm=dv.n, v_n2=v_n2)

                    if t + 2 < TT:
                        add_xr_load(t + 2, t % 2 == 0)

                    if t > 0:
                        emit_gating_tail(t - 1)

                for t in (TT - 1,):
                    emit_gating_tail(t)

                v_p4_dv = dv.n
                v_p4_dv = dv.n
                v_p4_ac = ac.n
                v_p4_pe = pe.n
                v_p4_po = po.n
                actq.flush(blk.scalar)

            with ExitStack() as p5ps:
                ha_ps = [p5ps.enter_context(
                    nc.psum_tensor(f"ha{i}_{s}", [128, 512], F32)) for i in range(2)]
                fb_ps = [p5ps.enter_context(
                    nc.psum_tensor(f"fp{i}_{s}", [128, 1024], F32)) for i in range(2)]

                w_free = [0, 0]
                ha_free = [0, 0]
                silu_free = [0, 0]
                fbps_free = [0, 0]
                eo_sb_free = [0, 0]
                phaseA_last = {}
                xgt_val = {}
                xgf_free = [0, 0]

                def emit_xgT(e_x, wait_pe):
                    par = (e_x - 1) % 2

                    @blk.sync
                    def _(e, e_x=e_x, par=par, wait_pe=wait_pe,
                          vs=[c_sct[q].n for q in range(4)]):
                        e.wait_ge(sem_pe, wait_pe)
                        for q in range(4):
                            e.wait_ge(sem_sct[q], vs[q])
                        for dc in range(8):
                            e.dma_start(
                                out=xg_fm[par][dc][:, 0:CAPE],
                                in_=xg_d[(e_x - 1) * CAPE:e_x * CAPE,
                                         dc * 128:(dc + 1) * 128],
                                transpose=True).then_inc(sem_xgt[par], 16)
                    c_xgt[par].n += 16 * 8
                    xgt_val[e_x] = (par, c_xgt[par].n)

                n_es = n_exp * n_sub
                v_w_arr = [0] * n_es

                def emit_wdma(es2):
                    e2_i, sub2 = divmod(es2, n_sub)
                    wpi2 = es2 % 2

                    @blk.gpsimd
                    def _(e, e2_i=e2_i, sub2=sub2, wpi2=wpi2, es2=es2):
                        if es2 == 0:
                            e.wait_ge(sem_dv, v_p4_dv)
                            e.wait_ge(sem_ac, v_p4_ac)
                            e.wait_ge(sem_pe, v_p4_pe)
                            e.wait_ge(sem_po, v_p4_po)
                            for q in range(4):
                                e.wait_ge(sem_sct[q], c_sct[q].n)
                        if w_free[wpi2]:
                            e.wait_ge(sem_pe, w_free[wpi2])
                        e.dma_start(out=w1b[wpi2][:, :], in_=w1_h[e2_i, sub2, :, :],
                                    max_dma_last_dim=8192).then_inc(sem_wf[wpi2], 16)
                        e.dma_start(out=w2b[wpi2][:, :], in_=w2_h[e2_i, sub2, :, :],
                                    max_dma_last_dim=8192).then_inc(sem_wf[wpi2], 16)
                    c_wf[wpi2].n += 32
                    v_w_arr[es2] = c_wf[wpi2].n

                emit_wdma(0)
                for es in range(n_es):
                    e_i, sub = divmod(es, n_sub)
                    wpi = es % 2
                    if e_i == 0:
                        blocks = [(tb * 512, 512) for tb in range(4)]
                    else:
                        blocks = [(0, 512), (512, CAPE - 512)]
                    ntbk = len(blocks)
                    if es + 1 < n_es:
                        emit_wdma(es + 1)
                    v_w = v_w_arr[es]

                    xpar = (e_i - 1) % 2
                    for tb, (c0, cw) in enumerate(blocks):
                        spi = tb % 2
                        for hc in range(8):
                            hpi = hc % 2

                            @blk.tensor
                            def _(e, e_i=e_i, sub=sub, hc=hc, hpi=hpi, tb=tb,
                                  spi=spi, wpi=wpi, v_w=v_w, xpar=xpar,
                                  c0=c0, cw=cw):
                                if tb == 0 and hc == 0:
                                    e.wait_ge(sem_wf[wpi], v_w)
                                    if e_i >= 1 and sub == 0:
                                        xp, xv = xgt_val[e_i]
                                        e.wait_ge(sem_xgt[xp], xv)
                                if ha_free[hpi]:
                                    e.wait_ge(sem_ac, ha_free[hpi])
                                if hc < 2 and silu_free[spi]:
                                    e.wait_ge(sem_pe, silu_free[spi])
                                for dc in range(8):
                                    rhs = (nf[dc][:, c0:c0 + cw]
                                           if e_i == 0 else
                                           xg_fm[xpar][dc][:, c0:c0 + cw])
                                    i = e.matmul(
                                        ha_ps[hpi][:, 0:cw],
                                        w1b[wpi][:, dc * 1024 + hc * 128:
                                                 dc * 1024 + (hc + 1) * 128],
                                        rhs, start=(dc == 0), stop=(dc == 7))
                                i.then_inc(sem_pe, 1)
                            pe.n += 1
                            v_h = pe.n

                            def _f(e, hc=hc, hpi=hpi, spi=spi, v_h=v_h, cw=cw):
                                e.wait_ge(sem_pe, v_h)
                                e.activation(silu[spi][:, hc * 512:hc * 512 + cw],
                                             ha_ps[hpi][:, 0:cw], AF.Silu
                                             ).then_inc(sem_ac, 1)
                            actq.add(_f)
                            ac.n += 1
                            ha_free[hpi] = ac.n
                        v_silu = ac.n
                        phaseA_last[e_i] = pe.n

                        for t2 in range(cw // 128):
                            t = tb * 4 + t2
                            fpi = t2 % 2

                            @blk.tensor
                            def _(e, t2=t2, fpi=fpi, spi=spi, wpi=wpi, v_silu=v_silu):
                                e.wait_ge(sem_ac, v_silu)
                                if fbps_free[fpi]:
                                    e.wait_ge(sem_dv, fbps_free[fpi])
                                for hc in range(8):
                                    for j in range(2):
                                        i = e.matmul(
                                            fb_ps[fpi][:, j * 512:(j + 1) * 512],
                                            silu[spi][:, hc * 512 + t2 * 128:
                                                      hc * 512 + (t2 + 1) * 128],
                                            w2b[wpi][:, hc * 1024 + j * 512:
                                                     hc * 1024 + (j + 1) * 512],
                                            start=(hc == 0), stop=(hc == 7))
                                i.then_inc(sem_pe, 1)
                            pe.n += 1
                            v_fb = pe.n
                            if t2 == cw // 128 - 1:
                                silu_free[spi] = pe.n
                                if tb == ntbk - 1:
                                    w_free[wpi] = pe.n

                            if e_i == 0:
                                @blk.vector
                                def _(e, t=t, fpi=fpi, v_fb=v_fb):
                                    e.wait_ge(sem_pe, v_fb)
                                    e.tensor_tensor(xn[t][:, :], fb_ps[fpi][:, :],
                                                    xn[t][:, :], ALU.add
                                                    ).then_inc(sem_dv, 1)
                                dv.n += 1
                                fbps_free[fpi] = dv.n
                            else:
                                @blk.vector
                                def _(e, fpi=fpi, v_fb=v_fb, ew=eo_sb_free[fpi]):
                                    e.wait_ge(sem_pe, v_fb)
                                    if ew:
                                        e.wait_ge(sem_eo2[fpi], ew)
                                    e.tensor_copy(eo_sb[fpi][:, :], fb_ps[fpi][:, :]
                                                  ).then_inc(sem_dv, 1)
                                dv.n += 1
                                fbps_free[fpi] = dv.n
                                v_ev = dv.n
                                row0 = (e_i - 1) * CAPE + c0 + t2 * 128

                                @blk.gpsimd
                                def _(e, fpi=fpi, sub=sub, row0=row0, v_ev=v_ev):
                                    e.wait_ge(sem_dv, v_ev)
                                    if sub == 0:
                                        e.dma_start(out=eo_d[row0:row0 + 128, :],
                                                    in_=eo_sb[fpi][:, :]
                                                    ).then_inc(sem_eo2[fpi], 16)
                                    else:
                                        e.dma_start(out=eo_d[row0:row0 + 128, :],
                                                    in_=eo_sb[fpi][:, :],
                                                    accum_op=ALU.add
                                                    ).then_inc(sem_eo2[fpi], 16)
                                c_eo2[fpi].n += 16
                                eo_sb_free[fpi] = c_eo2[fpi].n

                    if e_i == 0 and sub == n_sub - 1 and n_exp > 1:
                        emit_xgT(1, phaseA_last[0])
                    elif e_i >= 1 and sub == 0 and e_i + 1 <= n_exp - 1:
                        emit_xgT(e_i + 1, phaseA_last[e_i - 1])

                v_ffn_dv = dv.n
                v_eo0 = c_eo2[0].n
                v_eo1 = c_eo2[1].n

                geo_free = [0, 0]
                for t in range(TT):
                    pi = t % 2

                    @blk.gpsimd
                    def _(e, t=t, pi=pi):
                        if t == 0:
                            e.wait_ge(sem_eo2[0], v_eo0)
                            e.wait_ge(sem_eo2[1], v_eo1)
                            e.wait_ge(sem_dv, v_ffn_dv)
                        if geo_free[pi]:
                            e.wait_ge(sem_dv, geo_free[pi])
                        e.indirect_dma_start(
                            out=geo[0][pi][:, :], out_offset=None,
                            in_=eo_d[:, :],
                            in_offset=bass.IndirectOffsetOnAxis(
                                ap=sidx1[t][:, :], axis=0)).then_inc(sem_geo[pi], 16)
                        e.indirect_dma_start(
                            out=geo[1][pi][:, :], out_offset=None,
                            in_=eo_d[:, :],
                            in_offset=bass.IndirectOffsetOnAxis(
                                ap=sidx2[t][:, :], axis=0)).then_inc(sem_geo[pi], 16)
                    c_geo[pi].n += 32
                    v_g = c_geo[pi].n

                    @blk.vector
                    def _(e, t=t, pi=pi, v_g=v_g, dv0=dv.n):
                        e.wait_ge(sem_geo[pi], v_g)
                        e.scalar_tensor_tensor(xn[t][:, :], geo[0][pi][:, :],
                                               gsl1[t][:, :], xn[t][:, :],
                                               ALU.mult, ALU.add).then_inc(sem_dv, 1)
                        e.wait_ge(sem_dv, dv0 + 1)
                        e.scalar_tensor_tensor(xn[t][:, :], geo[1][pi][:, :],
                                               gsl2[t][:, :], xn[t][:, :],
                                               ALU.mult, ALU.add).then_inc(sem_dv, 1)
                    dv.n += 2
                    geo_free[pi] = dv.n
                    v_fin = dv.n

                    if t % 2 == 0:
                        @blk.sync
                        def _(e, t=t, v_fin=v_fin):
                            e.wait_ge(sem_dv, v_fin)
                            e.dma_start(out=out_h[s, t * 128:(t + 1) * 128, :],
                                        in_=xn[t][:, :]).then_inc(sem_st, 16)
                        st.n += 16
                    else:
                        def _f(e, t=t, v_fin=v_fin, s=s):
                            e.wait_ge(sem_dv, v_fin)
                            e.dma_start(out=out_h[s, t * 128:(t + 1) * 128, :],
                                        in_=xn[t][:, :]).then_inc(sem_st2, 16)
                        actq.add(_f)
                        st2.n += 16

                actq.flush(blk.scalar)

        @blk.sync
        def _(e):
            e.wait_ge(sem_st, st.n)
            e.wait_ge(sem_st2, st2.n)

    return nc



def _prep_weights(W_qkv, b_qkv, W_o, b_o, sw1, sw2, ew1, ew2, gw):
    w4 = np.asarray(W_qkv, np.float32).reshape(NH, DH, 3, D)
    wq = w4[:, :, 0, :].reshape(D, D)
    wk = w4[:, :, 1, :].reshape(D, D)
    wv = w4[:, :, 2, :].reshape(D, D)
    wqkvT = np.concatenate([wq.T, wk.T, wv.T], axis=1)
    wqkv_pre = np.empty((24, 128, 1024), np.float32)
    for fbk in range(24):
        sl = wqkvT[:, fbk * 128:(fbk + 1) * 128]
        wqkv_pre[fbk] = sl.reshape(8, 128, 128).transpose(1, 0, 2).reshape(128, 1024)

    b4 = np.asarray(b_qkv, np.float32).reshape(NH, DH, 3)
    b_r = np.concatenate([b4[:, :, 0].reshape(-1), b4[:, :, 1].reshape(-1),
                          b4[:, :, 2].reshape(-1)])
    bqkv_pre = np.ascontiguousarray(b_r.reshape(24, 128).T)

    woT = np.asarray(W_o, np.float32).T
    wo_pre = woT.reshape(8, 128, 1024).transpose(1, 0, 2).reshape(128, 8192)

    gwT = np.asarray(gw, np.float32).T
    gw_pre = np.ascontiguousarray(gwT.reshape(8, 128, 8).transpose(1, 0, 2)
                                  .reshape(128, 64))

    w1T = np.concatenate([np.asarray(sw1, np.float32).T[None],
                          np.asarray(ew1, np.float32).transpose(0, 2, 1)], axis=0)
    w2T = np.concatenate([np.asarray(sw2, np.float32).T[None],
                          np.asarray(ew2, np.float32).transpose(0, 2, 1)], axis=0)
    w1_pre = np.empty((NE, NSUB, 128, 8192), np.float32)
    w2_pre = np.empty((NE, NSUB, 128, 8192), np.float32)
    for e in range(NE):
        for sub in range(NSUB):
            sl = w1T[e][:, sub * 1024:(sub + 1) * 1024]
            w1_pre[e, sub] = sl.reshape(8, 128, 1024).transpose(1, 0, 2).reshape(128, 8192)
            sl2 = w2T[e][sub * 1024:(sub + 1) * 1024, :]
            w2_pre[e, sub] = sl2.reshape(8, 128, 1024).transpose(1, 0, 2).reshape(128, 8192)

    return {
        "wqkv": np.ascontiguousarray(wqkv_pre),
        "bqkv": bqkv_pre,
        "wo": np.ascontiguousarray(wo_pre),
        "bo": np.ascontiguousarray(np.asarray(b_o, np.float32).reshape(1, 1024)),
        "gw": gw_pre,
        "w1": np.ascontiguousarray(w1_pre),
        "w2": np.ascontiguousarray(w2_pre),
    }


def kernel(L0_attn_x, L0_attn_W_qkv, L0_attn_b_qkv, L0_attn_W_o, L0_attn_b_o,
           L0_ffn_shared_w1, L0_ffn_shared_w2, L0_ffn_experts_w1,
           L0_ffn_experts_w2, L0_ffn_gate_w):
    x = np.ascontiguousarray(np.asarray(L0_attn_x, dtype=np.float32))
    wmap = _prep_weights(L0_attn_W_qkv, L0_attn_b_qkv, L0_attn_W_o, L0_attn_b_o,
                         L0_ffn_shared_w1, L0_ffn_shared_w2, L0_ffn_experts_w1,
                         L0_ffn_experts_w2, L0_ffn_gate_w)
    nc = build_nc()
    in_maps = []
    for c in range(8):
        m = dict(wmap)
        m["x"] = np.ascontiguousarray(x[2 * c:2 * c + 2])
        in_maps.append(m)
    res = run_bass_kernel_spmd(nc, in_maps, core_ids=list(range(8)))
    out = np.concatenate([res.results[c]["out"] for c in range(8)], axis=0)
    return out.astype(np.float32)


# version 18 (final)
# speedup vs baseline: 1.1199x; 1.0570x over previous
import numpy as np
from contextlib import ExitStack

import concourse.bass as bass
import concourse.mybir as mybir
from concourse.bass_utils import run_bass_kernel_spmd

F32 = mybir.dt.float32
BF16 = mybir.dt.bfloat16
U32 = mybir.dt.uint32
AF = mybir.ActivationFunctionType
ALU = mybir.AluOpType
AX = mybir.AxisListType

D = 1024
S = 2048
NH = 16
DH = 64
NE = 9
NSUB = 4
TT = 16
CAPE = 1024
EPS1 = 1e-5
EPS2 = 1.1920929e-07
KB = 1024


class SC:
    def __init__(self, handle, step=1):
        self.h = handle
        self.n = 0
        self.step = step


class DeferredQ:
    def __init__(self):
        self.q = []

    def add(self, fn):
        self.q.append(fn)

    def flush(self, section_decorator):
        q = self.q
        self.q = []

        def _run(e):
            for fn in q:
                fn(e)
        section_decorator(_run)


def build_nc(n_seq=2, n_pairs=8, n_exp=NE, n_sub=NSUB):
    nc = bass.Bass("TRN2", target_bir_lowering=False)

    x_h = nc.dram_tensor("x", [n_seq, S, D], F32, kind="ExternalInput")
    wqkv_h = nc.dram_tensor("wqkv", [24, 128, 1024], F32, kind="ExternalInput")
    bqkv_h = nc.dram_tensor("bqkv", [128, 24], F32, kind="ExternalInput")
    wo_h = nc.dram_tensor("wo", [128, 8192], F32, kind="ExternalInput")
    bo_h = nc.dram_tensor("bo", [1, 1024], F32, kind="ExternalInput")
    gw_h = nc.dram_tensor("gw", [128, 64], F32, kind="ExternalInput")
    w1_h = nc.dram_tensor("w1", [NE, NSUB, 128, 8192], F32, kind="ExternalInput")
    w2_h = nc.dram_tensor("w2", [NE, NSUB, 128, 8192], F32, kind="ExternalInput")
    out_h = nc.dram_tensor("out", [n_seq, S, D], F32, kind="ExternalOutput")
    xg_d = nc.dram_tensor("xg_d", [8 * CAPE, 1024], BF16)
    eo_d = nc.dram_tensor("eo_d", [8 * CAPE, 1024], BF16)

    cur = [((nc.sbuf_base + 31) // 32) * 32]

    def place(name, shape, dt):
        t = nc.alloc_sbuf_tensor_at(name, shape, dt, offset=cur[0])
        lo, hi = t.manual_sbuf_range
        cur[0] = ((hi + 31) // 32) * 32
        return t

    def place_at(name, shape, dt, off):
        return nc.alloc_sbuf_tensor_at(name, shape, dt, offset=off)

    ident = place("ident", [128, 128], BF16)
    tri = place("tri", [128, 128], BF16)
    bqkv_t = place("bqkv_t", [128, 24], F32)
    bo_bc = place("bo_bc", [128, 1024], F32)
    bo_row = place("bo_row", [1, 1024], F32)
    gate_bf = place("gate_bf", [128, 64], BF16)
    ones_r = place("ones_r", [1, 128], F32)
    bnb = [place(f"bnb{i}", [128, 12], F32) for i in range(2)]
    bno = [place(f"bno{i}", [128, 2], F32) for i in range(2)]
    aab = [place(f"aab{i}", [128, 1], F32) for i in range(2)]
    bbb = [place(f"bbb{i}", [128, 1], F32) for i in range(2)]
    rm4 = [place(f"rm4{h}", [128, 4], F32) for h in range(2)]
    rs4 = [place(f"rs4{h}", [128, 8], F32) for h in range(2)]
    rmin = [place(f"rmin{h}", [128, 1], F32) for h in range(2)]
    rst = [place(f"rst{h}", [128, 1], F32) for h in range(2)]
    rin = [place(f"rin{h}", [128, 1], F32) for h in range(2)]
    vsc = [[place(f"vsc{h}_{i}", [128, 64], BF16) for i in range(2)] for h in range(2)]
    lgs = [place(f"lgs{i}", [128, 8], F32) for i in range(2)]
    glt = [place(f"glt{i}", [128, 8], F32) for i in range(2)]
    gt1 = [place(f"gt1{i}", [128, 8], F32) for i in range(2)]
    gm1 = [place(f"gm1{i}", [128, 1], F32) for i in range(2)]
    gm2 = [place(f"gm2{i}", [128, 1], F32) for i in range(2)]
    gsum = [place(f"gsum{i}", [128, 1], F32) for i in range(2)]
    gates = [place(f"gates{t}", [128, 8], F32) for t in range(TT)]
    eidx = place("eidx", [128, 8], F32)
    oh1 = [place(f"oh1_{i}", [128, 8], F32) for i in range(2)]
    oh2 = [place(f"oh2_{i}", [128, 8], F32) for i in range(2)]
    tf8 = [place(f"tf8_{i}", [128, 8], F32) for i in range(2)]
    e1f = [place(f"e1f{i}", [128, 1], F32) for i in range(2)]
    e2f = [place(f"e2f{i}", [128, 1], F32) for i in range(2)]
    s1f = [place(f"s1f{i}", [128, 1], F32) for i in range(2)]
    s2f = [place(f"s2f{i}", [128, 1], F32) for i in range(2)]
    slf = [place(f"slf{i}", [128, 1], F32) for i in range(2)]
    sidx1 = [place(f"sidx1_{t}", [128, 1], U32) for t in range(TT)]
    sidx2 = [place(f"sidx2_{t}", [128, 1], U32) for t in range(TT)]
    gsl1 = [place(f"gsl1_{t}", [128, 1], F32) for t in range(TT)]
    gsl2 = [place(f"gsl2_{t}", [128, 1], F32) for t in range(TT)]
    cumf = place("cumf", [8, 128], F32)
    cuml = place("cuml", [8, 1], F32)
    zero8 = place("zero8", [8, 128], F32)
    identf = place("identf", [128, 128], F32)
    cb_ln1 = place("cb_ln1", [128, 1], F32)
    cb_eps = place("cb_eps", [128, 1], F32)
    cb_att = place("cb_att", [128, 1], F32)

    NF0 = cur[0]
    nf = [place(f"nf{c}", [128, 2048], BF16) for c in range(8)]
    X0 = cur[0]
    qp = [place_at(f"qp{p}", [128, 2048], BF16, X0 + p * 4 * KB) for p in range(8)]
    vtm = [place_at(f"vtm{t}", [128, 1024], BF16, X0 + 32 * KB + t * 2 * KB)
           for t in range(TT)]
    xn = [place_at(f"xn{t}", [128, 1024], F32, X0 + t * 4 * KB) for t in range(TT)]
    Y0 = X0 + 64 * KB
    kp = [place_at(f"kp{p}", [128, 2048], BF16, Y0 + p * 4 * KB) for p in range(8)]
    zfm = [place_at(f"zfm{p}", [128, 2048], BF16, Y0 + p * 4 * KB) for p in range(8)]
    silu = [place_at(f"silu{i}", [128, 4096], BF16, Y0 + i * 8 * KB) for i in range(2)]
    Z0 = Y0 + 32 * KB
    wo_bf = place_at("wo_bf", [128, 8192], BF16, Z0)
    pt = [[place_at(f"pt{h}_{i}", [128, 2048], BF16,
                    Z0 + 16 * KB + (h * 2 + i) * 4 * KB) for i in range(2)]
          for h in range(2)]
    xs = [place_at(f"xs{i}", [128, 1024], F32, Z0 + 32 * KB + i * 4 * KB)
          for i in range(2)]
    ntb = [place_at(f"ntb{i}", [128, 1024], BF16, Z0 + 40 * KB + i * 2 * KB)
           for i in range(2)]
    wqb = [place_at(f"wqb{i}", [128, 1024], BF16, Z0 + 44 * KB + i * 2 * KB)
           for i in range(2)]
    vscr = place_at("vscr", [128, 2048], BF16, Z0 + 48 * KB)
    xr = [place_at(f"xr{i}", [128, 1024], F32, Z0 + 16 * KB + i * 4 * KB)
          for i in range(2)]
    xb = [place_at(f"xb{i}", [128, 1024], F32, Z0 + 24 * KB + i * 4 * KB)
          for i in range(2)]
    n2t = [place_at(f"n2t{i}", [128, 1024], BF16, Z0 + 52 * KB + i * 2 * KB)
           for i in range(4)]
    w1b = [place_at(f"w1b{i}", [128, 8192], BF16, Z0 + i * 16 * KB) for i in range(2)]
    w2b = [place_at(f"w2b{i}", [128, 8192], BF16, Z0 + 32 * KB + i * 16 * KB)
           for i in range(2)]
    xg_fm = [[place_at(f"xgfm{par}_{dc}", [128, 1024], BF16,
                       NF0 + par * 16 * KB + dc * 2 * KB) for dc in range(8)]
             for par in range(2)]
    geo = [[place_at(f"geo{g}_{i}", [128, 1024], BF16,
                     Y0 + 16 * KB + (g * 2 + i) * 2 * KB) for i in range(2)]
           for g in range(2)]
    eo_sb = [place_at(f"eosb{i}", [128, 1024], BF16, Y0 + 24 * KB + i * 2 * KB)
             for i in range(2)]
    assert Z0 + 64 * KB <= nc.sbuf_top, (Z0 + 64 * KB, nc.sbuf_top)

    with ExitStack() as top:
        sem_sc = top.enter_context(nc.semaphore("sc"))
        sem_gw = top.enter_context(nc.semaphore("gwl"))
        sem_x = [top.enter_context(nc.semaphore(f"x{i}")) for i in range(2)]
        sem_wq = [top.enter_context(nc.semaphore(f"wq{i}")) for i in range(2)]
        sem_wo = top.enter_context(nc.semaphore("wol"))
        sem_wf = [top.enter_context(nc.semaphore(f"wf{i}")) for i in range(2)]
        sem_st = top.enter_context(nc.semaphore("st"))
        sem_st2 = top.enter_context(nc.semaphore("st2"))
        sem_sct = [top.enter_context(nc.semaphore(f"sct{i}")) for i in range(4)]
        sem_eo2 = [top.enter_context(nc.semaphore(f"eo{i}")) for i in range(2)]
        sem_xgt = [top.enter_context(nc.semaphore(f"xgt{i}")) for i in range(2)]
        sem_geo = [top.enter_context(nc.semaphore(f"geo{i}")) for i in range(2)]
        sem_zi = top.enter_context(nc.semaphore("zi"))
        sem_pe = top.enter_context(nc.semaphore("pe"))
        sem_dv = top.enter_context(nc.semaphore("dv"))
        sem_ac = top.enter_context(nc.semaphore("ac"))
        sem_po = top.enter_context(nc.semaphore("po"))
        blk = top.enter_context(nc.Block())

        st = SC(sem_st, 16)
        st2 = SC(sem_st2, 16)
        c_sct = [SC(sem_sct[i], 16) for i in range(4)]
        c_eo2 = [SC(sem_eo2[0], 16), SC(sem_eo2[1], 16)]
        c_xgt = [SC(sem_xgt[0], 16), SC(sem_xgt[1], 16)]
        c_geo = [SC(sem_geo[0], 16), SC(sem_geo[1], 16)]
        c_zi = SC(sem_zi, 16)
        c_x = [SC(sem_x[0], 16), SC(sem_x[1], 16)]
        c_wq = [SC(sem_wq[0], 16), SC(sem_wq[1], 16)]
        c_wo = SC(sem_wo, 16)
        c_wf = [SC(sem_wf[0], 16), SC(sem_wf[1], 16)]
        pe, dv, ac, po = SC(sem_pe), SC(sem_dv), SC(sem_ac), SC(sem_po)

        @blk.sync
        def _(e):
            e.dma_start(out=bqkv_t[:, :], in_=bqkv_h[:, :]).then_inc(sem_sc, 16)
            e.dma_start(out=bo_row[:, :], in_=bo_h[:, :]).then_inc(sem_sc, 16)
        v_bias_ld = 32

        @blk.gpsimd
        def _(e):
            e.dma_start(out=gate_bf[:, :], in_=gw_h[:, :]).then_inc(sem_gw, 16)
            e.memset(ident[:, :], 0.0).then_inc(sem_po, 1)
            e.memset(tri[:, :], 0.0).then_inc(sem_po, 1)
            e.memset(ones_r[:, :], 1.0).then_inc(sem_po, 1)
            e.memset(cb_ln1[:, :], float(EPS2 * EPS1)).then_inc(sem_po, 1)
            e.memset(cb_eps[:, :], float(EPS1)).then_inc(sem_po, 1)
            e.memset(cb_att[:, :], -44.0).then_inc(sem_po, 1)
            e.memset(identf[:, :], 0.0).then_inc(sem_po, 1)
            e.memset(zero8[:, :], 0.0).then_inc(sem_po, 1)
            e.memset(silu[0][:, :], 0.0).then_inc(sem_po, 1)
            e.iota(eidx[:, :], pattern=[[1, 8]], base=0, channel_multiplier=0,
                   allow_small_or_imprecise_dtypes=True).then_inc(sem_po, 1)
            e.wait_ge(sem_po, 10)
            e.affine_select(out=identf[:, :], in_=identf[:, :],
                            compare_op=ALU.not_equal, fill=1.0, base=0,
                            pattern=[[-1, 128]], channel_multiplier=1
                            ).then_inc(sem_po, 1)
            e.wait_ge(sem_po, 11)
            e.affine_select(out=ident[:, :], in_=ident[:, :], compare_op=ALU.not_equal,
                            fill=1.0, base=0, pattern=[[-1, 128]],
                            channel_multiplier=1).then_inc(sem_po, 1)
            e.wait_ge(sem_po, 12)
            e.affine_select(out=tri[:, :], in_=tri[:, :], compare_op=ALU.is_gt,
                            fill=1.0, base=0, pattern=[[-1, 128]],
                            channel_multiplier=1).then_inc(sem_po, 1)
        po.n += 13
        v_gate_ld = 16
        v_consts = po.n

        with nc.psum_tensor("bc_ps", [128, 1024], F32) as bc_ps:
            @blk.tensor
            def _(e):
                e.wait_ge(sem_sc, v_bias_ld)
                e.wait_ge(sem_po, v_consts)
                e.matmul(bc_ps[:, 0:512], ones_r[:, :], bo_row[:, 0:512],
                         start=True, stop=True)
                e.matmul(bc_ps[:, 512:1024], ones_r[:, :], bo_row[:, 512:1024],
                         start=True, stop=True).then_inc(sem_pe, 1)
            pe.n += 1
            v_bc_mm = pe.n

            @blk.sync
            def _(e):
                e.wait_ge(sem_po, v_consts)
                nzc = 8 * CAPE // 512
                for c in range(nzc):
                    e.dma_start(out=xg_d[c * 512:(c + 1) * 512, :],
                                in_=silu[0][:, :]).then_inc(sem_zi, 16)
                for c in range(nzc):
                    e.dma_start(out=eo_d[c * 512:(c + 1) * 512, :],
                                in_=silu[0][:, :]).then_inc(sem_zi, 16)
            c_zi.n += 16 * 2 * (8 * CAPE // 512)
            v_zi = c_zi.n

            @blk.vector
            def _(e):
                e.wait_ge(sem_pe, v_bc_mm)
                e.tensor_copy(bo_bc[:, :], bc_ps[:, :]).then_inc(sem_dv, 1)
            dv.n += 1
        v_bc_done = dv.n

        qfbs = list(range(0, n_pairs)) + list(range(8, 8 + n_pairs)) \
            + list(range(16, 16 + n_pairs))

        actq = DeferredQ()
        for s in range(n_seq):
            if s > 0:
                bprev = dict(pe=pe.n, dv=dv.n, ac=ac.n, po=po.n,
                             st=st.n, st2=st2.n)

                @blk.sync
                def _(e, bprev=bprev):
                    e.wait_ge(sem_pe, bprev["pe"])

                @blk.gpsimd
                def _(e, bprev=bprev):
                    e.wait_ge(sem_pe, bprev["pe"])

                @blk.tensor
                def _(e, bprev=bprev):
                    e.wait_ge(sem_ac, bprev["ac"])

            with ExitStack() as p1ps:
                tp_ps = [p1ps.enter_context(
                    nc.psum_tensor(f"tp{i}_{s}", [128, 128], BF16)) for i in range(2)]
                qkv_ps = [p1ps.enter_context(
                    nc.psum_tensor(f"qk{i}_{s}", [128, 512], F32)) for i in range(4)]

                xs_free = [0, 0]
                ntb_free = [0, 0]
                tp_free = [0, 0]
                nf_ready = [0] * TT
                xv = {}

                def add_xs_load(tt, on_sync, after=None):
                    wfree = xs_free[tt % 2]
                    if on_sync:
                        @blk.sync
                        def _(e, tt=tt, wfree=wfree, after=after):
                            if after:
                                e.wait_ge(sem_x[after[0]], after[1])
                            if wfree:
                                e.wait_ge(sem_dv, wfree)
                            e.dma_start(out=xs[tt % 2][:, :],
                                        in_=x_h[s, tt * 128:(tt + 1) * 128, :]
                                        ).then_inc(sem_x[0], 16)
                        c_x[0].n += 16
                        xv[tt] = (0, c_x[0].n)
                    else:
                        def _f(e, tt=tt, wfree=wfree):
                            if wfree:
                                e.wait_ge(sem_dv, wfree)
                            e.dma_start(out=xs[tt % 2][:, :],
                                        in_=x_h[s, tt * 128:(tt + 1) * 128, :]
                                        ).then_inc(sem_x[1], 16)
                        actq.add(_f)
                        c_x[1].n += 16
                        xv[tt] = (1, c_x[1].n)

                add_xs_load(0, True)
                add_xs_load(1, False)

                for t in range(TT):
                    pi = t % 2
                    si, v_x = xv[t]

                    @blk.vector
                    def _(e, pi=pi, si=si, v_x=v_x):
                        e.wait_ge(sem_x[si], v_x)
                        e.bn_stats(bnb[pi][:, 0:6], xs[pi][:, 0:512])
                        e.bn_stats(bnb[pi][:, 6:12], xs[pi][:, 512:1024]
                                   ).then_inc(sem_dv, 1)
                        e.wait_ge(sem_dv, dv.n + 1)
                        e.bn_aggr(bno[pi][:, :], bnb[pi][:, :]).then_inc(sem_dv, 1)
                    dv.n += 2
                    v_aggr = dv.n

                    def _f(e, t=t, pi=pi, v_aggr=v_aggr, s=s, ac0=ac.n):
                        if t == 0 and s == 0:
                            e.wait_ge(sem_po, v_consts)
                        e.wait_ge(sem_dv, v_aggr)
                        e.activation(aab[pi][:, :], bno[pi][:, 1:2], AF.Ln,
                                     scale=float(1.0 + EPS2),
                                     bias=cb_ln1[:, :]).then_inc(sem_ac, 1)
                        e.wait_ge(sem_ac, ac0 + 1)
                        e.activation(aab[pi][:, :], aab[pi][:, :], AF.Exp,
                                     scale=-0.5).then_inc(sem_ac, 1)
                    actq.add(_f)
                    ac.n += 2
                    v_scale = ac.n

                    @blk.vector
                    def _(e, pi=pi, v_scale=v_scale):
                        e.wait_ge(sem_ac, v_scale)
                        e.tensor_scalar(bbb[pi][:, :], bno[pi][:, 0:1], aab[pi][:, :],
                                        -1.0, ALU.mult, ALU.mult).then_inc(sem_dv, 1)
                        e.wait_ge(sem_dv, dv.n + 1)
                        if ntb_free[pi]:
                            e.wait_ge(sem_pe, ntb_free[pi])
                        e.tensor_scalar(ntb[pi][:, :], xs[pi][:, :], aab[pi][:, :],
                                        bbb[pi][:, :], ALU.mult, ALU.add
                                        ).then_inc(sem_dv, 1)
                    dv.n += 2
                    v_norm = dv.n
                    xs_free[pi] = dv.n

                    base_ac = ac.n

                    @blk.tensor
                    def _(e, t=t, pi=pi, v_norm=v_norm, base_ac=base_ac):
                        e.wait_ge(sem_dv, v_norm)
                        if t == 0 and s == 0:
                            e.wait_ge(sem_po, v_consts)
                        if s > 0 and t == 0:
                            pass
                        for c in range(8):
                            w = tp_free[c % 2] if c < 2 else base_ac + c - 1
                            if w:
                                e.wait_ge(sem_ac, w)
                            e.transpose(tp_ps[c % 2][:, :],
                                        ntb[pi][:, c * 128:(c + 1) * 128],
                                        ident[:, :]).then_inc(sem_pe, 1)
                    base_pe = pe.n
                    pe.n += 8
                    ntb_free[pi] = pe.n

                    def _f(e, t=t, base_pe=base_pe):
                        for c in range(8):
                            e.wait_ge(sem_pe, base_pe + c + 1)
                            e.copy(nf[c][:, t * 128:(t + 1) * 128],
                                   tp_ps[c % 2][:, :]).then_inc(sem_ac, 1)
                    actq.add(_f)
                    ac.n += 8
                    tp_free[0] = ac.n - 1
                    tp_free[1] = ac.n
                    nf_ready[t] = ac.n
                    if t + 2 < TT:
                        add_xs_load(t + 2, t % 2 == 0)

                v_nf_all = ac.n

                wqb_free = [0, 0]
                qps_free = [0, 0, 0, 0]
                vscr_free = 0

                for k, fb in enumerate(qfbs):
                    pi = k % 2

                    @blk.gpsimd
                    def _(e, fb=fb, pi=pi):
                        if wqb_free[pi]:
                            e.wait_ge(sem_pe, wqb_free[pi])
                        e.dma_start(out=wqb[pi][:, :], in_=wqkv_h[fb, :, :]
                                    ).then_inc(sem_wq[pi], 16)
                    c_wq[pi].n += 16
                    v_wq = c_wq[pi].n

                    is_v = fb >= 16
                    if fb < 8:
                        dst = qp[fb]
                    elif fb < 16:
                        dst = kp[fb - 8]
                    else:
                        dst = vscr

                    for tc in range(4):
                        pj = (k * 4 + tc) % 4

                        @blk.tensor
                        def _(e, k=k, fb=fb, pi=pi, tc=tc, pj=pj, v_wq=v_wq):
                            e.wait_ge(sem_wq[pi], v_wq)
                            if k == 0:
                                e.wait_ge(sem_ac, nf_ready[4 * tc + 3])
                            if qps_free[pj]:
                                e.wait_ge(sem_ac, qps_free[pj])
                            for dc in range(8):
                                i = e.matmul(qkv_ps[pj][:, :],
                                             wqb[pi][:, dc * 128:(dc + 1) * 128],
                                             nf[dc][:, tc * 512:(tc + 1) * 512],
                                             start=(dc == 0), stop=(dc == 7))
                            i.then_inc(sem_pe, 1)
                        pe.n += 1
                        v_mm = pe.n
                        if tc == 3:
                            wqb_free[pi] = v_mm

                        def _f(e, k=k, fb=fb, tc=tc, pj=pj, v_mm=v_mm, dst=dst,
                               is_v=is_v, vf=vscr_free, s=s):
                            e.wait_ge(sem_pe, v_mm)
                            if tc == 0:
                                if s > 0 and k == 0:
                                    e.wait_ge(sem_st, bprev["st"])
                                    e.wait_ge(sem_st2, bprev["st2"])
                                e.wait_ge(sem_sc, v_bias_ld)
                                if s == 0 and fb == 8:
                                    e.wait_ge(sem_zi, v_zi)
                                if is_v and vf:
                                    e.wait_ge(sem_pe, vf)
                            e.activation(dst[:, tc * 512:(tc + 1) * 512],
                                         qkv_ps[pj][:, :], AF.Identity,
                                         bias=bqkv_t[:, fb:fb + 1]).then_inc(sem_ac, 1)
                        actq.add(_f)
                        ac.n += 1
                        qps_free[pj] = ac.n

                    if is_v:
                        v_evac = ac.n
                        col = (fb - 16) * 128
                        base_ac2 = ac.n

                        @blk.tensor
                        def _(e, v_evac=v_evac, base_ac2=base_ac2):
                            e.wait_ge(sem_ac, v_evac)
                            for t in range(TT):
                                w = tp_free[t % 2] if t < 2 else base_ac2 + t - 1
                                if w:
                                    e.wait_ge(sem_ac, w)
                                e.transpose(tp_ps[t % 2][:, :],
                                            vscr[:, t * 128:(t + 1) * 128],
                                            ident[:, :]).then_inc(sem_pe, 1)
                        base_pe2 = pe.n
                        pe.n += TT
                        vscr_free = pe.n

                        def _f(e, col=col, base_pe2=base_pe2):
                            for t in range(TT):
                                e.wait_ge(sem_pe, base_pe2 + t + 1)
                                e.copy(vtm[t][:, col:col + 128], tp_ps[t % 2][:, :]
                                       ).then_inc(sem_ac, 1)
                        actq.add(_f)
                        ac.n += TT
                        tp_free[0] = ac.n - 1
                        tp_free[1] = ac.n

                v_qkv_ac = ac.n

                @blk.gpsimd
                def _(e):
                    if wqb_free[0]:
                        e.wait_ge(sem_pe, wqb_free[0])
                    e.dma_start(out=wo_bf[:, :], in_=wo_h[:, :]).then_inc(sem_wo, 16)
                c_wo.n += 16
                v_wo = c_wo.n
                actq.flush(blk.scalar)

            with ExitStack() as p2ps:
                s_ps = [p2ps.enter_context(
                    nc.psum_tensor(f"sp{i}_{s}", [128, 512], F32)) for i in range(4)]
                z_ps = p2ps.enter_context(nc.psum_tensor(f"zp_{s}", [128, 2048], F32))

                pt_free = [[0, 0], [0, 0]]
                rs4_free = [[0, 0], [0, 0]]
                sps_exp = [0, 0, 0, 0]
                vsc_free = [[0, 0], [0, 0]]
                zwin_stop = [0] * 4
                zev_done = [0] * 4
                scnt = [0]

                def emit_S(p, kt):
                    k0 = kt * 128
                    w0 = k0 // 512
                    qa = w0 * 512
                    nch = 4 - w0
                    res = {}
                    ac_base = ac.n
                    v_p2 = {}
                    tiles = {}

                    @blk.tensor
                    def _(e, p=p, kt=kt, k0=k0, qa=qa, nch=nch, ac_base=ac_base):
                        local_exp = {}
                        if p == 0 and kt == 0:
                            e.wait_ge(sem_ac, v_qkv_ac)
                        for c in range(nch):
                            for h in range(2):
                                rows = slice(h * 64, h * 64 + 64)
                                j = scnt[0] % 4
                                scnt[0] += 1
                                tiles[(h, c)] = j
                                w = local_exp.get(j, sps_exp[j])
                                if w:
                                    e.wait_ge(sem_ac, w)
                                local_exp[j] = ac_base + 2 * c + h + 1
                                qc0 = qa + c * 512
                                e.matmul(s_ps[j][:, :],
                                         kp[p][rows, k0:k0 + 128],
                                         qp[p][rows, qc0:qc0 + 512],
                                         start=True, stop=True).then_inc(sem_pe, 1)
                                pe.n += 1
                                v_p2[(h, c)] = pe.n

                    ppi = kt % 2
                    ptw = {h: pt_free[h][ppi] for h in range(2)}
                    rs4w = {h: rs4_free[h][ppi] for h in range(2)}

                    def _f(e, kt=kt, k0=k0, qa=qa, nch=nch, ppi=ppi,
                           v_p2=dict(v_p2), tiles=dict(tiles), ptw=ptw, rs4w=rs4w):
                        for c in range(nch):
                            for h in range(2):
                                e.wait_ge(sem_pe, v_p2[(h, c)])
                                if c == 0:
                                    if ptw[h]:
                                        e.wait_ge(sem_pe, ptw[h])
                                    if rs4w[h]:
                                        e.wait_ge(sem_dv, rs4w[h])
                                    off = k0 - qa
                                    e.activation(pt[h][ppi][:, k0:qa + 512],
                                                 s_ps[tiles[(h, c)]][:, off:512],
                                                 AF.Exp,
                                                 bias=cb_att[:, :]).then_inc(sem_ac, 1)
                                else:
                                    e.activation(
                                        pt[h][ppi][:, qa + c * 512:qa + (c + 1) * 512],
                                        s_ps[tiles[(h, c)]][:, :], AF.Exp,
                                        bias=cb_att[:, :],
                                        accum_out=rs4[h][:, 4 * ppi + c:4 * ppi + c + 1]
                                        ).then_inc(sem_ac, 1)
                    actq.add(_f)
                    for (h, c), j in tiles.items():
                        sps_exp[j] = ac_base + 2 * c + h + 1
                    ac.n += 2 * nch

                    ppi2 = kt % 2
                    v_zero = {}
                    for h in range(2):
                        v_exp0 = ac_base + h + 1
                        v_zero[h] = 0
                        if qa < k0:
                            @blk.gpsimd
                            def _(e, h=h, qa=qa, k0=k0, ppi=ppi2, v_exp0=v_exp0):
                                e.wait_ge(sem_ac, v_exp0)
                                e.memset(pt[h][ppi][:, qa:k0], 0.0).then_inc(sem_po, 1)
                            po.n += 1
                            v_zero[h] = po.n

                    @blk.vector
                    def _(e, p=p, kt=kt, k0=k0, qa=qa, ppi=ppi2,
                          v_zero=dict(v_zero), ac_base=ac_base, dv0=dv.n):
                        for h in range(2):
                            e.wait_ge(sem_ac, ac_base + h + 1)
                            e.tensor_tensor(pt[h][ppi][:, k0:k0 + 128],
                                            pt[h][ppi][:, k0:k0 + 128], tri[:, :],
                                            ALU.mult).then_inc(sem_dv, 1)
                        for h in range(2):
                            e.wait_ge(sem_dv, dv0 + 1 + h)
                            if v_zero[h]:
                                e.wait_ge(sem_po, v_zero[h])
                            e.tensor_reduce(rs4[h][:, 4 * ppi:4 * ppi + 1],
                                            pt[h][ppi][:, qa:qa + 512], AX.X,
                                            ALU.add).then_inc(sem_dv, 1)
                    dv_red = {0: dv.n + 3, 1: dv.n + 4}
                    dv.n += 4

                    for h in range(2):
                        v_exp = ac_base + 2 * (nch - 1) + h + 1

                        @blk.vector
                        def _(e, h=h, p=p, kt=kt, nch=nch, ppi=ppi2,
                              v_exp=v_exp, dv1=dv_red[h], dv0=dv.n):
                            e.wait_ge(sem_ac, v_exp)
                            e.wait_ge(sem_dv, dv1)
                            e.tensor_reduce(rst[h][:, :],
                                            rs4[h][:, 4 * ppi:4 * ppi + nch], AX.X,
                                            ALU.add).then_inc(sem_dv, 1)
                            e.wait_ge(sem_dv, dv0 + 1)
                            e.reciprocal(rin[h][:, :], rst[h][:, :]
                                         ).then_inc(sem_dv, 1)
                            e.wait_ge(sem_dv, dv0 + 2)
                            if vsc_free[h][kt % 2]:
                                e.wait_ge(sem_pe, vsc_free[h][kt % 2])
                            e.tensor_scalar(
                                vsc[h][kt % 2][:, :],
                                vtm[kt][:, (2 * p + h) * 64:(2 * p + h) * 64 + 64],
                                rin[h][:, :], None, ALU.mult).then_inc(sem_dv, 1)
                        dv.n += 3
                        rs4_free[h][ppi2] = dv.n - 2
                        res[h] = (dv.n, v_zero[h], ppi2)
                    return res

                def emit_Z(p, kt, res):
                    k0 = kt * 128
                    w0 = k0 // 512
                    nw = 4 - w0
                    order = []
                    ai = bi = 0
                    for _i in range(2 * nw):
                        if ai < nw and ai - bi < 2:
                            order.append((0, w0 + ai))
                            ai += 1
                        else:
                            order.append((1, w0 + bi))
                            bi += 1

                    @blk.tensor
                    def _(e, p=p, kt=kt, w0=w0, res=dict(res), order=list(order)):
                        seen = set()
                        for h, w in order:
                            v_vsc, v_zero, ppi = res[h]
                            if h not in seen:
                                seen.add(h)
                                e.wait_ge(sem_dv, v_vsc)
                                if v_zero:
                                    e.wait_ge(sem_po, v_zero)
                            if kt == 0 and p > 0 and zev_done[w]:
                                e.wait_ge(sem_dv, zev_done[w])
                            orow = slice(h * 64, h * 64 + 64)
                            e.matmul(z_ps[orow, w * 512:(w + 1) * 512],
                                     vsc[h][kt % 2][:, :],
                                     pt[h][ppi][:, w * 512:(w + 1) * 512],
                                     start=(kt == 0), stop=(kt == 4 * w + 3),
                                     tile_position=(0, h * 64),
                                     skip_group_check=True).then_inc(sem_pe, 1)
                            pe.n += 1
                            if kt == 4 * w + 3 and h == 1:
                                zwin_stop[w] = pe.n
                    for h in range(2):
                        _, _, ppi = res[h]
                        vsc_free[h][kt % 2] = pe.n
                        pt_free[h][ppi] = pe.n

                for p in range(n_pairs):
                    prev = None
                    for kt in range(TT):
                        res = emit_S(p, kt)
                        if prev is not None:
                            emit_Z(p, kt - 1, prev)
                        prev = res
                    emit_Z(p, TT - 1, prev)

                    for w in range(4):
                        @blk.vector
                        def _(e, p=p, w=w):
                            e.wait_ge(sem_pe, zwin_stop[w])
                            e.tensor_copy(zfm[p][:, w * 512:(w + 1) * 512],
                                          z_ps[:, w * 512:(w + 1) * 512]
                                          ).then_inc(sem_dv, 1)
                        dv.n += 1
                        zev_done[w] = dv.n

                v_attn_dv = dv.n
                v_attn_pe = pe.n
                actq.flush(blk.scalar)

            with ExitStack() as p3ps:
                op_ps = [p3ps.enter_context(
                    nc.psum_tensor(f"op{i}_{s}", [128, 1024], F32)) for i in range(2)]
                tp2_ps = [p3ps.enter_context(
                    nc.psum_tensor(f"t2{i}_{s}", [128, 128], BF16)) for i in range(2)]
                lg_ps = p3ps.enter_context(
                    nc.psum_tensor(f"lg_{s}", [128, 8], F32))
                tpm_ps = p3ps.enter_context(
                    nc.psum_tensor(f"tpm_{s}", [128, 128], F32))
                ctm_ps = tpm_ps

                op_free = [0, 0]
                xr_free = [0, 0]
                xb_free = [0, 0]
                tp2_free = [0, 0]
                n2t_free = [0, 0, 0, 0]
                n2t_sct = [0, 0, 0, 0]
                lg_free = [0]
                tpm_free = [0]
                ctm_free = [0]
                gstate = {}

                def emit_gating_tail(t):
                    pi = gstate[t]["pi"]
                    v_gm = gstate[t]["v_gm"]

                    def _f(e, pi=pi, v_gm=v_gm):
                        e.wait_ge(sem_dv, v_gm)
                        e.activation(glt[pi][:, :], lgs[pi][:, :], AF.Exp,
                                     bias=gm1[pi][:, :]).then_inc(sem_ac, 1)
                    actq.add(_f)
                    ac.n += 1
                    v_ge = ac.n

                    @blk.tensor
                    def _(e, pi=pi, v_gm=v_gm, tw=tpm_free[0]):
                        e.wait_ge(sem_dv, v_gm)
                        if tw:
                            e.wait_ge(sem_dv, tw)
                        e.transpose(tpm_ps[0:8, 0:128], gt1[pi][:, :], identf[:, :]
                                    ).then_inc(sem_pe, 1)
                    pe.n += 1
                    v_mt = pe.n

                    @blk.vector
                    def _(e, t=t, pi=pi, v_ge=v_ge, v_mt=v_mt, dv0=dv.n):
                        e.wait_ge(sem_ac, v_ge)
                        e.tensor_tensor(glt[pi][:, :], glt[pi][:, :], gt1[pi][:, :],
                                        ALU.mult).then_inc(sem_dv, 1)
                        e.wait_ge(sem_dv, dv0 + 1)
                        e.tensor_reduce(gsum[pi][:, :], glt[pi][:, :], AX.X, ALU.add
                                        ).then_inc(sem_dv, 1)
                        e.wait_ge(sem_dv, dv0 + 2)
                        e.reciprocal(gm1[pi][:, :], gsum[pi][:, :]).then_inc(sem_dv, 1)
                        e.wait_ge(sem_dv, dv0 + 3)
                        e.tensor_scalar(gates[t][:, :], glt[pi][:, :], gm1[pi][:, :],
                                        None, ALU.mult).then_inc(sem_dv, 1)
                        e.wait_ge(sem_dv, dv0 + 4)
                        e.tensor_tensor(oh2[pi][:, :], gt1[pi][:, :], oh1[pi][:, :],
                                        ALU.subtract).then_inc(sem_dv, 1)
                        e.wait_ge(sem_dv, dv0 + 5)
                        e.scalar_tensor_tensor(tf8[pi][:, :], eidx[:, :], 1.0,
                                               oh1[pi][:, :], ALU.mult, ALU.mult,
                                               accum_out=e1f[pi][:, :]).then_inc(sem_dv, 1)
                        e.wait_ge(sem_dv, dv0 + 6)
                        e.scalar_tensor_tensor(tf8[pi][:, :], eidx[:, :], 1.0,
                                               oh2[pi][:, :], ALU.mult, ALU.mult,
                                               accum_out=e2f[pi][:, :]).then_inc(sem_dv, 1)
                        e.wait_ge(sem_dv, dv0 + 7)
                        e.scalar_tensor_tensor(tf8[pi][:, :], gates[t][:, :], 1.0,
                                               oh1[pi][:, :], ALU.mult, ALU.mult,
                                               accum_out=gsl1[t][:, :]).then_inc(sem_dv, 1)
                        e.wait_ge(sem_dv, dv0 + 8)
                        e.scalar_tensor_tensor(tf8[pi][:, :], gates[t][:, :], 1.0,
                                               oh2[pi][:, :], ALU.mult, ALU.mult,
                                               accum_out=gsl2[t][:, :]).then_inc(sem_dv, 1)
                        e.wait_ge(sem_pe, v_mt)
                        e.tensor_tensor_scan(cumf[:, :], tpm_ps[0:8, 0:128],
                                             zero8[:, :],
                                             (0.0 if t == 0 else cuml[:, :]),
                                             ALU.add, ALU.add).then_inc(sem_dv, 1)
                        e.wait_ge(sem_dv, dv0 + 10)
                        e.tensor_copy(cuml[:, :], cumf[:, 127:128]).then_inc(sem_dv, 1)
                    dv.n += 11
                    tpm_free[0] = dv.n
                    v_scan = dv.n

                    @blk.tensor
                    def _(e, v_scan=v_scan, cw=ctm_free[0]):
                        e.wait_ge(sem_dv, v_scan)
                        if cw:
                            e.wait_ge(sem_dv, cw)
                        e.transpose(ctm_ps[0:128, 0:8], cumf[:, :], identf[0:8, 0:8]
                                    ).then_inc(sem_pe, 1)
                    pe.n += 1
                    v_ct = pe.n

                    @blk.vector
                    def _(e, t=t, pi=pi, v_ct=v_ct, dv0=dv.n):
                        e.wait_ge(sem_pe, v_ct)
                        e.scalar_tensor_tensor(tf8[pi][:, :], ctm_ps[0:128, 0:8], 1.0,
                                               oh1[pi][:, :], ALU.mult, ALU.mult,
                                               accum_out=s1f[pi][:, :]).then_inc(sem_dv, 1)
                        e.wait_ge(sem_dv, dv0 + 1)
                        e.scalar_tensor_tensor(tf8[pi][:, :], ctm_ps[0:128, 0:8], 1.0,
                                               oh2[pi][:, :], ALU.mult, ALU.mult,
                                               accum_out=s2f[pi][:, :]).then_inc(sem_dv, 1)
                        e.wait_ge(sem_dv, dv0 + 2)
                        e.tensor_scalar(slf[pi][:, :], s1f[pi][:, :], -1.0,
                                        float(CAPE - 1), ALU.add, ALU.min
                                        ).then_inc(sem_dv, 1)
                        e.wait_ge(sem_dv, dv0 + 3)
                        e.scalar_tensor_tensor(s1f[pi][:, :], e1f[pi][:, :],
                                               float(CAPE), slf[pi][:, :],
                                               ALU.mult, ALU.add).then_inc(sem_dv, 1)
                        e.wait_ge(sem_dv, dv0 + 4)
                        e.tensor_copy(sidx1[t][:, :], s1f[pi][:, :]).then_inc(sem_dv, 1)
                        e.tensor_scalar(slf[pi][:, :], s2f[pi][:, :], -1.0,
                                        float(CAPE - 1), ALU.add, ALU.min
                                        ).then_inc(sem_dv, 1)
                        e.wait_ge(sem_dv, dv0 + 6)
                        e.scalar_tensor_tensor(s2f[pi][:, :], e2f[pi][:, :],
                                               float(CAPE), slf[pi][:, :],
                                               ALU.mult, ALU.add).then_inc(sem_dv, 1)
                        e.wait_ge(sem_dv, dv0 + 7)
                        e.tensor_copy(sidx2[t][:, :], s2f[pi][:, :]).then_inc(sem_dv, 1)
                    dv.n += 8
                    ctm_free[0] = dv.n
                    tpm_free[0] = dv.n
                    v_sidx = dv.n

                    @blk.gpsimd
                    def _(e, t=t, pi=pi, v_sidx=v_sidx):
                        e.wait_ge(sem_dv, v_sidx)
                        e.indirect_dma_start(
                            out=xg_d[:, :],
                            out_offset=bass.IndirectOffsetOnAxis(
                                ap=sidx1[t][:, :], axis=0),
                            in_=n2t[t % 4][:, :], in_offset=None
                            ).then_inc(sem_sct[t % 4], 16)
                        e.indirect_dma_start(
                            out=xg_d[:, :],
                            out_offset=bass.IndirectOffsetOnAxis(
                                ap=sidx2[t][:, :], axis=0),
                            in_=n2t[t % 4][:, :], in_offset=None
                            ).then_inc(sem_sct[t % 4], 16)
                    c_sct[t % 4].n += 32
                    n2t_sct[t % 4] = c_sct[t % 4].n

                xrv = {}

                def add_xr_load(tt, on_sync):
                    wfree = xr_free[tt % 2]
                    if on_sync:
                        @blk.sync
                        def _(e, tt=tt, wfree=wfree):
                            if tt < 2:
                                e.wait_ge(sem_pe, v_attn_pe)
                            if wfree:
                                e.wait_ge(sem_po, wfree)
                            e.dma_start(out=xr[tt % 2][:, :],
                                        in_=x_h[s, tt * 128:(tt + 1) * 128, :]
                                        ).then_inc(sem_x[0], 16)
                        c_x[0].n += 16
                        xrv[tt] = (0, c_x[0].n)
                    else:
                        def _f(e, tt=tt, wfree=wfree):
                            if tt < 2:
                                e.wait_ge(sem_pe, v_attn_pe)
                            if wfree:
                                e.wait_ge(sem_po, wfree)
                            e.dma_start(out=xr[tt % 2][:, :],
                                        in_=x_h[s, tt * 128:(tt + 1) * 128, :]
                                        ).then_inc(sem_x[1], 16)
                        actq.add(_f)
                        c_x[1].n += 16
                        xrv[tt] = (1, c_x[1].n)

                add_xr_load(0, True)
                add_xr_load(1, False)

                for t in range(TT):
                    pi = t % 2
                    si, v_xr = xrv[t]

                    @blk.gpsimd
                    def _(e, pi=pi, si=si, v_xr=v_xr):
                        e.wait_ge(sem_x[si], v_xr)
                        if s == 0 and pi == 0:
                            e.wait_ge(sem_dv, v_bc_done)
                        if xb_free[pi]:
                            e.wait_ge(sem_dv, xb_free[pi])
                        e.tensor_tensor(xb[pi][:, :], xr[pi][:, :], bo_bc[:, :],
                                        ALU.add).then_inc(sem_po, 1)
                    po.n += 1
                    v_xb = po.n
                    xr_free[pi] = po.n

                    @blk.tensor
                    def _(e, t=t, pi=pi):
                        if t == 0:
                            e.wait_ge(sem_dv, v_attn_dv)
                            e.wait_ge(sem_wo, v_wo)
                        if op_free[pi]:
                            e.wait_ge(sem_dv, op_free[pi])
                        for dc in range(n_pairs):
                            for j in range(2):
                                i = e.matmul(op_ps[pi][:, j * 512:(j + 1) * 512],
                                             zfm[dc][:, t * 128:(t + 1) * 128],
                                             wo_bf[:, dc * 1024 + j * 512:
                                                   dc * 1024 + (j + 1) * 512],
                                             start=(dc == 0), stop=(dc == n_pairs - 1))
                        i.then_inc(sem_pe, 1)
                    pe.n += 1
                    v_op = pe.n

                    @blk.vector
                    def _(e, t=t, pi=pi, v_op=v_op, v_xb=v_xb, s=s):
                        if s > 0 and t == 0:
                            e.wait_ge(sem_st, bprev["st"])
                            e.wait_ge(sem_st2, bprev["st2"])
                        e.wait_ge(sem_pe, v_op)
                        e.wait_ge(sem_po, v_xb)
                        e.tensor_tensor(xn[t][:, :], op_ps[pi][:, :], xb[pi][:, :],
                                        ALU.add).then_inc(sem_dv, 1)
                        e.wait_ge(sem_dv, dv.n + 1)
                        e.bn_stats(bnb[pi][:, 0:6], xn[t][:, 0:512])
                        e.bn_stats(bnb[pi][:, 6:12], xn[t][:, 512:1024]
                                   ).then_inc(sem_dv, 1)
                        e.wait_ge(sem_dv, dv.n + 2)
                        e.bn_aggr(bno[pi][:, :], bnb[pi][:, :]).then_inc(sem_dv, 1)
                    dv.n += 3
                    v_ag2 = dv.n
                    op_free[pi] = dv.n
                    xb_free[pi] = dv.n

                    def _f(e, pi=pi, v_ag2=v_ag2, ac0=ac.n):
                        e.wait_ge(sem_dv, v_ag2)
                        e.activation(aab[pi][:, :], bno[pi][:, 1:2], AF.Ln,
                                     bias=cb_eps[:, :]).then_inc(sem_ac, 1)
                        e.wait_ge(sem_ac, ac0 + 1)
                        e.activation(aab[pi][:, :], aab[pi][:, :], AF.Exp,
                                     scale=-0.5).then_inc(sem_ac, 1)
                    actq.add(_f)
                    ac.n += 2
                    v_sc2 = ac.n

                    @blk.vector
                    def _(e, t=t, pi=pi, v_sc2=v_sc2):
                        e.wait_ge(sem_ac, v_sc2)
                        e.tensor_scalar(bbb[pi][:, :], bno[pi][:, 0:1], aab[pi][:, :],
                                        -1.0, ALU.mult, ALU.mult).then_inc(sem_dv, 1)
                        e.wait_ge(sem_dv, dv.n + 1)
                        if n2t_free[t % 4]:
                            e.wait_ge(sem_pe, n2t_free[t % 4])
                        if n2t_sct[t % 4]:
                            e.wait_ge(sem_sct[t % 4], n2t_sct[t % 4])
                        e.tensor_scalar(n2t[t % 4][:, :], xn[t][:, :], aab[pi][:, :],
                                        bbb[pi][:, :], ALU.mult, ALU.add
                                        ).then_inc(sem_dv, 1)
                    dv.n += 2
                    v_n2 = dv.n

                    base_ac4 = ac.n

                    @blk.tensor
                    def _(e, pi=pi, v_n2=v_n2, base_ac4=base_ac4):
                        e.wait_ge(sem_dv, v_n2)
                        for c in range(8):
                            w = tp2_free[c % 2] if c < 2 else base_ac4 + c - 1
                            if w:
                                e.wait_ge(sem_ac, w)
                            e.transpose(tp2_ps[c % 2][:, :],
                                        n2t[t % 4][:, c * 128:(c + 1) * 128],
                                        ident[:, :]).then_inc(sem_pe, 1)
                    base_pe4 = pe.n
                    pe.n += 8
                    n2t_free[t % 4] = pe.n

                    def _f(e, t=t, base_pe4=base_pe4):
                        for c in range(8):
                            e.wait_ge(sem_pe, base_pe4 + c + 1)
                            e.copy(nf[c][:, t * 128:(t + 1) * 128],
                                   tp2_ps[c % 2][:, :]).then_inc(sem_ac, 1)
                    actq.add(_f)
                    ac.n += 8
                    tp2_free[0] = ac.n - 1
                    tp2_free[1] = ac.n
                    v_n2fm = ac.n

                    @blk.tensor
                    def _(e, t=t, pi=pi, v_n2fm=v_n2fm):
                        e.wait_ge(sem_ac, v_n2fm)
                        if t == 0 and s == 0:
                            e.wait_ge(sem_gw, v_gate_ld)
                        if lg_free[0]:
                            e.wait_ge(sem_dv, lg_free[0])
                        for dc in range(8):
                            i = e.matmul(lg_ps[:, :],
                                         nf[dc][:, t * 128:(t + 1) * 128],
                                         gate_bf[:, dc * 8:(dc + 1) * 8],
                                         start=(dc == 0), stop=(dc == 7))
                        i.then_inc(sem_pe, 1)
                    pe.n += 1
                    v_lg = pe.n

                    @blk.vector
                    def _(e, t=t, pi=pi, v_lg=v_lg, dv0=dv.n):
                        e.wait_ge(sem_pe, v_lg)
                        e.tensor_copy(lgs[pi][:, :], lg_ps[:, :]).then_inc(sem_dv, 1)
                        e.wait_ge(sem_dv, dv0 + 1)
                        e.tensor_reduce(gm1[pi][:, :], lgs[pi][:, :], AX.X, ALU.max,
                                        negate=True).then_inc(sem_dv, 1)
                        e.wait_ge(sem_dv, dv0 + 2)
                        e.tensor_scalar(oh1[pi][:, :], lgs[pi][:, :], gm1[pi][:, :],
                                        0.0, ALU.add, ALU.is_ge).then_inc(sem_dv, 1)
                        e.tensor_scalar(glt[pi][:, :], lgs[pi][:, :], gm1[pi][:, :],
                                        0.0, ALU.add, ALU.is_lt).then_inc(sem_dv, 1)
                        e.wait_ge(sem_dv, dv0 + 4)
                        e.tensor_scalar(gt1[pi][:, :], glt[pi][:, :], 1e30, -1e30,
                                        ALU.mult, ALU.add).then_inc(sem_dv, 1)
                        e.wait_ge(sem_dv, dv0 + 5)
                        e.tensor_tensor(gt1[pi][:, :], lgs[pi][:, :], gt1[pi][:, :],
                                        ALU.add).then_inc(sem_dv, 1)
                        e.wait_ge(sem_dv, dv0 + 6)
                        e.tensor_reduce(gm2[pi][:, :], gt1[pi][:, :], AX.X, ALU.max
                                        ).then_inc(sem_dv, 1)
                        e.wait_ge(sem_dv, dv0 + 7)
                        e.tensor_scalar(gt1[pi][:, :], lgs[pi][:, :], gm2[pi][:, :],
                                        None, ALU.is_ge).then_inc(sem_dv, 1)
                    lg_free[0] = dv.n + 1
                    dv.n += 8
                    gstate[t] = dict(pi=pi, v_gm=dv.n, v_n2=v_n2)

                    if t + 2 < TT:
                        add_xr_load(t + 2, t % 2 == 0)

                    if t > 0:
                        emit_gating_tail(t - 1)

                for t in (TT - 1,):
                    emit_gating_tail(t)

                v_p4_dv = dv.n
                v_p4_dv = dv.n
                v_p4_ac = ac.n
                v_p4_pe = pe.n
                v_p4_po = po.n
                actq.flush(blk.scalar)

            with ExitStack() as p5ps:
                ha_ps = [p5ps.enter_context(
                    nc.psum_tensor(f"ha{i}_{s}", [128, 512], F32)) for i in range(2)]
                fb_ps = [p5ps.enter_context(
                    nc.psum_tensor(f"fp{i}_{s}", [128, 1024], F32)) for i in range(2)]

                w_free = [0, 0]
                ha_free = [0, 0]
                silu_free = [0, 0]
                fbps_free = [0, 0]
                eo_sb_free = [0, 0]
                phaseA_last = {}
                xgt_val = {}
                xgf_free = [0, 0]

                def emit_xgT(e_x, wait_pe):
                    par = (e_x - 1) % 2

                    @blk.sync
                    def _(e, e_x=e_x, par=par, wait_pe=wait_pe,
                          vs=[c_sct[q].n for q in range(4)]):
                        e.wait_ge(sem_pe, wait_pe)
                        for q in range(4):
                            e.wait_ge(sem_sct[q], vs[q])
                        for dc in range(8):
                            e.dma_start(
                                out=xg_fm[par][dc][:, 0:CAPE],
                                in_=xg_d[(e_x - 1) * CAPE:e_x * CAPE,
                                         dc * 128:(dc + 1) * 128],
                                transpose=True).then_inc(sem_xgt[par], 16)
                    c_xgt[par].n += 16 * 8
                    xgt_val[e_x] = (par, c_xgt[par].n)

                n_es = n_exp * n_sub
                v_w_arr = [0] * n_es

                def emit_wdma(es2):
                    e2_i, sub2 = divmod(es2, n_sub)
                    wpi2 = es2 % 2

                    @blk.gpsimd
                    def _(e, e2_i=e2_i, sub2=sub2, wpi2=wpi2, es2=es2):
                        if es2 == 0:
                            e.wait_ge(sem_dv, v_p4_dv)
                            e.wait_ge(sem_ac, v_p4_ac)
                            e.wait_ge(sem_pe, v_p4_pe)
                            e.wait_ge(sem_po, v_p4_po)
                            for q in range(4):
                                e.wait_ge(sem_sct[q], c_sct[q].n)
                        if w_free[wpi2]:
                            e.wait_ge(sem_pe, w_free[wpi2])
                        e.dma_start(out=w1b[wpi2][:, :], in_=w1_h[e2_i, sub2, :, :],
                                    max_dma_last_dim=8192).then_inc(sem_wf[wpi2], 16)
                        e.dma_start(out=w2b[wpi2][:, :], in_=w2_h[e2_i, sub2, :, :],
                                    max_dma_last_dim=8192).then_inc(sem_wf[wpi2], 16)
                    c_wf[wpi2].n += 32
                    v_w_arr[es2] = c_wf[wpi2].n

                emit_wdma(0)
                for es in range(n_es):
                    e_i, sub = divmod(es, n_sub)
                    wpi = es % 2
                    if e_i == 0:
                        blocks = [(tb * 512, 512) for tb in range(4)]
                    else:
                        blocks = [(0, 512), (512, CAPE - 512)]
                    ntbk = len(blocks)
                    if es + 1 < n_es:
                        emit_wdma(es + 1)
                    v_w = v_w_arr[es]

                    xpar = (e_i - 1) % 2
                    for tb, (c0, cw) in enumerate(blocks):
                        spi = tb % 2
                        for hc in range(8):
                            hpi = hc % 2

                            @blk.tensor
                            def _(e, e_i=e_i, sub=sub, hc=hc, hpi=hpi, tb=tb,
                                  spi=spi, wpi=wpi, v_w=v_w, xpar=xpar,
                                  c0=c0, cw=cw):
                                if tb == 0 and hc == 0:
                                    e.wait_ge(sem_wf[wpi], v_w)
                                    if e_i >= 1 and sub == 0:
                                        xp, xv = xgt_val[e_i]
                                        e.wait_ge(sem_xgt[xp], xv)
                                if ha_free[hpi]:
                                    e.wait_ge(sem_ac, ha_free[hpi])
                                if hc < 2 and silu_free[spi]:
                                    e.wait_ge(sem_pe, silu_free[spi])
                                for dc in range(8):
                                    rhs = (nf[dc][:, c0:c0 + cw]
                                           if e_i == 0 else
                                           xg_fm[xpar][dc][:, c0:c0 + cw])
                                    i = e.matmul(
                                        ha_ps[hpi][:, 0:cw],
                                        w1b[wpi][:, dc * 1024 + hc * 128:
                                                 dc * 1024 + (hc + 1) * 128],
                                        rhs, start=(dc == 0), stop=(dc == 7))
                                i.then_inc(sem_pe, 1)
                            pe.n += 1
                            v_h = pe.n

                            def _f(e, hc=hc, hpi=hpi, spi=spi, v_h=v_h, cw=cw):
                                e.wait_ge(sem_pe, v_h)
                                e.activation(silu[spi][:, hc * 512:hc * 512 + cw],
                                             ha_ps[hpi][:, 0:cw], AF.Silu
                                             ).then_inc(sem_ac, 1)
                            actq.add(_f)
                            ac.n += 1
                            ha_free[hpi] = ac.n
                        v_silu = ac.n
                        phaseA_last[e_i] = pe.n

                        for t2 in range(cw // 128):
                            t = tb * 4 + t2
                            fpi = t2 % 2

                            @blk.tensor
                            def _(e, t2=t2, fpi=fpi, spi=spi, wpi=wpi, v_silu=v_silu):
                                e.wait_ge(sem_ac, v_silu)
                                if fbps_free[fpi]:
                                    e.wait_ge(sem_dv, fbps_free[fpi])
                                for hc in range(8):
                                    for j in range(2):
                                        i = e.matmul(
                                            fb_ps[fpi][:, j * 512:(j + 1) * 512],
                                            silu[spi][:, hc * 512 + t2 * 128:
                                                      hc * 512 + (t2 + 1) * 128],
                                            w2b[wpi][:, hc * 1024 + j * 512:
                                                     hc * 1024 + (j + 1) * 512],
                                            start=(hc == 0), stop=(hc == 7))
                                i.then_inc(sem_pe, 1)
                            pe.n += 1
                            v_fb = pe.n
                            if t2 == cw // 128 - 1:
                                silu_free[spi] = pe.n
                                if tb == ntbk - 1:
                                    w_free[wpi] = pe.n

                            if e_i == 0:
                                @blk.vector
                                def _(e, t=t, fpi=fpi, v_fb=v_fb):
                                    e.wait_ge(sem_pe, v_fb)
                                    e.tensor_tensor(xn[t][:, :], fb_ps[fpi][:, :],
                                                    xn[t][:, :], ALU.add
                                                    ).then_inc(sem_dv, 1)
                                dv.n += 1
                                fbps_free[fpi] = dv.n
                            else:
                                @blk.vector
                                def _(e, fpi=fpi, v_fb=v_fb, ew=eo_sb_free[fpi]):
                                    e.wait_ge(sem_pe, v_fb)
                                    if ew:
                                        e.wait_ge(sem_eo2[fpi], ew)
                                    e.tensor_copy(eo_sb[fpi][:, :], fb_ps[fpi][:, :]
                                                  ).then_inc(sem_dv, 1)
                                dv.n += 1
                                fbps_free[fpi] = dv.n
                                v_ev = dv.n
                                row0 = (e_i - 1) * CAPE + c0 + t2 * 128

                                @blk.gpsimd
                                def _(e, fpi=fpi, sub=sub, row0=row0, v_ev=v_ev):
                                    e.wait_ge(sem_dv, v_ev)
                                    if sub == 0:
                                        e.dma_start(out=eo_d[row0:row0 + 128, :],
                                                    in_=eo_sb[fpi][:, :]
                                                    ).then_inc(sem_eo2[fpi], 16)
                                    else:
                                        e.dma_start(out=eo_d[row0:row0 + 128, :],
                                                    in_=eo_sb[fpi][:, :],
                                                    accum_op=ALU.add
                                                    ).then_inc(sem_eo2[fpi], 16)
                                c_eo2[fpi].n += 16
                                eo_sb_free[fpi] = c_eo2[fpi].n

                    if e_i == 0 and sub == n_sub - 1 and n_exp > 1:
                        emit_xgT(1, phaseA_last[0])
                    elif e_i >= 1 and sub == 0 and e_i + 1 <= n_exp - 1:
                        emit_xgT(e_i + 1, phaseA_last[e_i - 1])

                v_ffn_dv = dv.n
                v_eo0 = c_eo2[0].n
                v_eo1 = c_eo2[1].n

                geo_free = [0, 0]
                for t in range(TT):
                    pi = t % 2

                    @blk.gpsimd
                    def _(e, t=t, pi=pi):
                        if t == 0:
                            e.wait_ge(sem_eo2[0], v_eo0)
                            e.wait_ge(sem_eo2[1], v_eo1)
                            e.wait_ge(sem_dv, v_ffn_dv)
                        if geo_free[pi]:
                            e.wait_ge(sem_dv, geo_free[pi])
                        e.indirect_dma_start(
                            out=geo[0][pi][:, :], out_offset=None,
                            in_=eo_d[:, :],
                            in_offset=bass.IndirectOffsetOnAxis(
                                ap=sidx1[t][:, :], axis=0)).then_inc(sem_geo[pi], 16)
                        e.indirect_dma_start(
                            out=geo[1][pi][:, :], out_offset=None,
                            in_=eo_d[:, :],
                            in_offset=bass.IndirectOffsetOnAxis(
                                ap=sidx2[t][:, :], axis=0)).then_inc(sem_geo[pi], 16)
                    c_geo[pi].n += 32
                    v_g = c_geo[pi].n

                    @blk.vector
                    def _(e, t=t, pi=pi, v_g=v_g, dv0=dv.n):
                        e.wait_ge(sem_geo[pi], v_g)
                        e.scalar_tensor_tensor(xn[t][:, :], geo[0][pi][:, :],
                                               gsl1[t][:, :], xn[t][:, :],
                                               ALU.mult, ALU.add).then_inc(sem_dv, 1)
                        e.wait_ge(sem_dv, dv0 + 1)
                        e.scalar_tensor_tensor(xn[t][:, :], geo[1][pi][:, :],
                                               gsl2[t][:, :], xn[t][:, :],
                                               ALU.mult, ALU.add).then_inc(sem_dv, 1)
                    dv.n += 2
                    geo_free[pi] = dv.n
                    v_fin = dv.n

                    if t % 2 == 0:
                        @blk.sync
                        def _(e, t=t, v_fin=v_fin):
                            e.wait_ge(sem_dv, v_fin)
                            e.dma_start(out=out_h[s, t * 128:(t + 1) * 128, :],
                                        in_=xn[t][:, :]).then_inc(sem_st, 16)
                        st.n += 16
                    else:
                        def _f(e, t=t, v_fin=v_fin, s=s):
                            e.wait_ge(sem_dv, v_fin)
                            e.dma_start(out=out_h[s, t * 128:(t + 1) * 128, :],
                                        in_=xn[t][:, :]).then_inc(sem_st2, 16)
                        actq.add(_f)
                        st2.n += 16

                actq.flush(blk.scalar)

        @blk.sync
        def _(e):
            e.wait_ge(sem_st, st.n)
            e.wait_ge(sem_st2, st2.n)

    return nc



def _prep_weights(W_qkv, b_qkv, W_o, b_o, sw1, sw2, ew1, ew2, gw):
    w4 = np.asarray(W_qkv, np.float32).reshape(NH, DH, 3, D)
    wq = w4[:, :, 0, :].reshape(D, D)
    wk = w4[:, :, 1, :].reshape(D, D)
    wv = w4[:, :, 2, :].reshape(D, D)
    wqkvT = np.concatenate([wq.T, wk.T, wv.T], axis=1)
    wqkv_pre = np.empty((24, 128, 1024), np.float32)
    for fbk in range(24):
        sl = wqkvT[:, fbk * 128:(fbk + 1) * 128]
        wqkv_pre[fbk] = sl.reshape(8, 128, 128).transpose(1, 0, 2).reshape(128, 1024)

    b4 = np.asarray(b_qkv, np.float32).reshape(NH, DH, 3)
    b_r = np.concatenate([b4[:, :, 0].reshape(-1), b4[:, :, 1].reshape(-1),
                          b4[:, :, 2].reshape(-1)])
    bqkv_pre = np.ascontiguousarray(b_r.reshape(24, 128).T)

    woT = np.asarray(W_o, np.float32).T
    wo_pre = woT.reshape(8, 128, 1024).transpose(1, 0, 2).reshape(128, 8192)

    gwT = np.asarray(gw, np.float32).T
    gw_pre = np.ascontiguousarray(gwT.reshape(8, 128, 8).transpose(1, 0, 2)
                                  .reshape(128, 64))

    w1T = np.concatenate([np.asarray(sw1, np.float32).T[None],
                          np.asarray(ew1, np.float32).transpose(0, 2, 1)], axis=0)
    w2T = np.concatenate([np.asarray(sw2, np.float32).T[None],
                          np.asarray(ew2, np.float32).transpose(0, 2, 1)], axis=0)
    w1_pre = np.empty((NE, NSUB, 128, 8192), np.float32)
    w2_pre = np.empty((NE, NSUB, 128, 8192), np.float32)
    for e in range(NE):
        for sub in range(NSUB):
            sl = w1T[e][:, sub * 1024:(sub + 1) * 1024]
            w1_pre[e, sub] = sl.reshape(8, 128, 1024).transpose(1, 0, 2).reshape(128, 8192)
            sl2 = w2T[e][sub * 1024:(sub + 1) * 1024, :]
            w2_pre[e, sub] = sl2.reshape(8, 128, 1024).transpose(1, 0, 2).reshape(128, 8192)

    return {
        "wqkv": np.ascontiguousarray(wqkv_pre),
        "bqkv": bqkv_pre,
        "wo": np.ascontiguousarray(wo_pre),
        "bo": np.ascontiguousarray(np.asarray(b_o, np.float32).reshape(1, 1024)),
        "gw": gw_pre,
        "w1": np.ascontiguousarray(w1_pre),
        "w2": np.ascontiguousarray(w2_pre),
    }


def kernel(L0_attn_x, L0_attn_W_qkv, L0_attn_b_qkv, L0_attn_W_o, L0_attn_b_o,
           L0_ffn_shared_w1, L0_ffn_shared_w2, L0_ffn_experts_w1,
           L0_ffn_experts_w2, L0_ffn_gate_w):
    x = np.ascontiguousarray(np.asarray(L0_attn_x, dtype=np.float32))
    wmap = _prep_weights(L0_attn_W_qkv, L0_attn_b_qkv, L0_attn_W_o, L0_attn_b_o,
                         L0_ffn_shared_w1, L0_ffn_shared_w2, L0_ffn_experts_w1,
                         L0_ffn_experts_w2, L0_ffn_gate_w)
    nc = build_nc()
    in_maps = []
    for c in range(8):
        m = dict(wmap)
        m["x"] = np.ascontiguousarray(x[2 * c:2 * c + 2])
        in_maps.append(m)
    res = run_bass_kernel_spmd(nc, in_maps, core_ids=list(range(8)))
    out = np.concatenate([res.results[c]["out"] for c in range(8)], axis=0)
    return out.astype(np.float32)
